# version 02
# baseline (speedup 1.0000x reference)
import numpy as np
from contextlib import ExitStack

import concourse.bass as bass
import concourse.mybir as mybir
from concourse.bass_utils import run_bass_kernel_spmd

F32 = mybir.dt.float32
BF16 = mybir.dt.bfloat16
AF = mybir.ActivationFunctionType
ALU = mybir.AluOpType
AX = mybir.AxisListType

B, T, D = 8, 4096, 1024
NT = T // 128
ND = D // 128
TC = 512
NCH = T // TC
NEG_BIG = -1e30

ENGMAP = {"te": "tensor", "dve": "vector", "act": "scalar", "sp": "sync", "gp": "gpsimd"}



class Prog:
    def __init__(self, nc):
        self.nc = nc
        self.es = ExitStack()
        self.block = None
        self.sems = {}
        self.cnt = {}
        self.sem("flow")

    def sem(self, name):
        if name not in self.sems:
            self.sems[name] = self.es.enter_context(self.nc.semaphore(name))
            self.cnt[name] = 0
        return self.sems[name]

    def sbuf(self, name, shape, dt):
        return self.es.enter_context(self.nc.sbuf_tensor(name, shape, dt))

    def psum(self, name, shape, dt=F32):
        return self.es.enter_context(self.nc.psum_tensor(name, shape, dt))

    def start_block(self):
        self.block = self.es.enter_context(self.nc.Block())

    def close(self):
        self.es.close()

    def step(self, eng, fn, inc=1):
        wait_at = self.cnt["flow"]

        def body(E):
            E.wait_ge(self.sems["flow"], wait_at)
            ins = fn(E)
            assert ins is not None, "step fn must return its last instruction"
            ins.then_inc(self.sems["flow"], inc)
        getattr(self.block, ENGMAP[eng])(body)
        self.cnt["flow"] += inc

    def fence(self, E, ins):
        ins.then_inc(self.sems["flow"], 1)
        self.cnt["flow"] += 1
        E.wait_ge(self.sems["flow"], self.cnt["flow"])
        return ins

    def dma_in_step(self, E, out, in_, **kw):
        ins = E.dma_start(out=out, in_=in_, **kw)
        ins.then_inc(self.sems["flow"], 16)
        self.cnt["flow"] += 16
        return ins


class SbAlloc:
    def __init__(self, tensor, total):
        self.t = tensor
        self.total = total
        self.off = 0
        self.marks = []

    def take(self, n):
        self.off = (self.off + 3) & ~3
        assert self.off + n <= self.total, f"sbuf pool overflow {self.off}+{n}>{self.total}"
        ap = self.t[:, self.off:self.off + n]
        self.off += n
        return ap

    def take3(self, a, b):
        return self.take(a * b).rearrange("p (a b) -> p a b", a=a)

    def push(self):
        self.marks.append(self.off)

    def pop(self):
        self.off = self.marks.pop()



def _dft_matrices():
    tau = np.arange(64)
    rows = [np.cos(2 * np.pi * f * tau / 64) for f in range(33)]
    rows += [-np.sin(2 * np.pi * f * tau / 64) for f in range(1, 32)]
    D64 = np.stack(rows).astype(np.float32)
    G = np.zeros((64, 64), np.float32)
    for f in range(33):
        w = 1.0 if f in (0, 32) else 2.0
        G[:, f] = w * np.cos(2 * np.pi * f * tau / 64) / 64.0
    for f in range(1, 32):
        G[:, 32 + f] = -2.0 * np.sin(2 * np.pi * f * tau / 64) / 64.0
    return D64, G


def make_consts():
    c = {}
    c["ident"] = np.eye(128, dtype=np.float32)
    c["ones"] = np.ones((128, 128), np.float32)
    hm = np.zeros((128, 8, 16), np.float32)
    for tile in range(8):
        for p in range(128):
            hm[p, tile, (tile * 128 + p) // 64] = 1.0
    c["headmask"] = hm.reshape(128, 128)
    sl = np.zeros((128, 128), np.float32)
    su = np.zeros((128, 128), np.float32)
    tu = np.zeros((128, 128), np.float32)
    for ch in range(4):
        o = ch * 32
        blk = np.tril(np.ones((32, 32), np.float32), -1)
        sl[o:o + 32, o:o + 32] = blk
        su[o:o + 32, o:o + 32] = blk.T
        tu[o:o + 32, o:o + 32] = np.triu(np.ones((32, 32), np.float32))
    c["m_sl_bd"] = sl
    c["m_su_bd"] = su
    c["m_tu_bd"] = tu
    D64, G64 = _dft_matrices()
    dft_l = np.zeros((128, 128), np.float32)
    idft_l = np.zeros((128, 128), np.float32)
    for g in range(2):
        o = g * 64
        dft_l[o:o + 64, o:o + 64] = D64.T
        idft_l[o:o + 64, o:o + 64] = G64.T
    c["dft_l"] = dft_l
    c["idft_l"] = idft_l
    ft = np.arange(256)
    mlow = (ft[:, None] // 4 >= ft[None, :] // 4).astype(np.float32)
    c["mlow_neg"] = ((1.0 - mlow) * NEG_BIG).astype(np.float32)
    bd = np.kron(np.eye(8, dtype=np.float32), np.ones((16, 16), np.float32))
    c["m_d_neg"] = ((1.0 - bd) * NEG_BIG).astype(np.float32)
    rep = np.zeros((4, 16), np.float32)
    for h in range(16):
        rep[h // 4, h] = 1.0
    c["rep4"] = rep
    return c


def host_prep(I):
    H = {}
    A = lambda x: np.ascontiguousarray(np.asarray(x))
    H["A_wqkv"] = A(np.concatenate([I["L0_attn_Wq"], I["L0_attn_Wk"], I["L0_attn_Wv"]], axis=1))
    H["A_wo"] = A(I["L0_attn_Wo"])
    H["A_wsmall"] = A(np.concatenate([I["L0_attn_Wb"], I["L0_attn_Wdec"], I["L0_attn_Wmix"]], axis=1))
    H["A_convs"] = A(np.stack([I["L0_attn_conv_q"], I["L0_attn_conv_k"], I["L0_attn_conv_v"]]))
    H["A_convmix"] = A(I["L0_attn_conv_mix"])
    H["A_mixbias"] = A(I["L0_attn_mix_bias"]).reshape(1, 4)
    H["A_rmsw"] = A(I["L0_attn_rms_w"]).reshape(1, 256)
    H["B_wiT"] = A(I["L0_ffn_in_proj_w"].T)
    H["B_bi"] = A(I["L0_ffn_in_proj_b"]).reshape(1, 3072)
    H["B_woT"] = A(I["L0_ffn_out_proj_w"].T)
    H["B_bo"] = A(I["L0_ffn_out_proj_b"]).reshape(1, 1024)
    H["B_ln"] = A(np.stack([I["L0_ffn_ln1_g"], I["L0_ffn_ln1_b"], I["L0_ffn_ln2_g"], I["L0_ffn_ln2_b"]]))
    H["B_wge"] = A(np.concatenate([I["L0_ffn_w_group"],
                                   I["L0_ffn_w_expert"].transpose(1, 0, 2).reshape(1024, 8)], axis=1))
    H["B_bge"] = A(np.concatenate([I["L0_ffn_b_group"], I["L0_ffn_b_expert"].reshape(8)])).reshape(1, 12)
    H["B_w1"] = A(I["L0_ffn_w1"])
    H["B_b1"] = A(I["L0_ffn_b1"])
    H["B_w2"] = A(I["L0_ffn_w2"])
    H["B_b2"] = A(I["L0_ffn_b2"])
    H["C_qw"] = A(I["L1_attn_q_w"].T)
    H["C_kvw"] = A(np.concatenate([I["L1_attn_k_w"].T, I["L1_attn_v_w"].T], axis=1))
    H["C_ow"] = A(I["L1_attn_o_w"].T)
    H["C_dw"] = A(np.concatenate([I["L1_attn_dw_r"][:, 0, :], I["L1_attn_dw_i"][:, 0, :]], axis=1))
    H["C_pw"] = A(np.concatenate([I["L1_attn_pw_r"].T, I["L1_attn_pw_i"].T], axis=1))
    H["C_gw"] = A(I["L1_attn_gate_w"].T)
    H["C_alpha"] = A(I["L1_attn_alpha_raw"]).reshape(1, 1)
    H["D_g12"] = A(np.stack([I["L1_ffn_gamma1"], I["L1_ffn_gamma2"]]))
    H["D_wq"] = A(I["L1_ffn_Wq"])
    H["D_bq"] = A(I["L1_ffn_bq"]).reshape(1, 1024)
    H["D_wdown"] = A(I["L1_ffn_Wdown"])
    H["D_bdown"] = A(I["L1_ffn_bdown"]).reshape(1, 64)
    wup = np.asarray(I["L1_ffn_Wup"]).reshape(64, 16, 2, 64)
    H["D_wupk"] = A(wup[:, :, 0, :].reshape(64, 1024))
    H["D_wupv"] = A(wup[:, :, 1, :].reshape(64, 1024))
    bup = np.asarray(I["L1_ffn_bup"]).reshape(16, 2, 64)
    H["D_bupk"] = A(bup[:, 0, :].reshape(1, 1024))
    H["D_bupv"] = A(bup[:, 1, :].reshape(1, 1024))
    H["D_wo"] = A(I["L1_ffn_Wo"])
    H["D_bo"] = A(I["L1_ffn_bo"]).reshape(1, 1024)
    H["D_wg"] = A(I["L1_ffn_Wg"])
    H["D_bg"] = A(I["L1_ffn_bg"]).reshape(1, 8)
    H["D_we"] = A(I["L1_ffn_We"])
    for k, v in make_consts().items():
        H["K_" + k] = v
    return H


def ln_var_ops(p, E, st, eps=1e-5):
    E.tensor_scalar_mul(st[:, 4:5], st[:, 0:1], 1.0 / D)
    E.tensor_scalar_mul(st[:, 2:3], st[:, 1:2], 1.0 / D)
    p.fence(E, E.tensor_mul(st[:, 5:6], st[:, 4:5], st[:, 4:5]))
    p.fence(E, E.tensor_sub(st[:, 2:3], st[:, 2:3], st[:, 5:6]))
    return E.tensor_scalar_add(st[:, 2:3], st[:, 2:3], eps)



class Ctx:
    def __init__(self, p, W):
        self.p = p
        self.W = W
        self.pb = SbAlloc(p.sbuf("poolb", [128, 75776], BF16), 75776)
        self.pf = SbAlloc(p.sbuf("poolf", [128, 9216], F32), 9216)
        self.ident = self.pb.take(128)
        self.ones = self.pb.take(128)
        self.stat = self.pf.take(32)
        ps = p.psum("ps", [128, 4096], F32)
        self.psA = ps[:, 0:2048]
        self.psB = ps[:, 2048:3072]
        self.psC = ps[:, 3072:3584]
        self.psD = ps[:, 3584:4096]
        self.ps = ps

    def load_consts(self):
        p = self.p

        def g(E):
            p.dma_in_step(E, self.ident, self.W["K_ident"][:, :])
            return E.dma_start(out=self.ones, in_=self.W["K_ones"][:, :])
        p.step("gp", g, inc=16)


def bcast_row(cx, src_row_bf, dst_f32, n):
    p = cx.p

    def te(E):
        ins = None
        for ci in range((n + 511) // 512):
            w = min(512, n - ci * 512)
            ins = E.matmul(cx.psD[:, :w], cx.ones[0:1, :], src_row_bf[0:1, ci * 512:ci * 512 + w],
                           start=True, stop=True)
        return ins
    assert n <= 512
    p.step("te", te)

    def dv(E):
        return E.tensor_copy(dst_f32[:, :n], cx.psD[:, :n])
    p.step("dve", dv)


def emit_ln_tm(cx, xin, xout_tm, n_rows=T, gbc=None, bbc=None, rms=False, eps=1e-5,
               res_in=None, res_out=None, prenorm_cb=None, ld_eng="sp"):
    p = cx.p
    cx.pf.push(); cx.pb.push()
    sbx = cx.pf.take(1024)
    sbq = cx.pf.take(1024)
    sbr = cx.pf.take(1024) if res_in is not None else None
    sbo = cx.pb.take(1024)
    st = cx.stat
    for i in range(n_rows // 128):
        def ld(E, i=i):
            if res_in is not None:
                p.dma_in_step(E, sbr, res_in[i * 128:(i + 1) * 128, :])
            return E.dma_start(out=sbx, in_=xin[i * 128:(i + 1) * 128, :])
        p.step(ld_eng, ld, inc=16)

        def dv(E, i=i):
            if res_in is not None:
                E.tensor_add(sbx, sbx, sbr)
            if prenorm_cb is not None:
                prenorm_cb(E, sbx, i)
            if rms:
                return E.engine_nop()
            return E.reduce_sum(st[:, 0:1], sbx, axis=AX.X)
        p.step("dve", dv)
        if res_out is not None:
            def st_res(E, i=i):
                return E.dma_start(out=res_out[i * 128:(i + 1) * 128, :], in_=sbx)
            p.step("sp", st_res, inc=16)

        def sc(E, i=i):
            return E.activation(sbq, sbx, AF.Square, accum_out=st[:, 1:2])
        p.step("act", sc)

        def dv2(E, i=i):
            if rms:
                E.tensor_scalar(st[:, 2:3], st[:, 1:2], 1.0 / D, eps, ALU.mult, ALU.add)
                return E.memset(st[:, 4:5], 0.0)
            return ln_var_ops(p, E, st, eps)
        p.step("dve", dv2)

        def sc2(E, i=i):
            return E.activation(st[:, 3:4], st[:, 2:3], AF.Sqrt)
        p.step("act", sc2)

        def dv3(E, i=i):
            p.fence(E, E.reciprocal(st[:, 3:4], st[:, 3:4]))
            tgt = sbq if gbc is not None else sbo
            E.tensor_scalar(tgt, sbx, st[:, 4:5], st[:, 3:4], ALU.subtract, ALU.mult)
            if gbc is not None:
                E.tensor_mul(sbq, sbq, gbc)
                if bbc is not None:
                    return E.tensor_add(sbo, sbq, bbc)
                return E.tensor_copy(sbo, sbq)
            return E.engine_nop()
        p.step("dve", dv3)

        def stt(E, i=i):
            return E.dma_start(out=xout_tm[i * 128:(i + 1) * 128, :], in_=sbo)
        p.step("sp", stt, inc=16)
    cx.pf.pop(); cx.pb.pop()


def emit_transpose(cx, src_tm, dst_fm, n_rows, n_cols):
    p = cx.p
    cx.pb.push()
    sbi = cx.pb.take(n_cols)
    sbo = cx.pb.take(n_cols)
    for i in range(n_rows // 128):
        def ld(E, i=i):
            return E.dma_start(out=sbi, in_=src_tm[i * 128:(i + 1) * 128, :])
        p.step("sp", ld, inc=16)
        for h in range(0, n_cols // 128, 4):
            nj = min(4, n_cols // 128 - h)

            def te(E, i=i, h=h, nj=nj):
                ins = None
                for j in range(nj):
                    ins = E.transpose(cx.psA.bitcast(BF16)[:, (j * 1024):(j * 1024) + 128],
                                      sbi[:, (h + j) * 128:(h + j + 1) * 128], cx.ident)
                return ins
            p.step("te", te)

            def dv(E, i=i, h=h, nj=nj):
                ins = None
                for j in range(nj):
                    ins = E.tensor_copy(sbo[:, (h + j) * 128:(h + j + 1) * 128],
                                        cx.psA.bitcast(BF16)[:, (j * 1024):(j * 1024) + 128])
                return ins
            p.step("dve", dv)

        def stt(E, i=i):
            ins = None
            for j in range(n_cols // 128):
                ins = p.dma_in_step(E, dst_fm[j * 128:(j + 1) * 128, i * 128:(i + 1) * 128],
                                    sbo[:, j * 128:(j + 1) * 128])
            return ins
        wait_at = p.cnt["flow"]

        def body(E, stt=stt, wait_at=wait_at):
            E.wait_ge(p.sems["flow"], wait_at)
            stt(E)
        p.block.sync(body)
    cx.pb.pop()


def emit_fm_matmul(cx, x_fm, w_dram, y_fm, din, dout, bias_dram=None, bias_off=0,
                   func=AF.Identity, w_sb=None, out_f32_cb=None):
    p = cx.p
    cx.pb.push()
    ndt, nft = din // 128, dout // 128
    if w_sb is None:
        w_sb = cx.pb.take3(ndt, dout)

        def lw(E):
            return E.dma_start(out=w_sb, in_=w_dram.rearrange("(c p) f -> p c f", p=128))
        p.step("gp", lw, inc=16)
    bsb = None
    if bias_dram is not None:
        bsb = cx.pf.take(nft)

        def lb(E):
            return E.dma_start(out=bsb, in_=bias_dram[0:1, bias_off:bias_off + dout]
                               .rearrange("o (c p) -> p (o c)", p=128),
                               allow_slow_non_contiguous=True)
        p.step("sp", lb, inc=16)
    xc = cx.pb.take3(ndt, TC)
    yc = cx.pb.take3(nft, TC)
    for ch in range(NCH):
        def lx(E, ch=ch):
            return E.dma_start(out=xc, in_=x_fm.rearrange("(c p) t -> p c t", p=128)[:, :, ch * TC:(ch + 1) * TC])
        p.step("sp", lx, inc=16)
        for f in range(nft):
            def te(E, f=f, ch=ch):
                ins = None
                for d in range(ndt):
                    ins = E.matmul(cx.psA[:, (f % 4) * 512:(f % 4) * 512 + 512],
                                   w_sb[:, d, f * 128:(f + 1) * 128], xc[:, d, :],
                                   start=(d == 0), stop=(d == ndt - 1))
                return ins
            p.step("te", te)

            def ev(E, f=f, ch=ch):
                pa = cx.psA[:, (f % 4) * 512:(f % 4) * 512 + 512]
                if bsb is not None:
                    return E.activation(yc[:, f, :], pa, func, bias=bsb[:, f:f + 1])
                return E.activation(yc[:, f, :], pa, func)
            p.step("act", ev)

        def st(E, ch=ch):
            return E.dma_start(out=y_fm.rearrange("(c p) t -> p c t", p=128)[:, :, ch * TC:(ch + 1) * TC],
                               in_=yc)
        p.step("sp", st, inc=16)
    cx.pb.pop()


def emit_row_bcast(cx, src_row_ap, dst_f32, n):
    p = cx.p

    def lr(E):
        return E.dma_start(out=cx.rowbf[0:1, 0:n], in_=src_row_ap)
    p.step("gp", lr, inc=16)
    for c0 in range(0, n, 512):
        w = min(512, n - c0)

        def te(E, c0=c0, w=w):
            return E.matmul(cx.psD[:, 0:w], cx.ones[0:1, :], cx.rowbf[0:1, c0:c0 + w],
                            start=True, stop=True)
        p.step("te", te)

        def dv(E, c0=c0, w=w):
            return E.tensor_copy(dst_f32[:, c0:c0 + w], cx.psD[:, 0:w])
        p.step("dve", dv)



def emit_stage_b(cx, xin, xout, scr):
    p = cx.p
    W = cx.W
    d_tm0, d_fm0, d_fm1, d_fm2 = scr["tm0"], scr["fm0"], scr["fm1"], scr["fm2"]
    d_tm1, d_fm3, d_tm2, d_fm4 = scr["tm1"], scr["fm3"], scr["tm2"], scr["fm4"]
    d_c, d_cT = scr["c32"], scr["cT"]

    emit_ln_tm(cx, xin, d_tm0)
    emit_transpose(cx, d_tm0, d_fm0, T, D)
    cx.pb.push()
    wqk_sb = cx.pb.take3(ND, 2048)

    def lw(E):
        return E.dma_start(out=wqk_sb, in_=W["B_wiT"].rearrange("(c p) f -> p c f", p=128)[:, :, 0:2048])
    p.step("gp", lw, inc=16)
    emit_fm_matmul(cx, d_fm0, None, d_fm1, D, D, bias_dram=W["B_bi"], bias_off=0,
                   w_sb=wqk_sb[:, :, 0:1024])
    emit_fm_matmul(cx, d_fm0, None, d_fm2, D, D, w_sb=wqk_sb[:, :, 1024:2048])
    cx.pb.pop()
    cx.pb.push()
    wv_sb = cx.pb.take3(ND, 1024)

    def lwv(E):
        return E.dma_start(out=wv_sb, in_=W["B_wiT"].rearrange("(c p) f -> p c f", p=128)[:, :, 2048:3072])
    p.step("gp", lwv, inc=16)
    xc = cx.pb.take3(ND, TC)
    vo = cx.pb.take(1024)
    for ch in range(NCH):
        def lx(E, ch=ch):
            return E.dma_start(out=xc, in_=d_fm0.rearrange("(c p) t -> p c t", p=128)[:, :, ch * TC:(ch + 1) * TC])
        p.step("sp", lx, inc=16)
        for tt in range(4):
            def te(E, ch=ch, tt=tt):
                ins = None
                for fc in range(2):
                    for d in range(ND):
                        ins = E.matmul(cx.psA[:, fc * 512:fc * 512 + 512],
                                       xc[:, d, tt * 128:(tt + 1) * 128],
                                       wv_sb[:, d, fc * 512:(fc + 1) * 512],
                                       start=(d == 0), stop=(d == ND - 1))
                return ins
            p.step("te", te)

            def ev(E, ch=ch, tt=tt):
                E.tensor_copy(vo[:, 0:512], cx.psA[:, 0:512])
                return E.tensor_copy(vo[:, 512:1024], cx.psA[:, 512:1024])
            p.step("dve", ev)

            def st(E, ch=ch, tt=tt):
                r = ch * 512 + tt * 128
                return E.dma_start(out=d_tm1[r:r + 128, :], in_=vo)
            p.step("sp", st, inc=16)
    cx.pb.pop()

    cx.pb.push(); cx.pf.push()
    hm_sb = cx.pb.take(128)

    def lhm(E):
        return E.dma_start(out=hm_sb, in_=W["K_headmask"][:, :])
    p.step("gp", lhm, inc=16)
    sq = cx.pf.take(512)
    nrm = cx.pf.take(16)
    negc_bc = cx.negc_bc
    xc2 = cx.pb.take3(ND, TC)
    for which, src in ((0, d_fm1), (1, d_fm2)):
        for ch in range(NCH):
            def lx(E, ch=ch, src=src):
                return E.dma_start(out=xc2, in_=src.rearrange("(c p) t -> p c t", p=128)[:, :, ch * TC:(ch + 1) * TC])
            p.step("sp", lx, inc=16)
            for d in range(ND):
                def sc_fix(E, ch=ch, d=d):
                    return E.activation(sq, xc2[:, d, :], AF.Square)
                p.step("act", sc_fix)

                def dv_cast(E, ch=ch, d=d):
                    return E.tensor_copy(cx.sq_bf, sq)
                p.step("dve", dv_cast)

                def te(E, ch=ch, d=d):
                    return E.matmul(cx.psC[0:16, :], hm_sb[:, d * 16:(d + 1) * 16],
                                    cx.sq_bf, start=(d == 0), stop=(d == ND - 1))
                p.step("te", te)

            def dv(E, ch=ch, which=which):
                p.fence(E, E.reduce_max(cx.stat[0:16, 8:9], cx.psC[0:16, :], axis=AX.X))
                if ch == 0:
                    return E.tensor_copy(nrm[0:16, which:which + 1], cx.stat[0:16, 8:9])
                return E.tensor_max(nrm[0:16, which:which + 1], nrm[0:16, which:which + 1],
                                    cx.stat[0:16, 8:9])
            p.step("dve", dv)

    def sc3(E):
        E.activation(nrm[0:16, 2:3], nrm[0:16, 0:1], AF.Sqrt)
        return E.activation(nrm[0:16, 3:4], nrm[0:16, 1:2], AF.Sqrt)
    p.step("act", sc3)

    def dv4(E):
        p.fence(E, E.tensor_mul(nrm[0:16, 4:5], nrm[0:16, 2:3], nrm[0:16, 3:4]))
        p.fence(E, E.tensor_scalar_mul(nrm[0:16, 4:5], nrm[0:16, 4:5], -1.0 / 8.0))
        p.fence(E, E.memset(cx.negc_bf, 0.0))
        return E.tensor_copy(cx.negc_bf[0:16, 0:1], nrm[0:16, 4:5])
    p.step("dve", dv4)

    def te4(E):
        return E.transpose(cx.psD.bitcast(BF16)[0:1, 0:128], cx.negc_bf, cx.ident)
    p.step("te", te4)

    def dv5(E):
        return E.tensor_copy(cx.negcT_bf[0:1, 0:16], cx.psD.bitcast(BF16)[0:1, 0:16])
    p.step("dve", dv5)

    def te5(E):
        return E.matmul(cx.psD[:, 0:16], cx.ones[0:1, :], cx.negcT_bf[0:1, 0:16], start=True, stop=True)
    p.step("te", te5)

    def dv6(E):
        return E.tensor_copy(negc_bc, cx.psD[:, 0:16])
    p.step("dve", dv6)
    cx.pb.pop(); cx.pf.pop()

    cx.pb.push()
    qh = cx.pb.take(4096)
    kh = cx.pb.take(4096)
    vaug = cx.pb.take(NT * 2 * 65).rearrange("p (t h e) -> p t h e", t=NT, h=2)
    es = cx.pb.take(2048)
    ob = cx.pb.take(512)
    bvs = cx.pf.take(8)

    def lbv(E):
        return E.dma_start(out=bvs, in_=W["B_bi"][0:1, 2048:3072].rearrange("o (c p) -> p (o c)", p=128),
                           allow_slow_non_contiguous=True)
    p.step("sp", lbv, inc=16)
    for hp in range(8):
        def lqk(E, hp=hp):
            p.dma_in_step(E, qh, d_fm1[hp * 128:(hp + 1) * 128, :])
            return E.dma_start(out=kh, in_=d_fm2[hp * 128:(hp + 1) * 128, :])
        p.step("sp", lqk, inc=16)

        def mset(E, hp=hp):
            return E.memset(vaug, 1.0)
        p.step("dve", mset)

        def lv(E, hp=hp):
            src = d_tm1.rearrange("(t p) (h e) -> p t h e", p=128, h=16)
            p.dma_in_step(E, vaug[:, :, 0, 0:64], src[:, :, 2 * hp, :])
            return E.dma_start(out=vaug[:, :, 1, 0:64], in_=src[:, :, 2 * hp + 1, :])
        p.step("sp", lv, inc=16)
        for h01 in range(2):
            h = 2 * hp + h01
            qb = 64 * h01
            for qi in range(8):
                for kq in range(8):
                    def te(E, h01=h01, qi=qi, kq=kq, qb=qb):
                        ins = None
                        for j in range(4):
                            ins = E.matmul(cx.psA[:, j * 512:(j + 1) * 512],
                                           kh[qb:qb + 64, (kq * 4 + j) * 128:(kq * 4 + j + 1) * 128],
                                           qh[qb:qb + 64, qi * 512:(qi + 1) * 512],
                                           start=True, stop=True)
                        return ins
                    p.step("te", te)

                    def sc(E, h=h, kq=kq):
                        return E.activation(es, cx.psA, AF.Exp,
                                            bias=negc_bc[:, h:h + 1], scale=0.125)
                    p.step("act", sc)

                    def te2(E, h01=h01, qi=qi, kq=kq):
                        ins = None
                        for j in range(4):
                            ins = E.matmul(cx.psB[0:65, 0:512],
                                           vaug[:, kq * 4 + j, h01, 0:65],
                                           es[:, j * 512:(j + 1) * 512],
                                           start=(kq == 0 and j == 0), stop=(kq == 7 and j == 3))
                        return ins
                    p.step("te", te2)
                def dv_b(E):
                    p.fence(E, E.reciprocal(cx.rd32[0:1, :], cx.psB[64:65, 0:512]))
                    return E.tensor_copy(cx.rd_bf[0:1, :], cx.rd32[0:1, :])
                p.step("dve", dv_b)

                def te3(E):
                    return E.matmul(cx.psC[0:64, 0:512], cx.ones[0:1, 0:64], cx.rd_bf[0:1, 0:512],
                                    start=True, stop=True)
                p.step("te", te3)

                def dv_c(E):
                    return E.tensor_copy(cx.rdb32[0:64, :], cx.psC[0:64, 0:512])
                p.step("dve", dv_c)

                def dv_d(E, h01=h01, qb=qb):
                    return E.tensor_mul(cx.o32[0:64, :], cx.psB[0:64, 0:512], cx.rdb32[0:64, :])
                p.step("dve", dv_d)

                def sc2(E, h=h, h01=h01, qb=qb):
                    return E.activation(ob[0:64, :], cx.o32[0:64, :], AF.Identity,
                                        bias=bvs[qb:qb + 64, (h // 2):(h // 2) + 1])
                p.step("act", sc2)

                def st(E, h=h, qi=qi):
                    return E.dma_start(out=d_fm3[h * 64:(h + 1) * 64, qi * 512:(qi + 1) * 512],
                                       in_=ob[0:64, :])
                p.step("sp", st, inc=16)
    cx.pb.pop()

    cx.pb.push(); cx.pf.push()
    wo_sb = cx.pb.take3(ND, 1024)

    def lwo(E):
        return E.dma_start(out=wo_sb, in_=W["B_woT"].rearrange("(c p) f -> p c f", p=128))
    p.step("gp", lwo, inc=16)
    rowbf = cx.pb.take(1024)
    cb_bo = cx.pf.take(1024)
    cb_g1 = cx.pf.take(1024)
    cb_b1 = cx.pf.take(1024)
    for dst, srcap in ((cb_bo, W["B_bo"][0:1, :]), (cb_g1, W["B_ln"][0:1, :]), (cb_b1, W["B_ln"][1:2, :])):
        def lr(E, srcap=srcap):
            return E.dma_start(out=rowbf[0:1, :], in_=srcap)
        p.step("gp", lr, inc=16)
        for half in range(2):
            def te(E, half=half):
                return E.matmul(cx.psD[:, 0:512], cx.ones[0:1, :], rowbf[0:1, half * 512:(half + 1) * 512],
                                start=True, stop=True)
            p.step("te", te)

            def dv(E, dst=dst, half=half):
                return E.tensor_copy(dst[:, half * 512:(half + 1) * 512], cx.psD[:, 0:512])
            p.step("dve", dv)
    oc = cx.pb.take3(ND, TC)
    xb_t = cx.pb.take(1024)
    y_t = cx.pb.take(1024)
    for ch in range(NCH):
        def lo(E, ch=ch):
            return E.dma_start(out=oc, in_=d_fm3.rearrange("(c p) t -> p c t", p=128)[:, :, ch * TC:(ch + 1) * TC])
        p.step("sp", lo, inc=16)
        for tt in range(4):
            i = ch * 4 + tt

            def te(E, ch=ch, tt=tt):
                ins = None
                for fc in range(2):
                    for d in range(ND):
                        ins = E.matmul(cx.psA[:, fc * 512:fc * 512 + 512],
                                       oc[:, d, tt * 128:(tt + 1) * 128],
                                       wo_sb[:, d, fc * 512:(fc + 1) * 512],
                                       start=(d == 0), stop=(d == ND - 1))
                return ins
            p.step("te", te)

            def lxb(E, i=i):
                return E.dma_start(out=xb_t, in_=d_tm0[i * 128:(i + 1) * 128, :])
            p.step("sp", lxb, inc=16)

            def dv(E, i=i):
                E.tensor_copy(cx.x32a, xb_t)
                E.tensor_add(cx.x32a, cx.x32a, cb_bo)
                E.tensor_add(cx.x32a[:, 0:512], cx.x32a[:, 0:512], cx.psA[:, 0:512])
                E.tensor_add(cx.x32a[:, 512:1024], cx.x32a[:, 512:1024], cx.psA[:, 512:1024])
                return E.reduce_sum(cx.stat[:, 0:1], cx.x32a, axis=AX.X)
            p.step("dve", dv)

            def sc(E, i=i):
                return E.activation(cx.x32b, cx.x32a, AF.Square, accum_out=cx.stat[:, 1:2])
            p.step("act", sc)

            def dv2(E, i=i):
                return ln_var_ops(p, E, cx.stat)
            p.step("dve", dv2)

            def sc2(E, i=i):
                return E.activation(cx.stat[:, 3:4], cx.stat[:, 2:3], AF.Sqrt)
            p.step("act", sc2)

            def dv3(E, i=i):
                p.fence(E, E.reciprocal(cx.stat[:, 3:4], cx.stat[:, 3:4]))
                E.tensor_scalar(cx.x32b, cx.x32a, cx.stat[:, 4:5], cx.stat[:, 3:4],
                                ALU.subtract, ALU.mult)
                E.tensor_mul(cx.x32b, cx.x32b, cb_g1)
                E.tensor_add(cx.x32b, cx.x32b, cb_b1)
                return E.tensor_copy(y_t, cx.x32b)
            p.step("dve", dv3)

            def sty(E, i=i):
                return E.dma_start(out=d_tm2[i * 128:(i + 1) * 128, :], in_=y_t)
            p.step("sp", sty, inc=16)
    cx.pb.pop(); cx.pf.pop()

    emit_transpose(cx, d_tm2, d_fm4, T, D)

    cx.pb.push(); cx.pf.push()
    wge_sb = cx.pb.take3(ND, 12)

    def lwge(E):
        return E.dma_start(out=wge_sb, in_=W["B_wge"].rearrange("(c p) f -> p c f", p=128))
    p.step("gp", lwge, inc=16)
    rowb2 = cx.pb.take(16)

    def lbge(E):
        return E.dma_start(out=rowb2[0:1, 0:12], in_=W["B_bge"][0:1, :])
    p.step("gp", lbge, inc=16)
    cbge = cx.pf.take(12)

    def te(E):
        return E.matmul(cx.psD[:, 0:12], cx.ones[0:1, :], rowb2[0:1, 0:12], start=True, stop=True)
    p.step("te", te)

    def dvge(E):
        return E.tensor_copy(cbge, cx.psD[:, 0:12])
    p.step("dve", dvge)
    yc2 = cx.pb.take3(ND, TC)
    r32 = cx.pf.take(12)
    w32 = cx.pf.take(32)
    cmb_bf = cx.pb.take(8)
    for ch in range(NCH):
        def ly(E, ch=ch):
            return E.dma_start(out=yc2, in_=d_fm4.rearrange("(c p) t -> p c t", p=128)[:, :, ch * TC:(ch + 1) * TC])
        p.step("sp", ly, inc=16)
        for tt in range(4):
            i = ch * 4 + tt

            def te2(E, ch=ch, tt=tt):
                ins = None
                for d in range(ND):
                    ins = E.matmul(cx.psC[:, 0:12], yc2[:, d, tt * 128:(tt + 1) * 128],
                                   wge_sb[:, d, :], start=(d == 0), stop=(d == ND - 1))
                return ins
            p.step("te", te2)

            def dvr(E, i=i):
                p.fence(E, E.tensor_add(r32, cx.psC[:, 0:12], cbge))
                lg = r32[:, 0:4]
                return E.reduce_max(cx.stat[:, 8:9], lg, axis=AX.X, negate=True)
            p.step("dve", dvr)

            def scr_(E, i=i):
                return E.activation(w32[:, 0:4], r32[:, 0:4], AF.Exp,
                                    bias=cx.stat[:, 8:9], accum_out=cx.stat[:, 9:10])
            p.step("act", scr_)

            def dvr2(E, i=i):
                st_ = cx.stat
                E.reciprocal(st_[:, 10:11], st_[:, 9:10])
                p.fence(E, E.tensor_scalar_mul(st_[:, 11:12], st_[:, 8:9], -1.0))
                lg = r32[:, 0:4]
                ohg = w32[:, 4:8]
                p.fence(E, E.tensor_tensor(ohg, lg, st_[:, 11:12].broadcast_to([128, 4]), ALU.is_ge))
                pref = w32[:, 8:12]
                p.fence(E, E.tensor_copy(pref[:, 0:1], ohg[:, 0:1]))
                p.fence(E, E.tensor_tensor(pref[:, 1:2], pref[:, 0:1], ohg[:, 1:2], ALU.max))
                p.fence(E, E.tensor_tensor(pref[:, 2:3], pref[:, 1:2], ohg[:, 2:3], ALU.max))
                inv = w32[:, 12:16]
                p.fence(E, E.tensor_scalar(inv[:, 0:3], pref[:, 0:3], -1.0, 1.0, ALU.mult, ALU.add))
                p.fence(E, E.tensor_mul(ohg[:, 1:4], ohg[:, 1:4], inv[:, 0:3]))
                le = r32[:, 4:12].rearrange("p (g e) -> p g e", e=2)
                p.fence(E, E.reduce_max(w32[:, 16:20], le, axis=AX.X, negate=True))
                les = w32[:, 20:28].rearrange("p (g e) -> p g e", e=2)
                return E.tensor_tensor(les, le, w32[:, 16:20][:, :, None].broadcast_to([128, 4, 2]), ALU.add)
            p.step("dve", dvr2)

            def scr2(E, i=i):
                return E.activation(w32[:, 20:28], w32[:, 20:28], AF.Exp)
            p.step("act", scr2)

            def dvr3(E, i=i):
                st_ = cx.stat
                ee = w32[:, 20:28].rearrange("p (g e) -> p g e", e=2)
                p.fence(E, E.reduce_sum(w32[:, 16:20], ee, axis=AX.X))
                p.fence(E, E.reciprocal(w32[:, 16:20], w32[:, 16:20]))
                p.fence(E, E.tensor_tensor(ee, ee, w32[:, 16:20][:, :, None].broadcast_to([128, 4, 2]), ALU.mult))
                ohg = w32[:, 4:8]
                p.fence(E, E.tensor_tensor(ee, ee, ohg[:, :, None].broadcast_to([128, 4, 2]), ALU.mult))
                sel = w32[:, 28:30]
                p.fence(E, E.tensor_reduce(sel, w32[:, 20:28].rearrange("p (g e) -> p e g", e=2), AX.X, ALU.add))
                p.fence(E, E.reduce_max(st_[:, 12:13], sel, axis=AX.X))
                ohe = w32[:, 30:32]
                p.fence(E, E.tensor_tensor(ohe, sel, st_[:, 12:13].broadcast_to([128, 2]), ALU.is_ge))
                p.fence(E, E.tensor_scalar(st_[:, 13:14], ohe[:, 0:1], -1.0, 1.0, ALU.mult, ALU.add))
                p.fence(E, E.tensor_mul(ohe[:, 1:2], ohe[:, 1:2], st_[:, 13:14]))
                p.fence(E, E.tensor_mul(st_[:, 14:15], st_[:, 10:11], st_[:, 12:13]))
                cmb = w32[:, 16:24].rearrange("p (g e) -> p g e", e=2)
                p.fence(E, E.tensor_tensor(cmb, w32[:, 4:8][:, :, None].broadcast_to([128, 4, 2]),
                                ohe[:, None, :].broadcast_to([128, 4, 2]), ALU.mult))
                p.fence(E, E.tensor_single_scalar(w32[:, 16:24], w32[:, 16:24], st_[:, 14:15], ALU.mult))
                return E.tensor_copy(cmb_bf, w32[:, 16:24])
            p.step("dve", dvr3)

            def stc(E, i=i):
                return E.dma_start(out=d_c[i * 128:(i + 1) * 128, 0:8], in_=w32[:, 16:24])
            p.step("sp", stc, inc=16)

            def te3(E, i=i):
                return E.transpose(cx.psD.bitcast(BF16)[0:8, 0:128], cmb_bf, cx.ident)
            p.step("te", te3)

            def dvr4(E, i=i):
                return E.tensor_copy(cx.ctrow_bf[0:8, 0:128], cx.psD.bitcast(BF16)[0:8, 0:128])
            p.step("dve", dvr4)

            def stct(E, i=i):
                return E.dma_start(out=d_cT[0:8, i * 128:(i + 1) * 128], in_=cx.ctrow_bf[0:8, 0:128])
            p.step("sp", stct, inc=16)
    cx.pb.pop(); cx.pf.pop()

    cx.pb.push(); cx.pf.push()
    wa = cx.pb.take3(ND, 2048)
    wb = cx.pb.take3(16, 1024)
    macc = cx.pb.take(16 * 1024).rearrange("p (s f) -> p s f", s=16)
    hsb = cx.pb.take3(16, 512)
    xc3 = cx.pb.take3(ND, TC)
    ctf = cx.pb.take(4096)
    b1s = cx.pf.take(16)
    csb = cx.pf.take(256).rearrange("p (s e) -> p s e", s=16)
    b2s = cx.pb.take(1024)

    def lct(E):
        return E.dma_start(out=ctf[0:8, :], in_=d_cT[0:8, :])
    p.step("sp", lct, inc=16)

    def lb2(E):
        return E.dma_start(out=b2s[0:8, :], in_=W["B_b2"][:, :])
    p.step("gp", lb2, inc=16)
    for thalf in range(2):
        def lc(E, thalf=thalf):
            return E.dma_start(out=csb,
                               in_=W_dc_rearr(d_c, thalf))
        p.step("sp", lc, inc=16)
        for e in range(8):
            def lw1(E, e=e):
                p.dma_in_step(E, wa, W["B_w1"][e].rearrange("(c p) f -> p c f", p=128))
                p.dma_in_step(E, wb, W["B_w2"][e].rearrange("(c p) f -> p c f", p=128))
                return E.dma_start(out=b1s, in_=W["B_b1"][e:e + 1, :].rearrange("o (c p) -> p (o c)", p=128),
                                   allow_slow_non_contiguous=True)
            p.step("gp", lw1, inc=16)
            for ch in range(4):
                def lx3(E, thalf=thalf, ch=ch):
                    cc = thalf * 4 + ch
                    return E.dma_start(out=xc3, in_=d_fm4.rearrange("(c p) t -> p c t", p=128)[:, :, cc * TC:(cc + 1) * TC])
                p.step("sp", lx3, inc=16)
                for f in range(16):
                    def te(E, f=f):
                        ins = None
                        for d in range(ND):
                            ins = E.matmul(cx.psA[:, (f % 4) * 512:(f % 4) * 512 + 512],
                                           wa[:, d, f * 128:(f + 1) * 128], xc3[:, d, :],
                                           start=(d == 0), stop=(d == ND - 1))
                        return ins
                    p.step("te", te)

                    def ev(E, f=f):
                        return E.activation(hsb[:, f, :], cx.psA[:, (f % 4) * 512:(f % 4) * 512 + 512],
                                            AF.Relu, bias=b1s[:, f:f + 1])
                    p.step("act", ev)
                for tt in range(4):
                    ts_local = ch * 4 + tt
                    tglob = thalf * 2048 + ts_local * 128

                    def te2(E, e=e, tt=tt, tglob=tglob):
                        ins = None
                        for dc in range(2):
                            first = True
                            if e == 0:
                                ins = E.matmul(cx.psB[:, dc * 512:dc * 512 + 512],
                                               ctf[0:8, tglob:tglob + 128],
                                               b2s[0:8, dc * 512:(dc + 1) * 512],
                                               start=True, stop=False)
                                first = False
                            for f in range(16):
                                ins = E.matmul(cx.psB[:, dc * 512:dc * 512 + 512],
                                               hsb[:, f, tt * 128:(tt + 1) * 128],
                                               wb[:, f, dc * 512:(dc + 1) * 512],
                                               start=first, stop=(f == 15))
                                first = False
                        return ins
                    p.step("te", te2)

                    def dv(E, e=e, ts_local=ts_local):
                        cs = csb[:, ts_local, e:e + 1]
                        m0 = macc[:, ts_local, 0:512]
                        m1 = macc[:, ts_local, 512:1024]
                        if e == 0:
                            E.tensor_single_scalar(m0, cx.psB[:, 0:512], cs, ALU.mult)
                            return E.tensor_single_scalar(m1, cx.psB[:, 512:1024], cs, ALU.mult)
                        E.scalar_tensor_tensor(m0, cx.psB[:, 0:512], cs, m0, ALU.mult, ALU.add)
                        return E.scalar_tensor_tensor(m1, cx.psB[:, 512:1024], cs, m1, ALU.mult, ALU.add)
                    p.step("dve", dv)
        cb_g2 = cx.pf.take(1024) if thalf == 0 else cx.cb_g2
        cb_b2 = cx.pf.take(1024) if thalf == 0 else cx.cb_b2
        if thalf == 0:
            cx.cb_g2, cx.cb_b2 = cb_g2, cb_b2
            rowb3 = cx.pb.take(1024)
            for dst, srcap in ((cb_g2, W["B_ln"][2:3, :]), (cb_b2, W["B_ln"][3:4, :])):
                def lr(E, srcap=srcap):
                    return E.dma_start(out=rowb3[0:1, :], in_=srcap)
                p.step("gp", lr, inc=16)
                for half in range(2):
                    def te_(E, half=half):
                        return E.matmul(cx.psD[:, 0:512], cx.ones[0:1, :],
                                        rowb3[0:1, half * 512:(half + 1) * 512], start=True, stop=True)
                    p.step("te", te_)

                    def dv_(E, dst=dst, half=half):
                        return E.tensor_copy(dst[:, half * 512:(half + 1) * 512], cx.psD[:, 0:512])
                    p.step("dve", dv_)
        yt2 = cx.pb.take(1024) if thalf == 0 else cx.yt2
        if thalf == 0:
            cx.yt2 = yt2
        for s in range(16):
            i = thalf * 16 + s

            def ly2(E, i=i):
                return E.dma_start(out=yt2, in_=d_tm2[i * 128:(i + 1) * 128, :])
            p.step("sp", ly2, inc=16)

            def lx1(E, i=i):
                return E.dma_start(out=cx.x32b, in_=xin[i * 128:(i + 1) * 128, :])
            p.step("sp", lx1, inc=16)

            def dvz(E, i=i, s=s):
                E.tensor_copy(cx.x32a, yt2)
                E.tensor_add(cx.x32a, cx.x32a, macc[:, s, :])
                return E.reduce_sum(cx.stat[:, 0:1], cx.x32a, axis=AX.X)
            p.step("dve", dvz)

            def scz(E, i=i):
                return E.activation(cx.x32c, cx.x32a, AF.Square, accum_out=cx.stat[:, 1:2])
            p.step("act", scz)

            def dvz2(E, i=i):
                return ln_var_ops(p, E, cx.stat)
            p.step("dve", dvz2)

            def scz2(E, i=i):
                return E.activation(cx.stat[:, 3:4], cx.stat[:, 2:3], AF.Sqrt)
            p.step("act", scz2)

            def dvz3(E, i=i):
                st_ = cx.stat
                p.fence(E, E.reciprocal(st_[:, 3:4], st_[:, 3:4]))
                E.tensor_scalar(cx.x32c, cx.x32a, st_[:, 4:5], st_[:, 3:4], ALU.subtract, ALU.mult)
                E.tensor_mul(cx.x32c, cx.x32c, cb_g2)
                E.tensor_add(cx.x32c, cx.x32c, cb_b2)
                return E.reduce_sum(st_[:, 0:1], cx.x32c, axis=AX.X)
            p.step("dve", dvz3)

            def scz3(E, i=i):
                return E.activation(cx.x32a, cx.x32c, AF.Square, accum_out=cx.stat[:, 1:2])
            p.step("act", scz3)

            def dvz4(E, i=i):
                return ln_var_ops(p, E, cx.stat)
            p.step("dve", dvz4)

            def scz4(E, i=i):
                return E.activation(cx.stat[:, 3:4], cx.stat[:, 2:3], AF.Sqrt)
            p.step("act", scz4)

            def dvz5(E, i=i):
                st_ = cx.stat
                p.fence(E, E.reciprocal(st_[:, 3:4], st_[:, 3:4]))
                E.tensor_scalar(cx.x32a, cx.x32c, st_[:, 4:5], st_[:, 3:4], ALU.subtract, ALU.mult)
                return E.tensor_add(cx.x32b, cx.x32b, cx.x32a)
            p.step("dve", dvz5)

            def stx(E, i=i):
                return E.dma_start(out=xout[i * 128:(i + 1) * 128, :], in_=cx.x32b)
            p.step("sp", stx, inc=16)
    cx.pb.pop(); cx.pf.pop()


def W_dc_rearr(d_c, thalf):
    return d_c[thalf * 2048:(thalf + 1) * 2048, 0:16].rearrange("(s p) e -> p s e", p=128)



STAGES = ("a", "b", "c", "d")
DEBUG_TAPS = []
DEBUG_A8 = False


def build_program(stages):
    nc = bass.Bass("TRN2", target_bir_lowering=False)
    W = {}
    xin = nc.dram_tensor("xin", [T, D], F32, kind="ExternalInput").ap()
    xout = nc.dram_tensor("xout", [T, D], F32, kind="ExternalOutput").ap()
    dummy = np.zeros(1)
    HKEYS = {
        "A_wqkv": [1024, 3072], "A_wo": [1024, 1024], "A_wsmall": [1024, 12],
        "A_convs": [3, 1024, 4], "A_convmix": [4, 4], "A_mixbias": [1, 4], "A_rmsw": [1, 256],
        "B_wiT": [1024, 3072], "B_bi": [1, 3072], "B_woT": [1024, 1024], "B_bo": [1, 1024],
        "B_ln": [4, 1024], "B_wge": [1024, 12], "B_bge": [1, 12],
        "B_w1": [8, 1024, 2048], "B_b1": [8, 2048], "B_w2": [8, 2048, 1024], "B_b2": [8, 1024],
        "C_qw": [1024, 1024], "C_kvw": [1024, 512], "C_ow": [1024, 1024], "C_dw": [1024, 14],
        "C_pw": [1024, 2048], "C_gw": [1024, 1024], "C_alpha": [1, 1],
        "D_g12": [2, 1024], "D_wq": [1024, 1024], "D_bq": [1, 1024], "D_wdown": [1024, 64],
        "D_bdown": [1, 64], "D_wupk": [64, 1024], "D_wupv": [64, 1024],
        "D_bupk": [1, 1024], "D_bupv": [1, 1024], "D_wo": [1024, 1024],
        "D_bo": [1, 1024], "D_wg": [1024, 8], "D_bg": [1, 8], "D_we": [8, 1024, 1024],
        "K_ident": [128, 128], "K_ones": [128, 128], "K_headmask": [128, 128],
        "K_m_sl_bd": [128, 128], "K_m_su_bd": [128, 128], "K_m_tu_bd": [128, 128],
        "K_dft_l": [128, 128], "K_idft_l": [128, 128], "K_mlow_neg": [256, 256],
        "K_m_d_neg": [128, 128], "K_rep4": [4, 16],
    }
    for k, shp in HKEYS.items():
        W[k] = nc.dram_tensor(k, shp, F32, kind="ExternalInput").ap()
    scr = {}
    for nm in ("tm0", "tm1", "tm2", "tm3"):
        scr[nm] = nc.dram_tensor("s_" + nm, [T, D], BF16, kind="Internal").ap()
    for nm in ("fm0", "fm1", "fm2", "fm3", "fm4"):
        scr[nm] = nc.dram_tensor("s_" + nm, [D, T], BF16, kind="Internal").ap()
    scr["c32"] = nc.dram_tensor("s_c32", [T, 16], F32, kind="Internal").ap()
    scr["vkb"] = nc.dram_tensor("s_vkb", [T, 2048], BF16, kind="Internal").ap()
    scr["g12"] = nc.dram_tensor("s_g12", [16, T], F32, kind="Internal").ap()
    scr["cT"] = nc.dram_tensor("s_cT", [16, T], BF16, kind="Internal").ap()
    scr["xmid"] = [nc.dram_tensor(f"s_xm{i}", [T, D], F32, kind="Internal").ap() for i in range(2)]

    p = Prog(nc)
    cx = Ctx(p, W)
    cx.sq_bf = cx.pb.take(512)
    cx.negc_bf = cx.pb.take(1)
    cx.negcT_bf = cx.pb.take(16)
    cx.rd_bf = cx.pb.take(512)
    cx.ctrow_bf = cx.pb.take(128)
    cx.negc_bc = cx.pf.take(16)
    cx.rowbf = cx.pb.take(4096)
    cx.rd32 = cx.pf.take(512)
    cx.rdb32 = cx.pf.take(512)
    cx.o32 = cx.pf.take(512)
    cx.x32a = cx.pf.take(1024)
    cx.x32b = cx.pf.take(1024)
    cx.x32c = cx.pf.take(1024)
    cx.epsc = cx.pf.take(4)
    p.start_block()
    cx.load_consts()

    def _eps(E):
        E.memset(cx.epsc[:, 0:1], 1e-6)
        return E.memset(cx.epsc[:, 1:2], 1e-5)
    p.step("dve", _eps)

    bufs = [xin]
    for i in range(len(stages) - 1):
        bufs.append(scr["xmid"][i % 2])
    bufs.append(xout)
    for si, s in enumerate(stages):
        args = (cx, bufs[si], bufs[si + 1], scr)
        if s == "b":
            emit_stage_b(*args)
        elif s == "a":
            emit_stage_a(*args)
        elif s == "c":
            emit_stage_c(*args)
        elif s == "d":
            emit_stage_d(*args)
        else:
            raise ValueError(s)
    for key in DEBUG_TAPS:
        src = scr[key]
        dbg = nc.dram_tensor("dbg_" + key, list(src.shape), src.dtype, kind="ExternalOutput").ap()

        def cp(E, src=src, dbg=dbg):
            return E.dma_start(out=dbg, in_=src)
        p.step("sp", cp, inc=16)

    final_cnt = p.cnt["flow"]

    def fin(E):
        E.wait_ge(p.sems["flow"], final_cnt)
    p.block.vector(fin)
    p.close()
    return nc


def emit_stage_a(cx, xin, xout, scr):
    p = cx.p
    W = cx.W
    d_tm0 = scr["tm0"]
    d_q, d_k, d_v, d_kb = scr["fm1"], scr["fm2"], scr["fm3"], scr["fm4"]
    d_xa = scr["fm0"]
    d_vtm, d_ktm = scr["tm1"], scr["tm3"]
    d_vkb = scr["vkb"]
    d_g = scr["g12"]
    d_gtm = scr["c32"]
    d_ofm = d_xa

    emit_ln_tm(cx, xin, d_tm0)
    emit_transpose(cx, d_tm0, d_xa, T, D)
    for i, dst in ((0, d_q), (1, d_k), (2, d_v)):
        emit_fm_matmul(cx, d_xa, W["A_wqkv"][:, i * 1024:(i + 1) * 1024], dst, D, D)
    cx.pb.push()
    ws_sb = cx.pb.take3(ND, 12)

    def lws(E):
        return E.dma_start(out=ws_sb, in_=W["A_wsmall"].rearrange("(c p) f -> p c f", p=128))
    p.step("gp", lws, inc=16)
    xcs = cx.pb.take3(ND, TC)
    gout = cx.pf.take(512)
    for ch in range(NCH):
        def lx(E, ch=ch):
            return E.dma_start(out=xcs, in_=d_xa.rearrange("(c p) t -> p c t", p=128)[:, :, ch * TC:(ch + 1) * TC])
        p.step("sp", lx, inc=16)

        def te(E, ch=ch):
            ins = None
            for d in range(ND):
                ins = E.matmul(cx.psC[0:12, 0:512], ws_sb[:, d, :], xcs[:, d, :],
                               start=(d == 0), stop=(d == ND - 1))
            return ins
        p.step("te", te)

        def dv(E, ch=ch):
            return E.tensor_copy(gout[0:12, :], cx.psC[0:12, 0:512])
        p.step("dve", dv)

        def stg(E, ch=ch):
            return E.dma_start(out=d_g[0:12, ch * TC:(ch + 1) * TC], in_=gout[0:12, :])
        p.step("sp", stg, inc=16)
    cx.pb.pop()
    cx.pb.push(); cx.pf.push()
    convw = cx.pf.take(3 * 8 * 4).rearrange("p (j c k) -> p j c k", j=3, c=8)

    def lcw(E):
        return E.dma_start(out=convw,
                           in_=W["A_convs"].rearrange("j (c p) k -> p j c k", p=128),
                           allow_slow_non_contiguous=True)
    p.step("gp", lcw, inc=16)
    pad = cx.pb.take(3 + T + 1)
    acc32 = cx.pb.take(2 * T).bitcast(F32)
    t32 = cx.pb.take(2 * T).bitcast(F32)
    obf = cx.pb.take(T)
    for j, dst in ((0, d_q), (1, d_k), (2, d_v)):
        for c in range(ND):
            def ld(E, j=j, c=c, dst=dst):
                return E.dma_start(out=pad[:, 3:3 + T], in_=dst[c * 128:(c + 1) * 128, :])
            p.step("sp", ld, inc=16)

            def dvc(E, j=j, c=c):
                p.fence(E, E.memset(pad[:, 0:3], 0.0))
                E.tensor_single_scalar(acc32, pad[:, 0:T], convw[:, j, c, 0:1], ALU.mult)
                for k in range(1, 4):
                    E.scalar_tensor_tensor(acc32, pad[:, k:k + T], convw[:, j, c, k:k + 1],
                                           acc32, ALU.mult, ALU.add)
                return E.engine_nop()
            p.step("dve", dvc)

            def scc(E, j=j, c=c):
                if j < 2:
                    E.activation(t32, acc32, AF.Silu)
                    return E.activation(obf, t32, AF.Silu)
                return E.activation(obf, acc32, AF.Silu)
            p.step("act", scc)

            def stc(E, j=j, c=c, dst=dst):
                return E.dma_start(out=dst[c * 128:(c + 1) * 128, :], in_=obf)
            p.step("sp", stc, inc=16)
    cx.pb.pop(); cx.pf.pop()
    cx.pb.push(); cx.pf.push()
    w12 = cx.pf.take(4)
    mixb = cx.pf.take(1)
    gpad = cx.pb.take(2 * (4 + T)).bitcast(F32)
    gw32 = cx.pb.take(2 * T).bitcast(F32)

    def dvg0(E):
        p.fence(E, E.memset(w12[:, :], 0.0))
        return E.memset(w12[:, 3:4], 1.0)
    p.step("dve", dvg0)

    def lg(E):
        p.dma_in_step(E, gpad[0:4, 3:3 + T], d_g[8:12, :])
        p.dma_in_step(E, gpad[4:12, 3:3 + T], d_g[0:8, :])
        p.dma_in_step(E, mixb[0:4, 0:1], W["A_mixbias"][0:1, :].rearrange("o p -> p o"),
                      allow_slow_non_contiguous=True)
        return E.dma_start(out=w12[0:4, :], in_=W["A_convmix"][:, :])
    p.step("sp", lg, inc=16)

    def dvg(E):
        return E.memset(gpad[0:12, 0:3], 0.0)
    p.step("dve", dvg)

    def dvg2(E):
        E.tensor_single_scalar(gw32[0:12, :], gpad[0:12, 0:T], w12[0:12, 0:1], ALU.mult)
        for k in range(1, 4):
            E.scalar_tensor_tensor(gw32[0:12, :], gpad[0:12, k:k + T], w12[0:12, k:k + 1],
                                   gw32[0:12, :], ALU.mult, ALU.add)
        return E.engine_nop()
    p.step("dve", dvg2)

    def scg(E):
        return E.activation(gw32[0:4, :], gw32[0:4, :], AF.Silu)
    p.step("act", scg)

    def dvg3(E):
        return E.tensor_single_scalar(gw32[0:4, :], gw32[0:4, :], mixb[0:4, 0:1], ALU.add)
    p.step("dve", dvg3)

    def scg2(E):
        return E.activation(gw32[0:12, :], gw32[0:12, :], AF.Sigmoid)
    p.step("act", scg2)

    def stg2(E):
        return E.dma_start(out=d_g[0:12, :], in_=gw32[0:12, :])
    p.step("sp", stg2, inc=16)
    g12bf = cx.pb.take(T)
    gtt = cx.pf.take(16)

    def dvg4(E):
        return E.tensor_copy(g12bf[0:12, :], gw32[0:12, :])
    p.step("dve", dvg4)
    for i in range(NT):
        def teg(E, i=i):
            return E.transpose(cx.psD.bitcast(BF16)[0:128, 0:12],
                               g12bf[0:12, i * 128:(i + 1) * 128], cx.ident[0:12, 0:12])
        p.step("te", teg)

        def dvg5(E, i=i):
            return E.tensor_copy(gtt[:, 0:12], cx.psD.bitcast(BF16)[0:128, 0:12])
        p.step("dve", dvg5)

        def stg3(E, i=i):
            return E.dma_start(out=d_gtm[i * 128:(i + 1) * 128, 0:12], in_=gtt[:, 0:12])
        p.step("sp", stg3, inc=16)
    cx.pb.pop(); cx.pf.pop()

    cx.pb.push(); cx.pf.push()
    qkt = cx.pb.take3(2, TC)
    sqbf = cx.pb.take(TC)
    rst32 = cx.pf.take(TC)
    brow = cx.pb.take(TC)
    bbc32 = cx.pf.take(TC)
    for src_d, dst_kb in ((d_q, None), (d_k, d_kb)):
        for h in range(4):
            for ch in range(NCH):
                def ld(E, src_d=src_d, h=h, ch=ch):
                    return E.dma_start(out=qkt, in_=src_d.rearrange("(c p) t -> p c t", p=128)
                                       [:, 2 * h:2 * h + 2, ch * TC:(ch + 1) * TC])
                p.step("sp", ld, inc=16)
                for c2 in range(2):
                    def sq(E, c2=c2):
                        return E.activation(sqbf, qkt[:, c2, :], AF.Square)
                    p.step("act", sq)

                    def tes(E, c2=c2):
                        return E.matmul(cx.psC[:, 0:512], cx.ones, sqbf,
                                        start=(c2 == 0), stop=(c2 == 1))
                    p.step("te", tes)

                def scr_(E, h=h, ch=ch):
                    return E.activation(rst32, cx.psC[:, 0:512], AF.Sqrt, bias=cx.epsc[:, 0:1])
                p.step("act", scr_)

                def dvn(E, h=h, ch=ch):
                    E.reciprocal(rst32, rst32)
                    E.tensor_mul(qkt[:, 0, :], qkt[:, 0, :], rst32)
                    return E.tensor_mul(qkt[:, 1, :], qkt[:, 1, :], rst32)
                p.step("dve", dvn)

                def stn(E, src_d=src_d, h=h, ch=ch):
                    return E.dma_start(out=src_d.rearrange("(c p) t -> p c t", p=128)
                                       [:, 2 * h:2 * h + 2, ch * TC:(ch + 1) * TC], in_=qkt)
                p.step("sp", stn, inc=16)
                if dst_kb is not None:
                    def ldb(E, h=h, ch=ch):
                        return E.dma_start(out=brow[0:1, :], in_=d_g[4 + h:5 + h, ch * TC:(ch + 1) * TC])
                    p.step("gp", ldb, inc=16)

                    def teb(E, h=h, ch=ch):
                        return E.matmul(cx.psD[:, 0:512], cx.ones[0:1, :], brow[0:1, :],
                                        start=True, stop=True)
                    p.step("te", teb)

                    def dvb(E, h=h, ch=ch):
                        E.tensor_copy(bbc32, cx.psD[:, 0:512])
                        E.tensor_mul(qkt[:, 0, :], qkt[:, 0, :], bbc32)
                        return E.tensor_mul(qkt[:, 1, :], qkt[:, 1, :], bbc32)
                    p.step("dve", dvb)

                    def stkb(E, h=h, ch=ch):
                        return E.dma_start(out=d_kb.rearrange("(c p) t -> p c t", p=128)
                                           [:, 2 * h:2 * h + 2, ch * TC:(ch + 1) * TC], in_=qkt)
                    p.step("sp", stkb, inc=16)
    cx.pb.pop(); cx.pf.pop()
    emit_transpose(cx, d_k, d_ktm, D, T)
    emit_transpose(cx, d_v, d_vtm, D, T)
    cx.pb.push(); cx.pf.push()
    vk_sb = cx.pb.take(2048)
    vkb_sb = cx.pb.take(2048)
    bt32 = cx.pf.take(16)
    for i in range(NT):
        def ldv(E, i=i):
            p.dma_in_step(E, vk_sb[:, 0:1024], d_vtm[i * 128:(i + 1) * 128, :])
            p.dma_in_step(E, vk_sb[:, 1024:2048], d_ktm[i * 128:(i + 1) * 128, :])
            return E.dma_start(out=bt32, in_=d_gtm[i * 128:(i + 1) * 128, :])
        p.step("sp", ldv, inc=16)

        def dvv(E, i=i):
            ins = None
            for h in range(4):
                E.tensor_single_scalar(vkb_sb[:, h * 512:h * 512 + 256],
                                       vk_sb[:, h * 256:(h + 1) * 256],
                                       bt32[:, 4 + h:5 + h], ALU.mult)
                ins = E.tensor_single_scalar(vkb_sb[:, h * 512 + 256:(h + 1) * 512],
                                             vk_sb[:, 1024 + h * 256:1024 + (h + 1) * 256],
                                             bt32[:, 4 + h:5 + h], ALU.mult)
            return ins
        p.step("dve", dvv)

        def stv(E, i=i):
            return E.dma_start(out=d_vkb[i * 128:(i + 1) * 128, :], in_=vkb_sb)
        p.step("sp", stv, inc=16)
    cx.pb.pop(); cx.pf.pop()

    cx.pb.push(); cx.pf.push()
    msl_sb = cx.pb.take(128)
    msu_sb = cx.pb.take(128)
    mtu_sb = cx.pb.take(128)

    def lm(E):
        p.dma_in_step(E, msl_sb, W["K_m_sl_bd"][:, :])
        p.dma_in_step(E, msu_sb, W["K_m_su_bd"][:, :])
        return E.dma_start(out=mtu_sb, in_=W["K_m_tu_bd"][:, :])
    p.step("gp", lm, inc=16)
    q_sb = cx.pb.take(4 * 256).rearrange("p (h c t) -> p h c t", h=4, c=2)
    k_sb = cx.pb.take(4 * 256).rearrange("p (h c t) -> p h c t", h=4, c=2)
    kb_sb = cx.pb.take(4 * 256).rearrange("p (h c t) -> p h c t", h=4, c=2)
    vkbg = cx.pb.take(4 * 512).rearrange("p (h f) -> p h f", h=4)
    ktm4 = cx.pb.take(4 * 4 * 256).rearrange("p (h c f) -> p h c f", h=4, c=4)
    a_sb = cx.pb.take(4 * 128).rearrange("p (h f) -> p h f", h=4)
    c_sb = cx.pb.take(4 * 5 * 128).rearrange("p (h l f) -> p h l f", h=4, l=5)
    uw_sb = cx.pb.take(4 * 512).rearrange("p (h f) -> p h f", h=4)
    wfm_sb = cx.pb.take(4 * 2 * 128).rearrange("p (h c f) -> p h c f", h=4, c=2)
    s_sb = cx.pb.take(4 * 512).rearrange("p (h f) -> p h f", h=4)
    at_sb = cx.pb.take(4 * 32).rearrange("p (h f) -> p h f", h=4)
    ua_sb = cx.pb.take(4 * 256).rearrange("p (h f) -> p h f", h=4)
    o_sb = cx.pb.take(4 * 2 * 128).rearrange("p (h c f) -> p h c f", h=4, c=2)

    def z0(E):
        return E.memset(s_sb[:, :, :], 0.0)
    p.step("dve", z0)
    NG = 32
    for g in range(NG):
        def lg2(E, g=g):
            for dd, dst in ((d_q, q_sb), (d_k, k_sb), (d_kb, kb_sb)):
                p.dma_in_step(E, dst, dd.rearrange("(c p) t -> p c t", p=128)
                              .rearrange("p (h c) t -> p h c t", h=4)[:, :, :, g * 128:(g + 1) * 128])
            p.dma_in_step(E, vkbg, d_vkb.rearrange("(g p) f -> p g f", p=128)
                          .rearrange("p g (h f) -> p g h f", h=4)[:, g, :, :])
            return E.dma_start(out=ktm4[0:32, :, :, :],
                               in_=d_ktm.rearrange("(g c j) (h f) -> j g h c f", c=4, j=32, h=4)[:, g, :, :, :])
        p.step("sp", lg2, inc=16)
        for h in range(4):
            def teg2(E, h=h):
                for c2 in range(2):
                    E.matmul(cx.psB[:, 0:128], kb_sb[:, h, c2, :], k_sb[:, h, c2, :],
                             start=(c2 == 0), stop=(c2 == 1))
                for c2 in range(2):
                    ins = E.matmul(cx.psB[:, 128:256], k_sb[:, h, c2, :], kb_sb[:, h, c2, :],
                                   start=(c2 == 0), stop=(c2 == 1))
                return ins
            p.step("te", teg2)

            def dvm(E, h=h):
                E.scalar_tensor_tensor(a_sb[:, h, :], cx.psB[:, 0:128], -1.0, msl_sb,
                                       ALU.mult, ALU.mult)
                return E.scalar_tensor_tensor(c_sb[:, h, 0, :], cx.psB[:, 128:256], -1.0, msu_sb,
                                              ALU.mult, ALU.mult)
            p.step("dve", dvm)
            for lv in range(4):
                def tsq(E, h=h, lv=lv):
                    E.matmul(cx.psB[:, 512:640], c_sb[:, h, lv, :], a_sb[:, h, :],
                             start=True, stop=True)
                    return E.matmul(cx.psB[:, 640:768], a_sb[:, h, :], c_sb[:, h, lv, :],
                                    start=True, stop=True)
                p.step("te", tsq)

                def dsq(E, h=h, lv=lv):
                    E.tensor_copy(a_sb[:, h, :], cx.psB[:, 512:640])
                    return E.tensor_copy(c_sb[:, h, lv + 1, :], cx.psB[:, 640:768])
                p.step("dve", dsq)
            def cp0(E, h=h):
                return E.tensor_copy(uw_sb[:, h, :], vkbg[:, h, :])
            p.step("dve", cp0)
            for lv in range(5):
                def tap(E, h=h, lv=lv):
                    E.matmul(cx.psC[:, 0:512], cx.ident, uw_sb[:, h, :], start=True, stop=False)
                    return E.matmul(cx.psC[:, 0:512], c_sb[:, h, lv, :], uw_sb[:, h, :],
                                    start=False, stop=True)
                p.step("te", tap)

                def dap(E, h=h, lv=lv):
                    return E.tensor_copy(uw_sb[:, h, :], cx.psC[:, 0:512])
                p.step("dve", dap)
            def twt(E, h=h):
                E.transpose(cx.psB.bitcast(BF16)[:, 512:640], uw_sb[:, h, 256:384], cx.ident)
                return E.transpose(cx.psB.bitcast(BF16)[:, 640:768], uw_sb[:, h, 384:512], cx.ident)
            p.step("te", twt)

            def dwt(E, h=h):
                E.tensor_scalar_mul(wfm_sb[:, h, 0, :], cx.psB.bitcast(BF16)[:, 512:640], -1.0)
                return E.tensor_scalar_mul(wfm_sb[:, h, 1, :], cx.psB.bitcast(BF16)[:, 640:768], -1.0)
            p.step("dve", dwt)
        for cc in range(4):
            def tat(E, cc=cc):
                ins = None
                for h in range(4):
                    for c2 in range(2):
                        ins = E.matmul(cx.psD[0:32, h * 32:(h + 1) * 32],
                                       k_sb[:, h, c2, cc * 32:(cc + 1) * 32],
                                       q_sb[:, h, c2, cc * 32:(cc + 1) * 32],
                                       start=(c2 == 0), stop=(c2 == 1))
                return ins
            p.step("te", tat)

            def dat(E, cc=cc):
                ins = None
                for h in range(4):
                    ins = E.tensor_mul(at_sb[0:32, h, :], cx.psD[0:32, h * 32:(h + 1) * 32],
                                       mtu_sb[cc * 32:(cc + 1) * 32, cc * 32:(cc + 1) * 32])
                return ins
            p.step("dve", dat)

            def tua(E, cc=cc):
                ins = None
                for h in range(4):
                    E.matmul(cx.psB[0:32, h * 256:(h + 1) * 256], cx.ident[:, cc * 32:(cc + 1) * 32],
                             uw_sb[:, h, 0:256], start=True, stop=False)
                    for c2 in range(2):
                        ins = E.matmul(cx.psB[0:32, h * 256:(h + 1) * 256],
                                       wfm_sb[:, h, c2, cc * 32:(cc + 1) * 32],
                                       s_sb[:, h, c2 * 256:(c2 + 1) * 256],
                                       start=False, stop=(c2 == 1))
                return ins
            p.step("te", tua)

            def dua(E, cc=cc):
                ins = None
                for h in range(4):
                    ins = E.tensor_copy(ua_sb[0:32, h, :], cx.psB[0:32, h * 256:(h + 1) * 256])
                return ins
            p.step("dve", dua)

            def to_(E, cc=cc):
                ins = None
                for h in range(4):
                    for dvh in range(2):
                        for c2 in range(2):
                            E.matmul(cx.psC[:, h * 128 + dvh * 32:h * 128 + dvh * 32 + 32],
                                     s_sb[:, h, c2 * 256 + dvh * 128:c2 * 256 + dvh * 128 + 128],
                                     q_sb[:, h, c2, cc * 32:(cc + 1) * 32],
                                     start=(c2 == 0), stop=False)
                        ins = E.matmul(cx.psC[:, h * 128 + dvh * 32:h * 128 + dvh * 32 + 32],
                                       ua_sb[0:32, h, dvh * 128:(dvh + 1) * 128],
                                       at_sb[0:32, h, :], start=False, stop=True)
                return ins
            p.step("te", to_)

            def do_(E, cc=cc):
                ins = None
                for h in range(4):
                    for dvh in range(2):
                        ins = E.tensor_copy(o_sb[:, h, dvh, cc * 32:(cc + 1) * 32],
                                            cx.psC[:, h * 128 + dvh * 32:h * 128 + dvh * 32 + 32])
                return ins
            p.step("dve", do_)

            def tsu(E, g=g, cc=cc):
                ins = None
                for h in range(4):
                    for c2 in range(2):
                        ins = E.matmul(cx.psA[:, h * 512 + c2 * 256:h * 512 + (c2 + 1) * 256],
                                       ktm4[0:32, h, cc, c2 * 128:(c2 + 1) * 128],
                                       ua_sb[0:32, h, :],
                                       start=(g == 0 and cc == 0 and c2 == 0), stop=False,
                                       skip_group_check=True)
                return ins
            p.step("te", tsu)

            def dsu(E, cc=cc):
                ins = None
                for h in range(4):
                    ins = E.tensor_copy(s_sb[:, h, :], cx.psA[:, h * 512:(h + 1) * 512])
                return ins
            p.step("dve", dsu)

        def sto(E, g=g):
            ins = None
            for h in range(4):
                ins = p.dma_in_step(E, d_ofm.rearrange("(c p) t -> p c t", p=128)
                                    [:, 2 * h:2 * h + 2, g * 128:(g + 1) * 128],
                                    o_sb[:, h, :, :])
            return ins
        wait_at = p.cnt["flow"]

        def sto_body(E, sto=sto, wait_at=wait_at):
            E.wait_ge(p.sems["flow"], wait_at)
            sto(E)
        p.block.sync(sto_body)
    cx.pb.pop(); cx.pf.pop()

    if DEBUG_A8:
        def cpo(E):
            return E.dma_start(out=scr["tm2"].bitcast(BF16) if False else scr["tm2"], in_=d_ofm.rearrange("a b -> a b") if False else d_ofm)
        def cpo2(E):
            return E.dma_start(out=scr["tm2"].rearrange("a b -> (a b)"),
                               in_=d_ofm.rearrange("a b -> (a b)"))
        p.step("sp", cpo2, inc=16)

    cx.pb.push(); cx.pf.push()
    gam_bc = cx.pb.take(2 * T).bitcast(F32)
    mix_bc = cx.pb.take(2 * T).bitcast(F32)
    inv_bc = cx.pb.take(2 * T).bitcast(F32)
    vrow = cx.pb.take(T)
    orow = cx.pb.take(T)
    d1w = cx.pb.take(2 * T).bitcast(F32)
    emw = cx.pb.take(2 * T).bitcast(F32)
    omx = [cx.pb.take(2 * T).bitcast(F32) for _ in range(2)]
    sq2 = cx.pb.take(TC)
    rs2 = cx.pf.take(TC)
    rmsw = cx.pf.take(2)
    grow = cx.pb.take(TC)

    def lrw(E):
        return E.dma_start(out=rmsw, in_=W["A_rmsw"][0:1, :].rearrange("o (c p) -> p (o c)", p=128),
                           allow_slow_non_contiguous=True)
    p.step("sp", lrw, inc=16)
    for h in range(4):
        for which, dst in ((8 + h, gam_bc), (0 + h, mix_bc)):
            for ch in range(NCH):
                def ldr(E, which=which, ch=ch):
                    return E.dma_start(out=grow[0:1, :], in_=d_g[which:which + 1, ch * TC:(ch + 1) * TC])
                p.step("gp", ldr, inc=16)

                def teb2(E, ch=ch):
                    return E.matmul(cx.psD[:, 0:512], cx.ones[0:1, :], grow[0:1, :],
                                    start=True, stop=True)
                p.step("te", teb2)

                def dvb2(E, dst=dst, ch=ch):
                    return E.tensor_copy(dst[:, ch * TC:(ch + 1) * TC], cx.psD[:, 0:512])
                p.step("dve", dvb2)

        def dvi(E, h=h):
            return E.tensor_scalar(inv_bc, gam_bc, -1.0, 1.0, ALU.mult, ALU.add)
        p.step("dve", dvi)
        for dvh in range(2):
            c = 2 * h + dvh

            def ldo(E, c=c):
                p.dma_in_step(E, vrow, d_v[c * 128:(c + 1) * 128, :])
                return E.dma_start(out=orow, in_=d_ofm[c * 128:(c + 1) * 128, :])
            p.step("sp", ldo, inc=16)

            def dve1(E, dvh=dvh):
                E.tensor_mul(d1w, vrow, inv_bc)
                E.tensor_tensor_scan(emw, gam_bc, d1w, 0.0, ALU.mult, ALU.add)
                E.tensor_sub(emw, emw, orow)
                E.tensor_mul(emw, emw, mix_bc)
                return E.tensor_add(omx[dvh], emw, orow)
            p.step("dve", dve1)
        for ch in range(NCH):
            def sqh2(E, ch=ch):
                E.activation(sq2, omx[0][:, ch * TC:(ch + 1) * TC], AF.Square)
                return E.activation(cx.sq_bf, omx[1][:, ch * TC:(ch + 1) * TC], AF.Square)
            p.step("act", sqh2)

            def teh(E, ch=ch):
                E.matmul(cx.psC[:, 0:512], cx.ones, sq2, start=True, stop=False)
                return E.matmul(cx.psC[:, 0:512], cx.ones, cx.sq_bf, start=False, stop=True)
            p.step("te", teh)

            def sch(E, ch=ch):
                return E.activation(rs2, cx.psC[:, 0:512], AF.Sqrt, bias=cx.epsc[:, 1:2], scale=1.0 / 256.0)
            p.step("act", sch)

            def dvh2(E, h=h, ch=ch):
                E.reciprocal(rs2, rs2)
                for dvh in range(2):
                    E.tensor_mul(omx[dvh][:, ch * TC:(ch + 1) * TC],
                                 omx[dvh][:, ch * TC:(ch + 1) * TC], rs2)
                ins = None
                for dvh in range(2):
                    ins = E.tensor_single_scalar(omx[dvh][:, ch * TC:(ch + 1) * TC],
                                                 omx[dvh][:, ch * TC:(ch + 1) * TC],
                                                 rmsw[:, dvh:dvh + 1], ALU.mult)
                return ins
            p.step("dve", dvh2)

        def dvc2(E, h=h):
            E.tensor_copy(vrow, omx[0])
            return E.tensor_copy(orow, omx[1])
        p.step("dve", dvc2)

        def sto2(E, h=h):
            p.dma_in_step(E, d_ofm[(2 * h) * 128:(2 * h + 1) * 128, :], vrow)
            return E.dma_start(out=d_ofm[(2 * h + 1) * 128:(2 * h + 2) * 128, :], in_=orow)
        p.step("sp", sto2, inc=16)
    cx.pb.pop(); cx.pf.pop()

    cx.pb.push(); cx.pf.push()
    wo_sb = cx.pb.take3(ND, 1024)

    def lwo(E):
        return E.dma_start(out=wo_sb, in_=W["A_wo"].rearrange("(c p) f -> p c f", p=128))
    p.step("gp", lwo, inc=16)
    oc = cx.pb.take3(ND, TC)
    st = cx.stat
    for ch in range(NCH):
        def lo(E, ch=ch):
            return E.dma_start(out=oc, in_=d_ofm.rearrange("(c p) t -> p c t", p=128)[:, :, ch * TC:(ch + 1) * TC])
        p.step("sp", lo, inc=16)
        for tt in range(4):
            i = ch * 4 + tt

            def te2(E, tt=tt):
                ins = None
                for fc in range(2):
                    for d in range(ND):
                        ins = E.matmul(cx.psA[:, fc * 512:fc * 512 + 512],
                                       oc[:, d, tt * 128:(tt + 1) * 128],
                                       wo_sb[:, d, fc * 512:(fc + 1) * 512],
                                       start=(d == 0), stop=(d == ND - 1))
                return ins
            p.step("te", te2)

            def lxi(E, i=i):
                return E.dma_start(out=cx.x32b, in_=xin[i * 128:(i + 1) * 128, :])
            p.step("sp", lxi, inc=16)

            def dvf(E, i=i):
                E.tensor_copy(cx.x32a[:, 0:512], cx.psA[:, 0:512])
                E.tensor_copy(cx.x32a[:, 512:1024], cx.psA[:, 512:1024])
                return E.reduce_sum(st[:, 0:1], cx.x32a, axis=AX.X)
            p.step("dve", dvf)

            def scf(E, i=i):
                return E.activation(cx.x32c, cx.x32a, AF.Square, accum_out=st[:, 1:2])
            p.step("act", scf)

            def dvf2(E, i=i):
                return ln_var_ops(p, E, cx.stat)
            p.step("dve", dvf2)

            def scf2(E, i=i):
                return E.activation(st[:, 3:4], st[:, 2:3], AF.Sqrt)
            p.step("act", scf2)

            def dvf3(E, i=i):
                p.fence(E, E.reciprocal(st[:, 3:4], st[:, 3:4]))
                E.tensor_scalar(cx.x32c, cx.x32a, st[:, 4:5], st[:, 3:4], ALU.subtract, ALU.mult)
                return E.tensor_add(cx.x32b, cx.x32b, cx.x32c)
            p.step("dve", dvf3)

            def stf(E, i=i):
                return E.dma_start(out=xout[i * 128:(i + 1) * 128, :], in_=cx.x32b)
            p.step("sp", stf, inc=16)
    cx.pb.pop(); cx.pf.pop()


def emit_stage_c(cx, xin, xout, scr):
    p = cx.p
    W = cx.W
    d_tm0, d_xf_tm = scr["tm0"], scr["tm1"]
    d_xf_fm, d_hg, d_hp = scr["fm1"], scr["fm2"], scr["fm3"]
    d_yhat = scr["fm4"]
    st = cx.stat
    NC64 = 64
    HL = 29
    SLOT = 36

    emit_ln_tm(cx, xin, d_tm0)
    cx.pb.push()
    dft_sb = cx.pb.take(128)

    def ldft(E):
        return E.dma_start(out=dft_sb, in_=W["K_dft_l"][:, :])
    p.step("gp", ldft, inc=16)
    xt = cx.pb.take(1024)
    ft = cx.pb.take(1024)
    for i in range(NT):
        def lx(E, i=i):
            return E.dma_start(out=xt, in_=d_tm0[i * 128:(i + 1) * 128, :])
        p.step("sp", lx, inc=16)

        def te(E, i=i):
            E.matmul(cx.psA[:, 0:512], dft_sb, xt[:, 0:512], start=True, stop=True)
            return E.matmul(cx.psA[:, 512:1024], dft_sb, xt[:, 512:1024], start=True, stop=True)
        p.step("te", te)

        def dv(E, i=i):
            E.tensor_copy(ft[:, 0:512], cx.psA[:, 0:512])
            return E.tensor_copy(ft[:, 512:1024], cx.psA[:, 512:1024])
        p.step("dve", dv)

        def stf(E, i=i):
            return E.dma_start(out=d_xf_tm[i * 128:(i + 1) * 128, :], in_=ft)
        p.step("sp", stf, inc=16)
    cx.pb.pop()
    emit_transpose(cx, d_xf_tm, d_xf_fm, T, D)

    cx.pb.push(); cx.pf.push()
    wq_sb = cx.pb.take3(ND, 1024)
    wkv_sb = cx.pb.take3(ND, 512)
    wo_sb = cx.pb.take3(ND, 1024)
    hm_sb = cx.pb.take(128)
    rep_sb = cx.pb.take(16)
    mlow_sb = cx.pb.take(2 * 256).rearrange("p (t q) -> p t q", t=2)

    def lw(E):
        p.dma_in_step(E, wq_sb, W["C_qw"].rearrange("(c p) f -> p c f", p=128))
        p.dma_in_step(E, wkv_sb, W["C_kvw"].rearrange("(c p) f -> p c f", p=128))
        p.dma_in_step(E, wo_sb, W["C_ow"].rearrange("(c p) f -> p c f", p=128))
        p.dma_in_step(E, hm_sb, W["K_headmask"][:, :])
        p.dma_in_step(E, rep_sb[0:4, :], W["K_rep4"][:, :])
        return E.dma_start(out=mlow_sb, in_=W["K_mlow_neg"].rearrange("(t p) q -> p t q", p=128))
    p.step("gp", lw, inc=16)
    xlow = [cx.pb.take3(ND, 256) for _ in range(2)]
    xfrow = cx.pb.take(4096)
    for d in range(ND):
        def lxr(E, d=d):
            return E.dma_start(out=xfrow, in_=d_xf_fm[d * 128:(d + 1) * 128, :])
        p.step("sp", lxr, inc=16)

        def dvl(E, d=d):
            v3 = xfrow.rearrange("p (c f) -> p c f", f=64)
            E.tensor_copy(xlow[0][:, d, :].rearrange("p (c f) -> p c f", f=4), v3[:, :, 0:4])
            p.fence(E, E.memset(xlow[1][:, d, :], 0.0))
            return E.tensor_copy(xlow[1][:, d, :].rearrange("p (c f) -> p c f", f=4)[:, :, 1:4],
                                 v3[:, :, 33:36])
        p.step("dve", dvl)
    qf = [cx.pb.take3(ND, 256) for _ in range(2)]
    kvf = [cx.pb.take3(4, 256) for _ in range(2)]
    vtm = [cx.pb.take(2 * 4 * 65).rearrange("p (t g e) -> p t g e", t=2, g=4) for _ in range(2)]
    alow = [cx.pb.take3(ND, 256) for _ in range(2)]
    es = cx.pb.take(512)
    qst = cx.pb.take(256)
    kst = cx.pb.take(256)
    ptb = cx.pb.take(256)
    for wi in range(2):
        for f in range(ND):
            def teq(E, wi=wi, f=f):
                ins = None
                for d in range(ND):
                    ins = E.matmul(cx.psA[:, 0:256], wq_sb[:, d, f * 128:(f + 1) * 128],
                                   xlow[wi][:, d, :], start=(d == 0), stop=(d == ND - 1))
                return ins
            p.step("te", teq)

            def dvq(E, wi=wi, f=f):
                return E.tensor_scalar_mul(qf[wi][:, f, :], cx.psA[:, 0:256], 0.125)
            p.step("dve", dvq)
        for f in range(4):
            def tek(E, wi=wi, f=f):
                ins = None
                for d in range(ND):
                    ins = E.matmul(cx.psA[:, 0:256], wkv_sb[:, d, f * 128:(f + 1) * 128],
                                   xlow[wi][:, d, :], start=(d == 0), stop=(d == ND - 1))
                return ins
            p.step("te", tek)

            def dvk(E, wi=wi, f=f):
                return E.tensor_copy(kvf[wi][:, f, :], cx.psA[:, 0:256])
            p.step("dve", dvk)
        def mset(E, wi=wi):
            return E.memset(vtm[wi][:, :, :, :], 1.0)
        p.step("dve", mset)
        for tt in range(2):
            def tev(E, wi=wi, tt=tt):
                ins = None
                for gc in range(2):
                    for d in range(ND):
                        ins = E.matmul(cx.psB[:, gc * 128:(gc + 1) * 128],
                                       xlow[wi][:, d, tt * 128:(tt + 1) * 128],
                                       wkv_sb[:, d, 256 + gc * 128:256 + (gc + 1) * 128],
                                       start=(d == 0), stop=(d == ND - 1))
                return ins
            p.step("te", tev)

            def dvv(E, wi=wi, tt=tt):
                ins = None
                for g in range(4):
                    ins = E.tensor_copy(vtm[wi][:, tt, g, 0:64],
                                        cx.psB[:, g * 64:(g + 1) * 64])
                return ins
            p.step("dve", dvv)
        for h in range(16):
            g = h // 4
            qb, qt = 64 * (h % 2), h // 2
            kb, kt = 64 * (g % 2), g // 2

            def dvst(E, wi=wi, qb=qb, qt=qt, kb=kb, kt=kt):
                E.tensor_copy(qst[0:64, :], qf[wi][qb:qb + 64, qt, :])
                return E.tensor_copy(kst[0:64, :], kvf[wi][kb:kb + 64, kt, :])
            p.step("dve", dvst)
            for qi in range(2):
                def tes(E, wi=wi, qi=qi):
                    E.matmul(cx.psA[:, 0:256], cx.ident, mlow_sb[:, qi, :],
                             start=True, stop=False)
                    return E.matmul(cx.psA[:, 0:256], qst[0:64, qi * 128:(qi + 1) * 128],
                                    kst[0:64, :], start=False, stop=True)
                p.step("te", tes)

                def dvm(E, wi=wi, qi=qi):
                    return E.reduce_max(st[:, 8:9], cx.psA[:, 0:256], axis=AX.X, negate=True)
                p.step("dve", dvm)

                def sce(E, wi=wi, qi=qi):
                    return E.activation(es[:, 0:256], cx.psA[:, 0:256], AF.Exp,
                                        bias=st[:, 8:9], accum_out=st[:, 9:10])
                p.step("act", sce)

                def dvn(E, wi=wi, qi=qi):
                    p.fence(E, E.reciprocal(st[:, 10:11], st[:, 9:10]))
                    return E.tensor_single_scalar(es[:, 0:256], es[:, 0:256],
                                                  st[:, 10:11], ALU.mult)
                p.step("dve", dvn)

                def tet(E, wi=wi, qi=qi):
                    E.transpose(cx.psB.bitcast(BF16)[:, 0:128], es[:, 0:128], cx.ident)
                    return E.transpose(cx.psB.bitcast(BF16)[:, 128:256], es[:, 128:256], cx.ident)
                p.step("te", tet)

                def dvt(E, wi=wi, qi=qi):
                    return E.tensor_copy(ptb, cx.psB.bitcast(BF16)[:, 0:256])
                p.step("dve", dvt)

                def tpv(E, wi=wi, g=g, qi=qi):
                    ins = None
                    for kj in range(2):
                        ins = E.matmul(cx.psC[0:64, 0:128], vtm[wi][:, kj, g, 0:64],
                                       ptb[:, kj * 128:(kj + 1) * 128],
                                       start=(kj == 0), stop=(kj == 1))
                    return ins
                p.step("te", tpv)

                def dvo2(E, wi=wi, h=h, qi=qi):
                    return E.tensor_copy(
                        alow[wi][(h % 2) * 64:(h % 2) * 64 + 64, h // 2, qi * 128:(qi + 1) * 128],
                        cx.psC[0:64, 0:128])
                p.step("dve", dvo2)
        ofm = alow[wi]
        res = qf[wi]
        for f in range(ND):
            def teo(E, wi=wi, f=f, ofm=ofm):
                ins = None
                for d in range(ND):
                    ins = E.matmul(cx.psA[:, 0:256], wo_sb[:, d, f * 128:(f + 1) * 128],
                                   ofm[:, d, :], start=(d == 0), stop=(d == ND - 1))
                return ins
            p.step("te", teo)

            def dvp(E, wi=wi, f=f, res=res):
                return E.tensor_copy(res[:, f, :], cx.psA[:, 0:256])
            p.step("dve", dvp)
    alf = cx.pf.take(2)

    def lal(E):
        return E.dma_start(out=cx.rowbf[0:1, 0:1], in_=W["C_alpha"][0:1, 0:1])
    p.step("gp", lal, inc=16)

    def scal(E):
        return E.activation(st[0:1, 8:9], cx.rowbf[0:1, 0:1], AF.Sigmoid)
    p.step("act", scal)

    def dval(E):
        return E.tensor_copy(cx.negc_bf[0:1, 0:1], st[0:1, 8:9])
    p.step("dve", dval)

    def teal(E):
        return E.matmul(cx.psD[:, 0:1], cx.ones[0:1, :], cx.negc_bf[0:1, 0:1],
                        start=True, stop=True)
    p.step("te", teal)

    def dval2(E):
        return E.tensor_copy(alf[:, 0:1], cx.psD[:, 0:1])
    p.step("dve", dval2)
    yl = cx.pb.take(256)
    yh_row = cx.pb.take(4096)
    for d in range(ND):
        def dvbl(E, d=d):
            for wi in range(2):
                p.fence(E, E.tensor_sub(cx.o32[:, 0:256], qf[wi][:, d, :], xlow[wi][:, d, :]))
                p.fence(E, E.tensor_single_scalar(cx.o32[:, 0:256], cx.o32[:, 0:256], alf[:, 0:1], ALU.mult))
                p.fence(E, E.tensor_add(cx.rdb32[:, 0:256], cx.o32[:, 0:256], xlow[wi][:, d, :]))
                p.fence(E, E.tensor_copy(yl, cx.rdb32[:, 0:256]))
                ylv = yl.rearrange("p (c f) -> p c f", f=4)
                if wi == 0:
                    p.fence(E, E.tensor_copy(yh_row.rearrange("p (c f) -> p c f", f=64)[:, :, 0:4], ylv))
                else:
                    p.fence(E, E.tensor_copy(yh_row.rearrange("p (c f) -> p c f", f=64)[:, :, 33:36],
                                             ylv[:, :, 1:4]))
            return E.engine_nop()
        p.step("dve", dvbl)

        def styl(E, d=d):
            return E.dma_start(out=d_yhat[d * 128:(d + 1) * 128, :], in_=yh_row)
        p.step("sp", styl, inc=16)
    cx.pb.pop(); cx.pf.pop()

    cx.pb.push(); cx.pf.push()
    dwc_sb = cx.pf.take(8 * 14).rearrange("p (c k) -> p c k", c=8)

    def ldw(E):
        return E.dma_start(out=dwc_sb, in_=W["C_dw"].rearrange("(c p) k -> p c k", p=128))
    p.step("sp", ldw, inc=16)
    hpadbuf = cx.pb.take(NC64 * SLOT)
    hacc = cx.pb.take(2 * NC64 * HL).bitcast(F32)
    hg = cx.pb.take(NC64 * HL)
    xfr2 = cx.pb.take(4096)
    for wi in range(2):
        for d in range(ND):
            def lx2(E, wi=wi, d=d):
                return E.dma_start(out=xfr2, in_=d_xf_fm[d * 128:(d + 1) * 128, :])
            p.step("sp", lx2, inc=16)

            def dvp1(E, wi=wi, d=d):
                p.fence(E, E.memset(hpadbuf, 0.0))
                v3 = xfr2.rearrange("p (c f) -> p c f", f=64)
                src2 = v3[:, :, 4:33] if wi == 0 else v3[:, :, 36:64]
                dst = hpadbuf.rearrange("p (c s) -> p c s", s=SLOT)
                if wi == 0:
                    return E.tensor_copy(dst[:, :, 6:35], src2)
                return E.tensor_copy(dst[:, :, 6:34], src2)
            p.step("dve", dvp1)

            def dvp2(E, wi=wi, d=d):
                hv = hacc.rearrange("p (c l) -> p c l", l=HL)
                pv = hpadbuf.rearrange("p (c s) -> p c s", s=SLOT)
                E.tensor_single_scalar(hv, pv[:, :, 0:HL], dwc_sb[:, d, 7 * wi:7 * wi + 1], ALU.mult)
                for k in range(1, 7):
                    E.scalar_tensor_tensor(hv, pv[:, :, k:k + HL], dwc_sb[:, d, 7 * wi + k:7 * wi + k + 1],
                                           hv, ALU.mult, ALU.add)
                return E.engine_nop()
            p.step("dve", dvp2)

            def scp(E, wi=wi, d=d):
                return E.activation(hg, hacc, AF.Gelu)
            p.step("act", scp)

            def sth(E, wi=wi, d=d):
                return E.dma_start(out=d_hg[d * 128:(d + 1) * 128, wi * 2048:wi * 2048 + NC64 * HL],
                                   in_=hg)
            p.step("sp", sth, inc=16)
    pw_sb = cx.pb.take3(ND, 1024)
    hgc = cx.pb.take3(ND, 512)
    hp_out = cx.pb.take3(ND, 512)
    CW = [512, 512, 512, 320]
    for wi in range(2):
        def lpw(E, wi=wi):
            return E.dma_start(out=pw_sb, in_=W["C_pw"].rearrange("(c p) f -> p c f", p=128)
                               [:, :, wi * 1024:(wi + 1) * 1024])
        p.step("gp", lpw, inc=16)
        off = 0
        for ci, cw in enumerate(CW):
            def lhg(E, wi=wi, off=off, cw=cw):
                return E.dma_start(out=hgc[:, :, 0:cw],
                                   in_=d_hg.rearrange("(c p) f -> p c f", p=128)
                                   [:, :, wi * 2048 + off:wi * 2048 + off + cw])
            p.step("sp", lhg, inc=16)
            for f in range(ND):
                def tep(E, wi=wi, f=f, cw=cw):
                    ins = None
                    for d in range(ND):
                        ins = E.matmul(cx.psA[:, 0:cw], pw_sb[:, d, f * 128:(f + 1) * 128],
                                       hgc[:, d, 0:cw], start=(d == 0), stop=(d == ND - 1))
                    return ins
                p.step("te", tep)

                def dvpp(E, wi=wi, f=f, cw=cw):
                    return E.tensor_copy(hp_out[:, f, 0:cw], cx.psA[:, 0:cw])
                p.step("dve", dvpp)

            def stp(E, wi=wi, off=off, cw=cw):
                return E.dma_start(out=d_hp.rearrange("(c p) f -> p c f", p=128)
                                   [:, :, wi * 2048 + off:wi * 2048 + off + cw],
                                   in_=hp_out[:, :, 0:cw])
            p.step("sp", stp, inc=16)
            off += cw
    cx.pb.pop(); cx.pf.pop()

    cx.pb.push(); cx.pf.push()
    gw_sb = cx.pb.take3(ND, 1024)
    cm_sb = cx.pb.take3(ND, 64)
    g_sb = cx.pb.take3(ND, 64)
    xfr3 = cx.pb.take(4096)

    def lgw(E):
        return E.dma_start(out=gw_sb, in_=W["C_gw"].rearrange("(c p) f -> p c f", p=128))
    p.step("gp", lgw, inc=16)
    for d in range(ND):
        def lx3(E, d=d):
            return E.dma_start(out=xfr3, in_=d_xf_fm[d * 128:(d + 1) * 128, :])
        p.step("sp", lx3, inc=16)

        def dvcm(E, d=d):
            return E.tensor_copy(cm_sb[:, d, :], xfr3.rearrange("p (c f) -> p c f", f=64)[:, :, 0])
        p.step("dve", dvcm)
    for f in range(ND):
        def teg(E, f=f):
            ins = None
            for d in range(ND):
                ins = E.matmul(cx.psC[:, 0:64], gw_sb[:, d, f * 128:(f + 1) * 128],
                               cm_sb[:, d, :], start=(d == 0), stop=(d == ND - 1))
            return ins
        p.step("te", teg)

        def scg(E, f=f):
            return E.activation(g_sb[:, f, :], cx.psC[:, 0:64], AF.Sigmoid, scale=1.0 / 64.0)
        p.step("act", scg)
    hpr = cx.pb.take3(2, NC64 * HL)
    yhrow = cx.pb.take(4096)
    xfr4 = cx.pb.take(4096)
    hi32 = cx.pb.take(2 * NC64 * HL).bitcast(F32)
    for d in range(ND):
        def lhp(E, d=d):
            p.dma_in_step(E, hpr[:, 0, :], d_hp[d * 128:(d + 1) * 128, 0:NC64 * HL])
            p.dma_in_step(E, hpr[:, 1, :], d_hp[d * 128:(d + 1) * 128, 2048:2048 + NC64 * HL])
            p.dma_in_step(E, xfr4, d_xf_fm[d * 128:(d + 1) * 128, :])
            return E.dma_start(out=yhrow, in_=d_yhat[d * 128:(d + 1) * 128, :])
        p.step("sp", lhp, inc=16)

        def dvh(E, d=d):
            xv = xfr4.rearrange("p (c f) -> p c f", f=64)
            yv = yhrow.rearrange("p (c f) -> p c f", f=64)
            gb = g_sb[:, d, :][:, :, None].broadcast_to([128, NC64, HL])
            for wi in range(2):
                hv = hpr[:, wi, :].rearrange("p (c l) -> p c l", l=HL)
                if wi == 0:
                    E.tensor_tensor(hi32.rearrange("p (c l) -> p c l", l=HL), hv, gb, ALU.mult)
                    E.tensor_tensor(yv[:, :, 4:33], hi32.rearrange("p (c l) -> p c l", l=HL),
                                    xv[:, :, 4:33], ALU.add)
                else:
                    E.tensor_tensor(hi32.rearrange("p (c l) -> p c l", l=HL),
                                    hv, gb, ALU.mult)
                    E.tensor_tensor(yv[:, :, 36:64],
                                    hi32.rearrange("p (c l) -> p c l", l=HL)[:, :, 0:28],
                                    xv[:, :, 36:64], ALU.add)
            return E.engine_nop()
        p.step("dve", dvh)

        def sty(E, d=d):
            return E.dma_start(out=d_yhat[d * 128:(d + 1) * 128, :], in_=yhrow)
        p.step("sp", sty, inc=16)
    cx.pb.pop(); cx.pf.pop()

    emit_transpose(cx, d_yhat, d_xf_tm, D, T)
    cx.pb.push()
    idft_sb = cx.pb.take(128)

    def lidft(E):
        return E.dma_start(out=idft_sb, in_=W["K_idft_l"][:, :])
    p.step("gp", lidft, inc=16)
    yt2 = cx.pb.take(1024)
    for i in range(NT):
        def ly(E, i=i):
            return E.dma_start(out=yt2, in_=d_xf_tm[i * 128:(i + 1) * 128, :])
        p.step("sp", ly, inc=16)

        def tei(E, i=i):
            E.matmul(cx.psA[:, 0:512], idft_sb, yt2[:, 0:512], start=True, stop=True)
            return E.matmul(cx.psA[:, 512:1024], idft_sb, yt2[:, 512:1024], start=True, stop=True)
        p.step("te", tei)

        def lxi(E, i=i):
            return E.dma_start(out=cx.x32b, in_=xin[i * 128:(i + 1) * 128, :])
        p.step("sp", lxi, inc=16)

        def dvf(E, i=i):
            E.tensor_copy(cx.x32a[:, 0:512], cx.psA[:, 0:512])
            E.tensor_copy(cx.x32a[:, 512:1024], cx.psA[:, 512:1024])
            return E.reduce_sum(st[:, 0:1], cx.x32a, axis=AX.X)
        p.step("dve", dvf)

        def scf(E, i=i):
            return E.activation(cx.x32c, cx.x32a, AF.Square, accum_out=st[:, 1:2])
        p.step("act", scf)

        def dvf2(E, i=i):
            return ln_var_ops(p, E, cx.stat)
        p.step("dve", dvf2)

        def scf2(E, i=i):
            return E.activation(st[:, 3:4], st[:, 2:3], AF.Sqrt)
        p.step("act", scf2)

        def dvf3(E, i=i):
            p.fence(E, E.reciprocal(st[:, 3:4], st[:, 3:4]))
            E.tensor_scalar(cx.x32c, cx.x32a, st[:, 4:5], st[:, 3:4], ALU.subtract, ALU.mult)
            return E.tensor_add(cx.x32b, cx.x32b, cx.x32c)
        p.step("dve", dvf3)

        def stf(E, i=i):
            return E.dma_start(out=xout[i * 128:(i + 1) * 128, :], in_=cx.x32b)
        p.step("sp", stf, inc=16)
    cx.pb.pop()


def emit_stage_d(cx, xin, xout, scr):
    p = cx.p
    W = cx.W
    d_tm0, d_tm1, d_tm2 = scr["tm0"], scr["tm1"], scr["tm2"]
    d_fm0, d_fm1 = scr["fm0"], scr["fm1"]
    d_vsc = scr["tm3"]

    emit_ln_tm(cx, xin, d_tm0)
    cx.pf.push()
    cb_g = cx.pf.take(1024)
    emit_row_bcast(cx, W["D_g12"][0:1, :], cb_g, 1024)
    emit_ln_tm(cx, d_tm0, d_tm1, gbc=cb_g, rms=True, eps=1e-6, ld_eng="gp")
    cx.pf.pop()
    emit_transpose(cx, d_tm1, d_fm0, T, D)

    cx.pb.push(); cx.pf.push()
    wq_sb = cx.pb.take3(ND, 1024)
    wdn_sb = cx.pb.take3(ND, 64)
    wupk_sb = cx.pb.take(1024)
    wupv_sb = cx.pb.take(1024)
    wo_sb = cx.pb.take(16 * 1024).rearrange("p (h f) -> p h f", h=16)
    mneg_sb = cx.pb.take(128)

    def lw(E):
        p.dma_in_step(E, wq_sb, W["D_wq"].rearrange("(c p) f -> p c f", p=128))
        p.dma_in_step(E, wdn_sb, W["D_wdown"].rearrange("(c p) f -> p c f", p=128))
        p.dma_in_step(E, wupk_sb[0:64, :], W["D_wupk"][:, :])
        p.dma_in_step(E, wupv_sb[0:64, :], W["D_wupv"][:, :])
        p.dma_in_step(E, wo_sb[0:64, :, :],
                      W["D_wo"].rearrange("(h p) f -> p h f", p=64))
        return E.dma_start(out=mneg_sb, in_=W["K_m_d_neg"][:, :])
    p.step("gp", lw, inc=16)
    bq_sb = cx.pf.take(16)
    bupk_sb = cx.pf.take(16)
    bdn_sb = cx.pf.take(1)
    bupv_bc = cx.pf.take(1024)
    bo_bc = cx.pf.take(1024)

    def lb(E):
        p.dma_in_step(E, bq_sb[0:64, :], W["D_bq"][0:1, :].rearrange("o (h p) -> p (o h)", p=64),
                      allow_slow_non_contiguous=True)
        p.dma_in_step(E, bupk_sb[0:64, :], W["D_bupk"][0:1, :].rearrange("o (h p) -> p (o h)", p=64),
                      allow_slow_non_contiguous=True)
        return E.dma_start(out=bdn_sb[0:64, :], in_=W["D_bdown"][0:1, :].rearrange("o p -> p o"),
                           allow_slow_non_contiguous=True)
    p.step("sp", lb, inc=16)

    def sc8(E):
        return E.tensor_scalar_mul(bq_sb[0:64, :], bq_sb[0:64, :], 0.125)
    p.step("dve", sc8)
    emit_row_bcast(cx, W["D_bupv"][0:1, :], bupv_bc, 1024)
    emit_row_bcast(cx, W["D_bo"][0:1, :], bo_bc, 1024)

    xc = cx.pb.take3(ND, TC)
    qdm = cx.pb.take(8192).rearrange("p (t h) -> p t h", h=16)
    kdm = cx.pb.take(8192).rearrange("p (t h) -> p t h", h=16)
    dn_sb = cx.pb.take(512)
    vtm_sb = cx.pb.take3(4, 1024)
    v8_sb = cx.pb.take(64 * 64).rearrange("p (g e) -> p g e", g=64)
    es8 = cx.pb.take(128)
    p8_sb = cx.pb.take(128)
    odhm = cx.pb.take(16 * 512).rearrange("p (h t) -> p h t", h=16)
    xd_t = cx.pb.take(1024)
    x2_t = cx.pb.take(1024)
    st = cx.stat
    for ch in range(NCH):
        def lx(E, ch=ch):
            return E.dma_start(out=xc, in_=d_fm0.rearrange("(c p) t -> p c t", p=128)[:, :, ch * TC:(ch + 1) * TC])
        p.step("sp", lx, inc=16)

        def te_dn(E, ch=ch):
            ins = None
            for d in range(ND):
                ins = E.matmul(cx.psC[0:64, 0:512], wdn_sb[:, d, :], xc[:, d, :],
                               start=(d == 0), stop=(d == ND - 1))
            return ins
        p.step("te", te_dn)

        def ev_dn(E, ch=ch):
            return E.activation(dn_sb[0:64, :], cx.psC[0:64, 0:512], AF.Identity,
                                bias=bdn_sb[0:64, 0:1])
        p.step("act", ev_dn)
        for h in range(16):
            def te_q(E, h=h):
                ins = None
                for d in range(ND):
                    ins = E.matmul(cx.psA[0:64, 0:512], wq_sb[:, d, h * 64:(h + 1) * 64],
                                   xc[:, d, :], start=(d == 0), stop=(d == ND - 1))
                return ins
            p.step("te", te_q)

            def ev_q(E, h=h):
                return E.activation(qdm[0:64, :, h:h + 1], cx.psA[0:64, 0:512],
                                    AF.Identity, bias=bq_sb[0:64, h:h + 1], scale=0.125)
            p.step("act", ev_q)

            def te_k(E, h=h):
                return E.matmul(cx.psA[0:64, 512:1024], wupk_sb[0:64, h * 64:(h + 1) * 64],
                                dn_sb[0:64, :], start=True, stop=True)
            p.step("te", te_k)

            def ev_k(E, h=h):
                return E.activation(kdm[0:64, :, h:h + 1], cx.psA[0:64, 512:1024],
                                    AF.Identity, bias=bupk_sb[0:64, h:h + 1])
            p.step("act", ev_k)
        for ts in range(4):
            def te_v(E, ts=ts):
                for gc in range(2):
                    ins = E.matmul(cx.psB[:, gc * 512:(gc + 1) * 512],
                                   dn_sb[0:64, ts * 128:(ts + 1) * 128],
                                   wupv_sb[0:64, gc * 512:(gc + 1) * 512],
                                   start=True, stop=True)
                return ins
            p.step("te", te_v)

            def ev_v(E, ts=ts):
                E.scalar_tensor_tensor(vtm_sb[:, ts, 0:512], cx.psB[:, 0:512], 1.0,
                                       bupv_bc[:, 0:512], ALU.mult, ALU.add)
                return E.scalar_tensor_tensor(vtm_sb[:, ts, 512:1024], cx.psB[:, 512:1024], 1.0,
                                              bupv_bc[:, 512:1024], ALU.mult, ALU.add)
            p.step("dve", ev_v)

            def st_v(E, ch=ch, ts=ts):
                r = ch * 512 + ts * 128
                return E.dma_start(out=d_vsc[r:r + 128, :], in_=vtm_sb[:, ts, :])
            p.step("sp", st_v, inc=16)

        def ld_v8(E, ch=ch):
            src = d_vsc.rearrange("(r j) (g e) -> r (j g) e", j=8, g=16)
            src = src[ch * 64:(ch + 1) * 64].transpose([1, 0, 2])
            return E.dma_start(out=v8_sb, in_=src)
        p.step("sp", ld_v8, inc=16)
        for grp in range(64):
            def te_s(E, grp=grp):
                E.matmul(cx.psA[:, 0:128], cx.ident, mneg_sb, start=True, stop=False)
                return E.matmul(cx.psA[:, 0:128],
                                qdm[0:64, :, :].rearrange("p t h -> p (t h)")[:, grp * 128:(grp + 1) * 128],
                                kdm[0:64, :, :].rearrange("p t h -> p (t h)")[:, grp * 128:(grp + 1) * 128],
                                start=False, stop=True)
            p.step("te", te_s)

            def dv_m(E, grp=grp):
                return E.reduce_max(st[:, 8:9], cx.psA[:, 0:128], axis=AX.X, negate=True)
            p.step("dve", dv_m)

            def sc_e(E, grp=grp):
                return E.activation(es8, cx.psA[:, 0:128], AF.Exp, bias=st[:, 8:9],
                                    accum_out=st[:, 9:10])
            p.step("act", sc_e)

            def dv_n(E, grp=grp):
                p.fence(E, E.reciprocal(st[:, 10:11], st[:, 9:10]))
                return E.tensor_single_scalar(es8, es8, st[:, 10:11], ALU.mult)
            p.step("dve", dv_n)

            def te_t(E, grp=grp):
                return E.transpose(cx.psB.bitcast(BF16)[:, 0:128], es8, cx.ident)
            p.step("te", te_t)

            def dv_p(E, grp=grp):
                return E.tensor_copy(p8_sb, cx.psB.bitcast(BF16)[:, 0:128])
            p.step("dve", dv_p)

            def te_pv(E, grp=grp):
                return E.matmul(cx.psC[0:64, 0:128], v8_sb[:, grp, :], p8_sb,
                                start=True, stop=True)
            p.step("te", te_pv)

            def dv_o(E, grp=grp):
                src = cx.psC[0:64, 0:128].rearrange("p (j h) -> p j h", j=8)
                dst = odhm[0:64, :, grp * 8:grp * 8 + 8].transpose([0, 2, 1])
                return E.tensor_copy(dst, src)
            p.step("dve", dv_o)
        for ts in range(4):
            i = ch * 4 + ts

            def ld_xd(E, i=i):
                return E.dma_start(out=xd_t, in_=d_tm0[i * 128:(i + 1) * 128, :])
            p.step("sp", ld_xd, inc=16)

            def te_o(E, ts=ts):
                ins = None
                for fc in range(2):
                    for h in range(16):
                        ins = E.matmul(cx.psA[:, fc * 512:(fc + 1) * 512],
                                       odhm[0:64, h, ts * 128:(ts + 1) * 128],
                                       wo_sb[0:64, h, fc * 512:(fc + 1) * 512],
                                       start=(h == 0), stop=(h == 15))
                return ins
            p.step("te", te_o)

            def dv_x2(E, i=i):
                E.tensor_copy(cx.x32a, xd_t)
                E.tensor_add(cx.x32a, cx.x32a, bo_bc)
                E.tensor_add(cx.x32a[:, 0:512], cx.x32a[:, 0:512], cx.psA[:, 0:512])
                E.tensor_add(cx.x32a[:, 512:1024], cx.x32a[:, 512:1024], cx.psA[:, 512:1024])
                return E.tensor_copy(x2_t, cx.x32a)
            p.step("dve", dv_x2)

            def st_x2(E, i=i):
                return E.dma_start(out=d_tm2[i * 128:(i + 1) * 128, :], in_=x2_t)
            p.step("sp", st_x2, inc=16)
    cx.pb.pop(); cx.pf.pop()

    cx.pf.push()
    cb_g2 = cx.pf.take(1024)
    emit_row_bcast(cx, W["D_g12"][1:2, :], cb_g2, 1024)
    emit_ln_tm(cx, d_tm2, d_tm1, gbc=cb_g2, rms=True, eps=1e-6, ld_eng="gp")
    cx.pf.pop()
    emit_transpose(cx, d_tm1, d_fm1, T, D)

    cx.pb.push(); cx.pf.push()
    wg_sb = cx.pb.take3(ND, 8)
    gate_all = cx.pf.take(NT * 8).rearrange("p (t e) -> p t e", t=NT)
    bg_bc = cx.pf.take(8)
    g32 = cx.pf.take(8)

    def lwg(E):
        return E.dma_start(out=wg_sb, in_=W["D_wg"].rearrange("(c p) f -> p c f", p=128))
    p.step("gp", lwg, inc=16)
    emit_row_bcast(cx, W["D_bg"][0:1, :], bg_bc, 8)
    xc2 = cx.pb.take3(ND, TC)
    for ch in range(NCH):
        def lx2(E, ch=ch):
            return E.dma_start(out=xc2, in_=d_fm1.rearrange("(c p) t -> p c t", p=128)[:, :, ch * TC:(ch + 1) * TC])
        p.step("sp", lx2, inc=16)
        for tt in range(4):
            i = ch * 4 + tt

            def te_g(E, tt=tt):
                ins = None
                for d in range(ND):
                    ins = E.matmul(cx.psC[:, 0:8], xc2[:, d, tt * 128:(tt + 1) * 128],
                                   wg_sb[:, d, :], start=(d == 0), stop=(d == ND - 1))
                return ins
            p.step("te", te_g)

            def dv_g(E, i=i):
                p.fence(E, E.tensor_add(g32, cx.psC[:, 0:8], bg_bc))
                return E.reduce_max(st[:, 8:9], g32, axis=AX.X, negate=True)
            p.step("dve", dv_g)

            def sc_g(E, i=i):
                return E.activation(g32, g32, AF.Exp, bias=st[:, 8:9], accum_out=st[:, 9:10])
            p.step("act", sc_g)

            def dv_g2(E, i=i):
                p.fence(E, E.reciprocal(st[:, 10:11], st[:, 9:10]))
                return E.tensor_single_scalar(gate_all[:, i, :], g32, st[:, 10:11], ALU.mult)
            p.step("dve", dv_g2)
    cx.pb.pop()

    cx.pb.push()
    we_sb = cx.pb.take3(ND, 1024)
    macc2 = cx.pb.take(16 * 1024).rearrange("p (s f) -> p s f", s=16)
    xc3 = cx.pb.take3(ND, TC)
    x2b = cx.pb.take(1024)
    for thalf in range(2):
        for e in range(8):
            def lwe(E, e=e):
                return E.dma_start(out=we_sb, in_=W["D_we"][e].rearrange("(c p) f -> p c f", p=128))
            p.step("gp", lwe, inc=16)
            for ch in range(4):
                cc = thalf * 4 + ch

                def lx3(E, cc=cc):
                    return E.dma_start(out=xc3, in_=d_fm1.rearrange("(c p) t -> p c t", p=128)[:, :, cc * TC:(cc + 1) * TC])
                p.step("sp", lx3, inc=16)
                for ts in range(4):
                    s_loc = ch * 4 + ts
                    i = cc * 4 + ts

                    def te_z(E, ts=ts):
                        ins = None
                        for dc in range(2):
                            for d in range(ND):
                                ins = E.matmul(cx.psB[:, dc * 512:(dc + 1) * 512],
                                               xc3[:, d, ts * 128:(ts + 1) * 128],
                                               we_sb[:, d, dc * 512:(dc + 1) * 512],
                                               start=(d == 0), stop=(d == ND - 1))
                        return ins
                    p.step("te", te_z)

                    def dv_z(E, e=e, s_loc=s_loc, i=i):
                        gs = gate_all[:, i, e:e + 1]
                        m0 = macc2[:, s_loc, 0:512]
                        m1 = macc2[:, s_loc, 512:1024]
                        if e == 0:
                            E.tensor_single_scalar(m0, cx.psB[:, 0:512], gs, ALU.mult)
                            return E.tensor_single_scalar(m1, cx.psB[:, 512:1024], gs, ALU.mult)
                        E.scalar_tensor_tensor(m0, cx.psB[:, 0:512], gs, m0, ALU.mult, ALU.add)
                        return E.scalar_tensor_tensor(m1, cx.psB[:, 512:1024], gs, m1, ALU.mult, ALU.add)
                    p.step("dve", dv_z)
        for s in range(16):
            i = thalf * 16 + s

            def ld_x2(E, i=i):
                return E.dma_start(out=x2b, in_=d_tm2[i * 128:(i + 1) * 128, :])
            p.step("sp", ld_x2, inc=16)

            def ld_xi(E, i=i):
                return E.dma_start(out=cx.x32b, in_=xin[i * 128:(i + 1) * 128, :])
            p.step("sp", ld_xi, inc=16)

            def dvz(E, i=i, s=s):
                E.tensor_copy(cx.x32a, x2b)
                E.tensor_add(cx.x32a, cx.x32a, macc2[:, s, :])
                return E.reduce_sum(st[:, 0:1], cx.x32a, axis=AX.X)
            p.step("dve", dvz)

            def scz(E, i=i):
                return E.activation(cx.x32c, cx.x32a, AF.Square, accum_out=st[:, 1:2])
            p.step("act", scz)

            def dvz2(E, i=i):
                return ln_var_ops(p, E, cx.stat)
            p.step("dve", dvz2)

            def scz2(E, i=i):
                return E.activation(st[:, 3:4], st[:, 2:3], AF.Sqrt)
            p.step("act", scz2)

            def dvz3(E, i=i):
                p.fence(E, E.reciprocal(st[:, 3:4], st[:, 3:4]))
                E.tensor_scalar(cx.x32c, cx.x32a, st[:, 4:5], st[:, 3:4], ALU.subtract, ALU.mult)
                return E.tensor_add(cx.x32b, cx.x32b, cx.x32c)
            p.step("dve", dvz3)

            def stx(E, i=i):
                return E.dma_start(out=xout[i * 128:(i + 1) * 128, :], in_=cx.x32b)
            p.step("sp", stx, inc=16)
    cx.pb.pop(); cx.pf.pop()


def run_stages(stage_list, x_full, inputs):
    H = host_prep(inputs)
    nc = build_program(stage_list)
    in_maps = []
    for c in range(B):
        m = {"xin": np.ascontiguousarray(x_full[c])}
        m.update(H)
        in_maps.append(m)
    res = run_bass_kernel_spmd(nc, in_maps, core_ids=list(range(B)))
    out = np.stack([res.results[c]["xout"] for c in range(B)])
    if DEBUG_TAPS:
        return out, res.results
    return out


def kernel(**inputs):
    x = np.asarray(inputs["L0_attn_hidden_states"])
    return run_stages(("a", "b", "c", "d"), x, inputs)


# version 03
# speedup vs baseline: 1.0200x; 1.0200x over previous
import numpy as np
from contextlib import ExitStack

import concourse.bass as bass
import concourse.mybir as mybir
from concourse.bass_utils import run_bass_kernel_spmd

F32 = mybir.dt.float32
BF16 = mybir.dt.bfloat16
AF = mybir.ActivationFunctionType
ALU = mybir.AluOpType
AX = mybir.AxisListType

B, T, D = 8, 4096, 1024
NT = T // 128
ND = D // 128
TC = 512
NCH = T // TC
NEG_BIG = -1e30

ENGMAP = {"te": "tensor", "dve": "vector", "act": "scalar", "sp": "sync", "gp": "gpsimd"}



class Prog:
    def __init__(self, nc):
        self.nc = nc
        self.es = ExitStack()
        self.block = None
        self.sems = {}
        self.cnt = {}
        self.sem("flow")

    def sem(self, name):
        if name not in self.sems:
            self.sems[name] = self.es.enter_context(self.nc.semaphore(name))
            self.cnt[name] = 0
        return self.sems[name]

    def sbuf(self, name, shape, dt):
        return self.es.enter_context(self.nc.sbuf_tensor(name, shape, dt))

    def psum(self, name, shape, dt=F32):
        return self.es.enter_context(self.nc.psum_tensor(name, shape, dt))

    def start_block(self):
        self.block = self.es.enter_context(self.nc.Block())

    def close(self):
        self.es.close()

    DCH = ("dsp", "dgp", "dact")

    def _init_sync(self):
        if "dsp" not in self.sems:
            for c in self.DCH:
                self.sem(c)
            self.last_fw = {}
            self.last_dw = {}

    def dsnap(self):
        return {c: self.cnt[c] for c in self.DCH}

    def mark(self):
        self._init_sync()
        return (self.cnt["flow"], self.dsnap())

    def step(self, eng, fn, inc=1, wait_at=None, dml_at=None):
        self._init_sync()
        flow_tgt = self.cnt["flow"] if wait_at is None else wait_at
        dml_tgt = self.dsnap() if dml_at is None else dml_at
        need_flow = flow_tgt > self.last_fw.get(eng, -1)
        if need_flow:
            self.last_fw[eng] = flow_tgt
        need_d = []
        for c in self.DCH:
            tgt = dml_tgt.get(c, 0)
            if tgt > self.last_dw.get((eng, c), -1) and tgt > 0:
                self.last_dw[(eng, c)] = tgt
                need_d.append((c, tgt))
        sem_name = ("d" + ("sp" if eng == "sp" else ("gp" if eng == "gp" else "act"))) \
            if inc == 16 else "flow"

        def body(E):
            if need_flow:
                E.wait_ge(self.sems["flow"], flow_tgt)
            for c, tgt in need_d:
                E.wait_ge(self.sems[c], tgt)
            ins = fn(E)
            assert ins is not None, "step fn must return its last instruction"
            ins.then_inc(self.sems[sem_name], inc)
        getattr(self.block, ENGMAP[eng])(body)
        self.cnt[sem_name] += inc
        return (self.cnt["flow"], self.dsnap())

    def fence(self, E, ins):
        ins.then_inc(self.sems["flow"], 1)
        self.cnt["flow"] += 1
        E.wait_ge(self.sems["flow"], self.cnt["flow"])
        return ins

    def dma_in_step(self, E, out, in_, eng="sp", **kw):
        self._init_sync()
        ch = "d" + ("sp" if eng == "sp" else ("gp" if eng == "gp" else "act"))
        ins = E.dma_start(out=out, in_=in_, **kw)
        ins.then_inc(self.sems[ch], 16)
        self.cnt[ch] += 16
        return ins


class SbAlloc:
    def __init__(self, tensor, total):
        self.t = tensor
        self.total = total
        self.off = 0
        self.marks = []

    def take(self, n):
        self.off = (self.off + 3) & ~3
        assert self.off + n <= self.total, f"sbuf pool overflow {self.off}+{n}>{self.total}"
        ap = self.t[:, self.off:self.off + n]
        self.off += n
        return ap

    def take3(self, a, b):
        return self.take(a * b).rearrange("p (a b) -> p a b", a=a)

    def push(self):
        self.marks.append(self.off)

    def pop(self):
        self.off = self.marks.pop()



def _dft_matrices():
    tau = np.arange(64)
    rows = [np.cos(2 * np.pi * f * tau / 64) for f in range(33)]
    rows += [-np.sin(2 * np.pi * f * tau / 64) for f in range(1, 32)]
    D64 = np.stack(rows).astype(np.float32)
    G = np.zeros((64, 64), np.float32)
    for f in range(33):
        w = 1.0 if f in (0, 32) else 2.0
        G[:, f] = w * np.cos(2 * np.pi * f * tau / 64) / 64.0
    for f in range(1, 32):
        G[:, 32 + f] = -2.0 * np.sin(2 * np.pi * f * tau / 64) / 64.0
    return D64, G


def make_consts():
    c = {}
    c["ident"] = np.eye(128, dtype=np.float32)
    c["ones"] = np.ones((128, 128), np.float32)
    hm = np.zeros((128, 8, 16), np.float32)
    for tile in range(8):
        for p in range(128):
            hm[p, tile, (tile * 128 + p) // 64] = 1.0
    c["headmask"] = hm.reshape(128, 128)
    sl = np.zeros((128, 128), np.float32)
    su = np.zeros((128, 128), np.float32)
    tu = np.zeros((128, 128), np.float32)
    for ch in range(4):
        o = ch * 32
        blk = np.tril(np.ones((32, 32), np.float32), -1)
        sl[o:o + 32, o:o + 32] = blk
        su[o:o + 32, o:o + 32] = blk.T
        tu[o:o + 32, o:o + 32] = np.triu(np.ones((32, 32), np.float32))
    c["m_sl_bd"] = sl
    c["m_su_bd"] = su
    c["m_tu_bd"] = tu
    D64, G64 = _dft_matrices()
    dft_l = np.zeros((128, 128), np.float32)
    idft_l = np.zeros((128, 128), np.float32)
    for g in range(2):
        o = g * 64
        dft_l[o:o + 64, o:o + 64] = D64.T
        idft_l[o:o + 64, o:o + 64] = G64.T
    c["dft_l"] = dft_l
    c["idft_l"] = idft_l
    ft = np.arange(256)
    mlow = (ft[:, None] // 4 >= ft[None, :] // 4).astype(np.float32)
    c["mlow_neg"] = ((1.0 - mlow) * NEG_BIG).astype(np.float32)
    bd = np.kron(np.eye(8, dtype=np.float32), np.ones((16, 16), np.float32))
    c["m_d_neg"] = ((1.0 - bd) * NEG_BIG).astype(np.float32)
    rep = np.zeros((4, 16), np.float32)
    for h in range(16):
        rep[h // 4, h] = 1.0
    c["rep4"] = rep
    return c


def host_prep(I):
    H = {}
    A = lambda x: np.ascontiguousarray(np.asarray(x))
    H["A_wqkv"] = A(np.concatenate([I["L0_attn_Wq"], I["L0_attn_Wk"], I["L0_attn_Wv"]], axis=1))
    H["A_wo"] = A(I["L0_attn_Wo"])
    H["A_wsmall"] = A(np.concatenate([I["L0_attn_Wb"], I["L0_attn_Wdec"], I["L0_attn_Wmix"]], axis=1))
    H["A_convs"] = A(np.stack([I["L0_attn_conv_q"], I["L0_attn_conv_k"], I["L0_attn_conv_v"]]))
    H["A_convmix"] = A(I["L0_attn_conv_mix"])
    H["A_mixbias"] = A(I["L0_attn_mix_bias"]).reshape(1, 4)
    H["A_rmsw"] = A(I["L0_attn_rms_w"]).reshape(1, 256)
    H["B_wiT"] = A(I["L0_ffn_in_proj_w"].T)
    H["B_bi"] = A(I["L0_ffn_in_proj_b"]).reshape(1, 3072)
    H["B_woT"] = A(I["L0_ffn_out_proj_w"].T)
    H["B_bo"] = A(I["L0_ffn_out_proj_b"]).reshape(1, 1024)
    H["B_ln"] = A(np.stack([I["L0_ffn_ln1_g"], I["L0_ffn_ln1_b"], I["L0_ffn_ln2_g"], I["L0_ffn_ln2_b"]]))
    H["B_wge"] = A(np.concatenate([I["L0_ffn_w_group"],
                                   I["L0_ffn_w_expert"].transpose(1, 0, 2).reshape(1024, 8)], axis=1))
    H["B_bge"] = A(np.concatenate([I["L0_ffn_b_group"], I["L0_ffn_b_expert"].reshape(8)])).reshape(1, 12)
    H["B_w1"] = A(I["L0_ffn_w1"])
    H["B_b1"] = A(I["L0_ffn_b1"])
    H["B_w2"] = A(I["L0_ffn_w2"])
    H["B_b2"] = A(I["L0_ffn_b2"])
    H["C_qw"] = A(I["L1_attn_q_w"].T)
    H["C_kvw"] = A(np.concatenate([I["L1_attn_k_w"].T, I["L1_attn_v_w"].T], axis=1))
    H["C_ow"] = A(I["L1_attn_o_w"].T)
    H["C_dw"] = A(np.concatenate([I["L1_attn_dw_r"][:, 0, :], I["L1_attn_dw_i"][:, 0, :]], axis=1))
    H["C_pw"] = A(np.concatenate([I["L1_attn_pw_r"].T, I["L1_attn_pw_i"].T], axis=1))
    H["C_gw"] = A(I["L1_attn_gate_w"].T)
    H["C_alpha"] = A(I["L1_attn_alpha_raw"]).reshape(1, 1)
    H["D_g12"] = A(np.stack([I["L1_ffn_gamma1"], I["L1_ffn_gamma2"]]))
    H["D_wq"] = A(I["L1_ffn_Wq"])
    H["D_bq"] = A(I["L1_ffn_bq"]).reshape(1, 1024)
    H["D_wdown"] = A(I["L1_ffn_Wdown"])
    H["D_bdown"] = A(I["L1_ffn_bdown"]).reshape(1, 64)
    wup = np.asarray(I["L1_ffn_Wup"]).reshape(64, 16, 2, 64)
    H["D_wupk"] = A(wup[:, :, 0, :].reshape(64, 1024))
    H["D_wupv"] = A(wup[:, :, 1, :].reshape(64, 1024))
    bup = np.asarray(I["L1_ffn_bup"]).reshape(16, 2, 64)
    H["D_bupk"] = A(bup[:, 0, :].reshape(1, 1024))
    H["D_bupv"] = A(bup[:, 1, :].reshape(1, 1024))
    H["D_wo"] = A(I["L1_ffn_Wo"])
    H["D_bo"] = A(I["L1_ffn_bo"]).reshape(1, 1024)
    H["D_wg"] = A(I["L1_ffn_Wg"])
    H["D_bg"] = A(I["L1_ffn_bg"]).reshape(1, 8)
    H["D_we"] = A(I["L1_ffn_We"])
    for k, v in make_consts().items():
        H["K_" + k] = v
    return H


def ln_var_ops(p, E, st, eps=1e-5):
    E.tensor_scalar_mul(st[:, 4:5], st[:, 0:1], 1.0 / D)
    E.tensor_scalar_mul(st[:, 2:3], st[:, 1:2], 1.0 / D)
    p.fence(E, E.tensor_mul(st[:, 5:6], st[:, 4:5], st[:, 4:5]))
    p.fence(E, E.tensor_sub(st[:, 2:3], st[:, 2:3], st[:, 5:6]))
    return E.tensor_scalar_add(st[:, 2:3], st[:, 2:3], eps)



class Ctx:
    def __init__(self, p, W):
        self.p = p
        self.W = W
        self.pb = SbAlloc(p.sbuf("poolb", [128, 75776], BF16), 75776)
        self.pf = SbAlloc(p.sbuf("poolf", [128, 9216], F32), 9216)
        self.ident = self.pb.take(128)
        self.ones = self.pb.take(128)
        self.stat = self.pf.take(32)
        ps = p.psum("ps", [128, 4096], F32)
        self.psA = ps[:, 0:2048]
        self.psB = ps[:, 2048:3072]
        self.psC = ps[:, 3072:3584]
        self.psD = ps[:, 3584:4096]
        self.ps = ps

    def load_consts(self):
        p = self.p

        def g(E):
            p.dma_in_step(E, self.ident, self.W["K_ident"][:, :], eng="gp")
            return E.dma_start(out=self.ones, in_=self.W["K_ones"][:, :])
        p.step("gp", g, inc=16)


def bcast_row(cx, src_row_bf, dst_f32, n):
    p = cx.p

    def te(E):
        ins = None
        for ci in range((n + 511) // 512):
            w = min(512, n - ci * 512)
            ins = E.matmul(cx.psD[:, :w], cx.ones[0:1, :], src_row_bf[0:1, ci * 512:ci * 512 + w],
                           start=True, stop=True)
        return ins
    assert n <= 512
    p.step("te", te)

    def dv(E):
        return E.tensor_copy(dst_f32[:, :n], cx.psD[:, :n])
    p.step("dve", dv)


def emit_ln_tm(cx, xin, xout_tm, n_rows=T, gbc=None, bbc=None, rms=False, eps=1e-5,
               res_in=None, res_out=None, prenorm_cb=None, ld_eng="sp"):
    p = cx.p
    cx.pf.push(); cx.pb.push()
    sbx = [cx.pb.take(2048).bitcast(F32) for _ in range(2)]
    sbq = cx.pf.take(1024)
    sbr = [cx.pb.take(2048).bitcast(F32) for _ in range(2)] if res_in is not None else None
    sbo = [cx.pb.take(1024) for _ in range(2)]
    st = cx.stat
    n = n_rows // 128
    comp_done = [None] * n
    store_done = [None] * n
    pass_start = p.mark()
    st_eng = "gp" if ld_eng == "sp" else "sp"
    for i in range(n):
        b = i % 2

        def ld(E, i=i, b=b):
            if res_in is not None:
                p.dma_in_step(E, sbr[b], res_in[i * 128:(i + 1) * 128, :], eng=ld_eng)
            return E.dma_start(out=sbx[b], in_=xin[i * 128:(i + 1) * 128, :])
        wa, da = (pass_start if i < 2 else (comp_done[i - 2][0], store_done[i - 2][1]))
        p.step(ld_eng, ld, inc=16, wait_at=wa, dml_at=da)

        def dv(E, i=i, b=b):
            if res_in is not None:
                E.tensor_add(sbx[b], sbx[b], sbr[b])
            if prenorm_cb is not None:
                prenorm_cb(E, sbx[b], i)
            if rms:
                return E.engine_nop()
            return E.reduce_sum(st[:, 0:1], sbx[b], axis=AX.X)
        p.step("dve", dv)
        if res_out is not None:
            def st_res(E, i=i, b=b):
                return E.dma_start(out=res_out[i * 128:(i + 1) * 128, :], in_=sbx[b])
            p.step(st_eng, st_res, inc=16)

        def sc(E, i=i, b=b):
            return E.activation(sbq, sbx[b], AF.Square, accum_out=st[:, 1:2])
        p.step("act", sc)

        def dv2(E, i=i):
            if rms:
                E.tensor_scalar(st[:, 2:3], st[:, 1:2], 1.0 / D, eps, ALU.mult, ALU.add)
                return E.memset(st[:, 4:5], 0.0)
            return ln_var_ops(p, E, st, eps)
        p.step("dve", dv2)

        def sc2(E, i=i):
            return E.activation(st[:, 3:4], st[:, 2:3], AF.Sqrt)
        p.step("act", sc2)

        def dv3(E, i=i, b=b):
            p.fence(E, E.reciprocal(st[:, 3:4], st[:, 3:4]))
            tgt = sbq if gbc is not None else sbo[b]
            E.tensor_scalar(tgt, sbx[b], st[:, 4:5], st[:, 3:4], ALU.subtract, ALU.mult)
            if gbc is not None:
                E.tensor_mul(sbq, sbq, gbc)
                if bbc is not None:
                    return E.tensor_add(sbo[b], sbq, bbc)
                return E.tensor_copy(sbo[b], sbq)
            return E.engine_nop()
        comp_done[i] = p.step("dve", dv3)

        def stt(E, i=i, b=b):
            return E.dma_start(out=xout_tm[i * 128:(i + 1) * 128, :], in_=sbo[b])
        p.step(st_eng, stt, inc=16)
        store_done[i] = p.mark()
    cx.pf.pop(); cx.pb.pop()


def emit_transpose(cx, src_tm, dst_fm, n_rows, n_cols):
    p = cx.p
    cx.pb.push()
    sbi = cx.pb.take(n_cols)
    sbo = cx.pb.take(n_cols)
    for i in range(n_rows // 128):
        def ld(E, i=i):
            return E.dma_start(out=sbi, in_=src_tm[i * 128:(i + 1) * 128, :])
        p.step("sp", ld, inc=16)
        for h in range(0, n_cols // 128, 4):
            nj = min(4, n_cols // 128 - h)

            def te(E, i=i, h=h, nj=nj):
                ins = None
                for j in range(nj):
                    ins = E.transpose(cx.psA.bitcast(BF16)[:, (j * 1024):(j * 1024) + 128],
                                      sbi[:, (h + j) * 128:(h + j + 1) * 128], cx.ident)
                return ins
            p.step("te", te)

            def dv(E, i=i, h=h, nj=nj):
                ins = None
                for j in range(nj):
                    ins = E.tensor_copy(sbo[:, (h + j) * 128:(h + j + 1) * 128],
                                        cx.psA.bitcast(BF16)[:, (j * 1024):(j * 1024) + 128])
                return ins
            p.step("dve", dv)

        def stt(E, i=i):
            ins = None
            for j in range(n_cols // 128):
                ins = p.dma_in_step(E, dst_fm[j * 128:(j + 1) * 128, i * 128:(i + 1) * 128],
                                    sbo[:, j * 128:(j + 1) * 128])
            return ins
        wait_at = p.cnt["flow"]

        def body(E, stt=stt, wait_at=wait_at):
            E.wait_ge(p.sems["flow"], wait_at)
            stt(E)
        p.block.sync(body)
    cx.pb.pop()


def emit_fm_matmul(cx, x_fm, w_dram, y_fm, din, dout, bias_dram=None, bias_off=0,
                   func=AF.Identity, w_sb=None, out_f32_cb=None):
    p = cx.p
    cx.pb.push()
    ndt, nft = din // 128, dout // 128
    if w_sb is None:
        w_sb = cx.pb.take3(ndt, dout)

        def lw(E):
            return E.dma_start(out=w_sb, in_=w_dram.rearrange("(c p) f -> p c f", p=128))
        p.step("gp", lw, inc=16)
    bsb = None
    if bias_dram is not None:
        bsb = cx.pf.take(nft)

        def lb(E):
            return E.dma_start(out=bsb, in_=bias_dram[0:1, bias_off:bias_off + dout]
                               .rearrange("o (c p) -> p (o c)", p=128),
                               allow_slow_non_contiguous=True)
        p.step("sp", lb, inc=16)
    xc = [cx.pb.take3(ndt, TC) for _ in range(2)]
    yc = [cx.pb.take3(nft, TC) for _ in range(2)]
    pass_start = p.mark()
    ev_done = []
    ld_done = [None] * NCH
    ch_done = [None] * NCH
    for ch in range(NCH):
        b = ch % 2

        def lx(E, ch=ch, b=b):
            return E.dma_start(out=xc[b], in_=x_fm.rearrange("(c p) t -> p c t", p=128)[:, :, ch * TC:(ch + 1) * TC])
        wa, da = (pass_start if ch < 2 else (ch_done[ch - 2]))
        ld_done[ch] = p.step("sp", lx, inc=16, wait_at=wa, dml_at=da)
        for f in range(nft):
            fi = ch * nft + f

            def te(E, f=f, ch=ch, b=b):
                ins = None
                for d in range(ndt):
                    ins = E.matmul(cx.psA[:, (f % 4) * 512:(f % 4) * 512 + 512],
                                   w_sb[:, d, f * 128:(f + 1) * 128], xc[b][:, d, :],
                                   start=(d == 0), stop=(d == ndt - 1))
                return ins
            twa = pass_start[0] if fi < 4 else ev_done[fi - 4][0]
            p.step("te", te, wait_at=twa, dml_at=ld_done[ch][1])

            def ev(E, f=f, ch=ch, b=b):
                pa = cx.psA[:, (f % 4) * 512:(f % 4) * 512 + 512]
                if bsb is not None:
                    return E.activation(yc[b][:, f, :], pa, func, bias=bsb[:, f:f + 1])
                return E.activation(yc[b][:, f, :], pa, func)
            ev_done.append(p.step("act", ev, dml_at=ld_done[ch][1]))

        def st(E, ch=ch, b=b):
            return E.dma_start(out=y_fm.rearrange("(c p) t -> p c t", p=128)[:, :, ch * TC:(ch + 1) * TC],
                               in_=yc[b])
        ch_done[ch] = p.step("gp", st, inc=16)
    cx.pb.pop()


def emit_row_bcast(cx, src_row_ap, dst_f32, n):
    p = cx.p

    def lr(E):
        return E.dma_start(out=cx.rowbf[0:1, 0:n], in_=src_row_ap)
    p.step("gp", lr, inc=16)
    for c0 in range(0, n, 512):
        w = min(512, n - c0)

        def te(E, c0=c0, w=w):
            return E.matmul(cx.psD[:, 0:w], cx.ones[0:1, :], cx.rowbf[0:1, c0:c0 + w],
                            start=True, stop=True)
        p.step("te", te)

        def dv(E, c0=c0, w=w):
            return E.tensor_copy(dst_f32[:, c0:c0 + w], cx.psD[:, 0:w])
        p.step("dve", dv)



def emit_stage_b(cx, xin, xout, scr):
    p = cx.p
    W = cx.W
    d_tm0, d_fm0, d_fm1, d_fm2 = scr["tm0"], scr["fm0"], scr["fm1"], scr["fm2"]
    d_tm1, d_fm3, d_tm2, d_fm4 = scr["tm1"], scr["fm3"], scr["tm2"], scr["fm4"]
    d_c, d_cT = scr["c32"], scr["cT"]

    emit_ln_tm(cx, xin, d_tm0)
    emit_transpose(cx, d_tm0, d_fm0, T, D)
    cx.pb.push()
    wqk_sb = cx.pb.take3(ND, 2048)

    def lw(E):
        return E.dma_start(out=wqk_sb, in_=W["B_wiT"].rearrange("(c p) f -> p c f", p=128)[:, :, 0:2048])
    p.step("gp", lw, inc=16)
    emit_fm_matmul(cx, d_fm0, None, d_fm1, D, D, bias_dram=W["B_bi"], bias_off=0,
                   w_sb=wqk_sb[:, :, 0:1024])
    emit_fm_matmul(cx, d_fm0, None, d_fm2, D, D, w_sb=wqk_sb[:, :, 1024:2048])
    cx.pb.pop()
    cx.pb.push()
    wv_sb = cx.pb.take3(ND, 1024)

    def lwv(E):
        return E.dma_start(out=wv_sb, in_=W["B_wiT"].rearrange("(c p) f -> p c f", p=128)[:, :, 2048:3072])
    p.step("gp", lwv, inc=16)
    xc = cx.pb.take3(ND, TC)
    vo = cx.pb.take(1024)
    for ch in range(NCH):
        def lx(E, ch=ch):
            return E.dma_start(out=xc, in_=d_fm0.rearrange("(c p) t -> p c t", p=128)[:, :, ch * TC:(ch + 1) * TC])
        p.step("sp", lx, inc=16)
        for tt in range(4):
            def te(E, ch=ch, tt=tt):
                ins = None
                for fc in range(2):
                    for d in range(ND):
                        ins = E.matmul(cx.psA[:, fc * 512:fc * 512 + 512],
                                       xc[:, d, tt * 128:(tt + 1) * 128],
                                       wv_sb[:, d, fc * 512:(fc + 1) * 512],
                                       start=(d == 0), stop=(d == ND - 1))
                return ins
            p.step("te", te)

            def ev(E, ch=ch, tt=tt):
                E.tensor_copy(vo[:, 0:512], cx.psA[:, 0:512])
                return E.tensor_copy(vo[:, 512:1024], cx.psA[:, 512:1024])
            p.step("dve", ev)

            def st(E, ch=ch, tt=tt):
                r = ch * 512 + tt * 128
                return E.dma_start(out=d_tm1[r:r + 128, :], in_=vo)
            p.step("sp", st, inc=16)
    cx.pb.pop()

    cx.pb.push(); cx.pf.push()
    hm_sb = cx.pb.take(128)

    def lhm(E):
        return E.dma_start(out=hm_sb, in_=W["K_headmask"][:, :])
    p.step("gp", lhm, inc=16)
    sq = cx.pf.take(512)
    nrm = cx.pf.take(16)
    negc_bc = cx.negc_bc
    xc2 = cx.pb.take3(ND, TC)
    for which, src in ((0, d_fm1), (1, d_fm2)):
        for ch in range(NCH):
            def lx(E, ch=ch, src=src):
                return E.dma_start(out=xc2, in_=src.rearrange("(c p) t -> p c t", p=128)[:, :, ch * TC:(ch + 1) * TC])
            p.step("sp", lx, inc=16)
            for d in range(ND):
                def sc_fix(E, ch=ch, d=d):
                    return E.activation(sq, xc2[:, d, :], AF.Square)
                p.step("act", sc_fix)

                def dv_cast(E, ch=ch, d=d):
                    return E.tensor_copy(cx.sq_bf, sq)
                p.step("dve", dv_cast)

                def te(E, ch=ch, d=d):
                    return E.matmul(cx.psC[0:16, :], hm_sb[:, d * 16:(d + 1) * 16],
                                    cx.sq_bf, start=(d == 0), stop=(d == ND - 1))
                p.step("te", te)

            def dv(E, ch=ch, which=which):
                p.fence(E, E.reduce_max(cx.stat[0:16, 8:9], cx.psC[0:16, :], axis=AX.X))
                if ch == 0:
                    return E.tensor_copy(nrm[0:16, which:which + 1], cx.stat[0:16, 8:9])
                return E.tensor_max(nrm[0:16, which:which + 1], nrm[0:16, which:which + 1],
                                    cx.stat[0:16, 8:9])
            p.step("dve", dv)

    def sc3(E):
        E.activation(nrm[0:16, 2:3], nrm[0:16, 0:1], AF.Sqrt)
        return E.activation(nrm[0:16, 3:4], nrm[0:16, 1:2], AF.Sqrt)
    p.step("act", sc3)

    def dv4(E):
        p.fence(E, E.tensor_mul(nrm[0:16, 4:5], nrm[0:16, 2:3], nrm[0:16, 3:4]))
        p.fence(E, E.tensor_scalar_mul(nrm[0:16, 4:5], nrm[0:16, 4:5], -1.0 / 8.0))
        p.fence(E, E.memset(cx.negc_bf, 0.0))
        return E.tensor_copy(cx.negc_bf[0:16, 0:1], nrm[0:16, 4:5])
    p.step("dve", dv4)

    def te4(E):
        return E.transpose(cx.psD.bitcast(BF16)[0:1, 0:128], cx.negc_bf, cx.ident)
    p.step("te", te4)

    def dv5(E):
        return E.tensor_copy(cx.negcT_bf[0:1, 0:16], cx.psD.bitcast(BF16)[0:1, 0:16])
    p.step("dve", dv5)

    def te5(E):
        return E.matmul(cx.psD[:, 0:16], cx.ones[0:1, :], cx.negcT_bf[0:1, 0:16], start=True, stop=True)
    p.step("te", te5)

    def dv6(E):
        return E.tensor_copy(negc_bc, cx.psD[:, 0:16])
    p.step("dve", dv6)
    cx.pb.pop(); cx.pf.pop()

    cx.pb.push()
    qh = [cx.pb.take(4096) for _ in range(2)]
    kh = [cx.pb.take(4096) for _ in range(2)]
    vaug = [cx.pb.take(NT * 2 * 66).rearrange("p (t h e) -> p t h e", t=NT, h=2) for _ in range(2)]
    es = cx.pb.take(2048)
    ob = cx.pb.take(512)
    bvs = cx.pf.take(8)

    def lbv(E):
        return E.dma_start(out=bvs, in_=W["B_bi"][0:1, 2048:3072].rearrange("o (c p) -> p (o c)", p=128),
                           allow_slow_non_contiguous=True)
    p.step("sp", lbv, inc=16)
    b4_start = p.mark()
    hp_done = [None] * 8
    for hp in range(8):
        hb = hp % 2

        def mset(E, hp=hp, hb=hb):
            return E.memset(vaug[hb], 1.0)
        mset_mark = p.step("dve", mset, wait_at=(b4_start[0] if hp < 2 else hp_done[hp - 2][0]),
                           dml_at=b4_start[1])

        def lqk(E, hp=hp, hb=hb):
            p.dma_in_step(E, qh[hb], d_fm1[hp * 128:(hp + 1) * 128, :])
            p.dma_in_step(E, kh[hb], d_fm2[hp * 128:(hp + 1) * 128, :])
            src = d_tm1.rearrange("(t p) (h e) -> p t h e", p=128, h=16)
            p.dma_in_step(E, vaug[hb][:, :, 0, 0:64], src[:, :, 2 * hp, :])
            return E.dma_start(out=vaug[hb][:, :, 1, 0:64], in_=src[:, :, 2 * hp + 1, :])
        ld_hp = p.step("sp", lqk, inc=16, wait_at=mset_mark[0], dml_at=b4_start[1])
        for h01 in range(2):
            h = 2 * hp + h01
            qb = 64 * h01
            for qi in range(8):
                exp_done = []
                for pr in range(17):
                    if pr < 16:
                        def te(E, h01=h01, qi=qi, pr=pr, qb=qb, hb=hb):
                            ins = None
                            for j in range(2):
                                kt = pr * 2 + j
                                ins = E.matmul(cx.psA[:, (pr % 2) * 1024 + j * 512:(pr % 2) * 1024 + (j + 1) * 512],
                                               kh[hb][qb:qb + 64, kt * 128:(kt + 1) * 128],
                                               qh[hb][qb:qb + 64, qi * 512:(qi + 1) * 512],
                                               start=True, stop=True)
                            return ins
                        p.step("te", te, wait_at=(exp_done[pr - PIPE_DEPTH][0] if pr >= PIPE_DEPTH else None) if PIPE_B4B else None,
                               dml_at=ld_hp[1] if PIPE_B4B else None)

                        def sc(E, h=h, pr=pr):
                            return E.activation(es[:, (pr % 2) * 1024:(pr % 2 + 1) * 1024],
                                                cx.psA[:, (pr % 2) * 1024:(pr % 2 + 1) * 1024],
                                                AF.Exp, bias=negc_bc[:, h:h + 1], scale=0.125)
                        exp_done.append(p.step("act", sc, dml_at=ld_hp[1]))
                    if pr >= 1:
                        pp = pr - 1

                        def te2(E, h01=h01, qi=qi, pp=pp, hb=hb):
                            ins = None
                            for j in range(2):
                                kt = pp * 2 + j
                                ins = E.matmul(cx.psB[0:65, 0:512],
                                               vaug[hb][:, kt, h01, 0:65],
                                               es[:, (pp % 2) * 1024 + j * 512:(pp % 2) * 1024 + (j + 1) * 512],
                                               start=(pp == 0 and j == 0), stop=(pp == 15 and j == 1))
                            return ins
                        p.step("te", te2, wait_at=exp_done[pp][0] if (PIPE_B4B and PIPE_TE2) else None,
                               dml_at=ld_hp[1] if PIPE_B4B else None)
                def dv_b(E):
                    p.fence(E, E.reciprocal(cx.rd32[0:1, :], cx.psB[64:65, 0:512]))
                    return E.tensor_copy(cx.rd_bf[0:1, :], cx.rd32[0:1, :])
                p.step("dve", dv_b)

                def te3(E):
                    return E.matmul(cx.psC[0:64, 0:512], cx.ones[0:1, 0:64], cx.rd_bf[0:1, 0:512],
                                    start=True, stop=True)
                p.step("te", te3)

                def dv_c(E):
                    return E.tensor_copy(cx.rdb32[0:64, :], cx.psC[0:64, 0:512])
                p.step("dve", dv_c)

                def dv_d(E, h01=h01, qb=qb):
                    return E.tensor_mul(cx.o32[0:64, :], cx.psB[0:64, 0:512], cx.rdb32[0:64, :])
                p.step("dve", dv_d)

                def sc2(E, h=h, h01=h01, qb=qb):
                    return E.activation(ob[0:64, :], cx.o32[0:64, :], AF.Identity,
                                        bias=bvs[qb:qb + 64, (h // 2):(h // 2) + 1])
                p.step("act", sc2)

                def st(E, h=h, qi=qi):
                    return E.dma_start(out=d_fm3[h * 64:(h + 1) * 64, qi * 512:(qi + 1) * 512],
                                       in_=ob[0:64, :])
                p.step("gp", st, inc=16)
        hp_done[hp] = p.mark()
    cx.pb.pop()

    cx.pb.push(); cx.pf.push()
    wo_sb = cx.pb.take3(ND, 1024)

    def lwo(E):
        return E.dma_start(out=wo_sb, in_=W["B_woT"].rearrange("(c p) f -> p c f", p=128))
    p.step("gp", lwo, inc=16)
    rowbf = cx.pb.take(1024)
    cb_bo = cx.pf.take(1024)
    cb_g1 = cx.pf.take(1024)
    cb_b1 = cx.pf.take(1024)
    for dst, srcap in ((cb_bo, W["B_bo"][0:1, :]), (cb_g1, W["B_ln"][0:1, :]), (cb_b1, W["B_ln"][1:2, :])):
        def lr(E, srcap=srcap):
            return E.dma_start(out=rowbf[0:1, :], in_=srcap)
        p.step("gp", lr, inc=16)
        for half in range(2):
            def te(E, half=half):
                return E.matmul(cx.psD[:, 0:512], cx.ones[0:1, :], rowbf[0:1, half * 512:(half + 1) * 512],
                                start=True, stop=True)
            p.step("te", te)

            def dv(E, dst=dst, half=half):
                return E.tensor_copy(dst[:, half * 512:(half + 1) * 512], cx.psD[:, 0:512])
            p.step("dve", dv)
    oc = cx.pb.take3(ND, TC)
    xb_t = cx.pb.take(1024)
    y_t = cx.pb.take(1024)
    for ch in range(NCH):
        def lo(E, ch=ch):
            return E.dma_start(out=oc, in_=d_fm3.rearrange("(c p) t -> p c t", p=128)[:, :, ch * TC:(ch + 1) * TC])
        p.step("sp", lo, inc=16)
        for tt in range(4):
            i = ch * 4 + tt

            def te(E, ch=ch, tt=tt):
                ins = None
                for fc in range(2):
                    for d in range(ND):
                        ins = E.matmul(cx.psA[:, fc * 512:fc * 512 + 512],
                                       oc[:, d, tt * 128:(tt + 1) * 128],
                                       wo_sb[:, d, fc * 512:(fc + 1) * 512],
                                       start=(d == 0), stop=(d == ND - 1))
                return ins
            p.step("te", te)

            def lxb(E, i=i):
                return E.dma_start(out=xb_t, in_=d_tm0[i * 128:(i + 1) * 128, :])
            p.step("sp", lxb, inc=16)

            def dv(E, i=i):
                E.tensor_copy(cx.x32a, xb_t)
                E.tensor_add(cx.x32a, cx.x32a, cb_bo)
                E.tensor_add(cx.x32a[:, 0:512], cx.x32a[:, 0:512], cx.psA[:, 0:512])
                E.tensor_add(cx.x32a[:, 512:1024], cx.x32a[:, 512:1024], cx.psA[:, 512:1024])
                return E.reduce_sum(cx.stat[:, 0:1], cx.x32a, axis=AX.X)
            p.step("dve", dv)

            def sc(E, i=i):
                return E.activation(cx.x32b, cx.x32a, AF.Square, accum_out=cx.stat[:, 1:2])
            p.step("act", sc)

            def dv2(E, i=i):
                return ln_var_ops(p, E, cx.stat)
            p.step("dve", dv2)

            def sc2(E, i=i):
                return E.activation(cx.stat[:, 3:4], cx.stat[:, 2:3], AF.Sqrt)
            p.step("act", sc2)

            def dv3(E, i=i):
                p.fence(E, E.reciprocal(cx.stat[:, 3:4], cx.stat[:, 3:4]))
                E.tensor_scalar(cx.x32b, cx.x32a, cx.stat[:, 4:5], cx.stat[:, 3:4],
                                ALU.subtract, ALU.mult)
                E.tensor_mul(cx.x32b, cx.x32b, cb_g1)
                E.tensor_add(cx.x32b, cx.x32b, cb_b1)
                return E.tensor_copy(y_t, cx.x32b)
            p.step("dve", dv3)

            def sty(E, i=i):
                return E.dma_start(out=d_tm2[i * 128:(i + 1) * 128, :], in_=y_t)
            p.step("sp", sty, inc=16)
    cx.pb.pop(); cx.pf.pop()

    emit_transpose(cx, d_tm2, d_fm4, T, D)

    cx.pb.push(); cx.pf.push()
    wge_sb = cx.pb.take3(ND, 12)

    def lwge(E):
        return E.dma_start(out=wge_sb, in_=W["B_wge"].rearrange("(c p) f -> p c f", p=128))
    p.step("gp", lwge, inc=16)
    rowb2 = cx.pb.take(16)

    def lbge(E):
        return E.dma_start(out=rowb2[0:1, 0:12], in_=W["B_bge"][0:1, :])
    p.step("gp", lbge, inc=16)
    cbge = cx.pf.take(12)

    def te(E):
        return E.matmul(cx.psD[:, 0:12], cx.ones[0:1, :], rowb2[0:1, 0:12], start=True, stop=True)
    p.step("te", te)

    def dvge(E):
        return E.tensor_copy(cbge, cx.psD[:, 0:12])
    p.step("dve", dvge)
    yc2 = cx.pb.take3(ND, TC)
    r32 = cx.pf.take(12)
    w32 = cx.pf.take(32)
    cmb_bf = cx.pb.take(8)
    for ch in range(NCH):
        def ly(E, ch=ch):
            return E.dma_start(out=yc2, in_=d_fm4.rearrange("(c p) t -> p c t", p=128)[:, :, ch * TC:(ch + 1) * TC])
        p.step("sp", ly, inc=16)
        for tt in range(4):
            i = ch * 4 + tt

            def te2(E, ch=ch, tt=tt):
                ins = None
                for d in range(ND):
                    ins = E.matmul(cx.psC[:, 0:12], yc2[:, d, tt * 128:(tt + 1) * 128],
                                   wge_sb[:, d, :], start=(d == 0), stop=(d == ND - 1))
                return ins
            p.step("te", te2)

            def dvr(E, i=i):
                p.fence(E, E.tensor_add(r32, cx.psC[:, 0:12], cbge))
                lg = r32[:, 0:4]
                return E.reduce_max(cx.stat[:, 8:9], lg, axis=AX.X, negate=True)
            p.step("dve", dvr)

            def scr_(E, i=i):
                return E.activation(w32[:, 0:4], r32[:, 0:4], AF.Exp,
                                    bias=cx.stat[:, 8:9], accum_out=cx.stat[:, 9:10])
            p.step("act", scr_)

            def dvr2(E, i=i):
                st_ = cx.stat
                E.reciprocal(st_[:, 10:11], st_[:, 9:10])
                p.fence(E, E.tensor_scalar_mul(st_[:, 11:12], st_[:, 8:9], -1.0))
                lg = r32[:, 0:4]
                ohg = w32[:, 4:8]
                p.fence(E, E.tensor_tensor(ohg, lg, st_[:, 11:12].broadcast_to([128, 4]), ALU.is_ge))
                pref = w32[:, 8:12]
                p.fence(E, E.tensor_copy(pref[:, 0:1], ohg[:, 0:1]))
                p.fence(E, E.tensor_tensor(pref[:, 1:2], pref[:, 0:1], ohg[:, 1:2], ALU.max))
                p.fence(E, E.tensor_tensor(pref[:, 2:3], pref[:, 1:2], ohg[:, 2:3], ALU.max))
                inv = w32[:, 12:16]
                p.fence(E, E.tensor_scalar(inv[:, 0:3], pref[:, 0:3], -1.0, 1.0, ALU.mult, ALU.add))
                p.fence(E, E.tensor_mul(ohg[:, 1:4], ohg[:, 1:4], inv[:, 0:3]))
                le = r32[:, 4:12].rearrange("p (g e) -> p g e", e=2)
                p.fence(E, E.reduce_max(w32[:, 16:20], le, axis=AX.X, negate=True))
                les = w32[:, 20:28].rearrange("p (g e) -> p g e", e=2)
                return E.tensor_tensor(les, le, w32[:, 16:20][:, :, None].broadcast_to([128, 4, 2]), ALU.add)
            p.step("dve", dvr2)

            def scr2(E, i=i):
                return E.activation(w32[:, 20:28], w32[:, 20:28], AF.Exp)
            p.step("act", scr2)

            def dvr3(E, i=i):
                st_ = cx.stat
                ee = w32[:, 20:28].rearrange("p (g e) -> p g e", e=2)
                p.fence(E, E.reduce_sum(w32[:, 16:20], ee, axis=AX.X))
                p.fence(E, E.reciprocal(w32[:, 16:20], w32[:, 16:20]))
                p.fence(E, E.tensor_tensor(ee, ee, w32[:, 16:20][:, :, None].broadcast_to([128, 4, 2]), ALU.mult))
                ohg = w32[:, 4:8]
                p.fence(E, E.tensor_tensor(ee, ee, ohg[:, :, None].broadcast_to([128, 4, 2]), ALU.mult))
                sel = w32[:, 28:30]
                p.fence(E, E.tensor_reduce(sel, w32[:, 20:28].rearrange("p (g e) -> p e g", e=2), AX.X, ALU.add))
                p.fence(E, E.reduce_max(st_[:, 12:13], sel, axis=AX.X))
                ohe = w32[:, 30:32]
                p.fence(E, E.tensor_tensor(ohe, sel, st_[:, 12:13].broadcast_to([128, 2]), ALU.is_ge))
                p.fence(E, E.tensor_scalar(st_[:, 13:14], ohe[:, 0:1], -1.0, 1.0, ALU.mult, ALU.add))
                p.fence(E, E.tensor_mul(ohe[:, 1:2], ohe[:, 1:2], st_[:, 13:14]))
                p.fence(E, E.tensor_mul(st_[:, 14:15], st_[:, 10:11], st_[:, 12:13]))
                cmb = w32[:, 16:24].rearrange("p (g e) -> p g e", e=2)
                p.fence(E, E.tensor_tensor(cmb, w32[:, 4:8][:, :, None].broadcast_to([128, 4, 2]),
                                ohe[:, None, :].broadcast_to([128, 4, 2]), ALU.mult))
                p.fence(E, E.tensor_single_scalar(w32[:, 16:24], w32[:, 16:24], st_[:, 14:15], ALU.mult))
                return E.tensor_copy(cmb_bf, w32[:, 16:24])
            p.step("dve", dvr3)

            def stc(E, i=i):
                return E.dma_start(out=d_c[i * 128:(i + 1) * 128, 0:8], in_=w32[:, 16:24])
            p.step("sp", stc, inc=16)

            def te3(E, i=i):
                return E.transpose(cx.psD.bitcast(BF16)[0:8, 0:128], cmb_bf, cx.ident)
            p.step("te", te3)

            def dvr4(E, i=i):
                return E.tensor_copy(cx.ctrow_bf[0:8, 0:128], cx.psD.bitcast(BF16)[0:8, 0:128])
            p.step("dve", dvr4)

            def stct(E, i=i):
                return E.dma_start(out=d_cT[0:8, i * 128:(i + 1) * 128], in_=cx.ctrow_bf[0:8, 0:128])
            p.step("sp", stct, inc=16)
    cx.pb.pop(); cx.pf.pop()

    cx.pb.push(); cx.pf.push()
    wa = cx.pb.take3(ND, 2048)
    wb = cx.pb.take3(16, 1024)
    macc = cx.pb.take(16 * 1024).rearrange("p (s f) -> p s f", s=16)
    hsb = cx.pb.take3(16, 512)
    xc3 = cx.pb.take3(ND, TC)
    ctf = cx.pb.take(4096)
    b1s = cx.pf.take(16)
    csb = cx.pf.take(256).rearrange("p (s e) -> p s e", s=16)
    b2s = cx.pb.take(1024)

    def lct(E):
        return E.dma_start(out=ctf[0:8, :], in_=d_cT[0:8, :])
    p.step("sp", lct, inc=16)

    def lb2(E):
        return E.dma_start(out=b2s[0:8, :], in_=W["B_b2"][:, :])
    p.step("gp", lb2, inc=16)
    xc3b = [xc3, cx.pb.take3(ND, TC)]
    zbanks = [(cx.psB[:, 0:512], cx.psB[:, 512:1024]), (cx.psC, cx.psD)]
    for thalf in range(2):
        def lc(E, thalf=thalf):
            return E.dma_start(out=csb,
                               in_=W_dc_rearr(d_c, thalf))
        p.step("sp", lc, inc=16)
        for e in range(8):
            def lw1(E, e=e):
                p.dma_in_step(E, wa, W["B_w1"][e].rearrange("(c p) f -> p c f", p=128), eng="gp")
                p.dma_in_step(E, wb, W["B_w2"][e].rearrange("(c p) f -> p c f", p=128), eng="gp")
                return E.dma_start(out=b1s, in_=W["B_b1"][e:e + 1, :].rearrange("o (c p) -> p (o c)", p=128),
                                   allow_slow_non_contiguous=True)
            p.step("gp", lw1, inc=16)
            ev_done = []
            teh_done = [None] * 4
            ld_done = [None] * 4
            dz_done = []
            pass_start = p.mark()
            for ch in range(4):
                cb = ch % 2

                def lx3(E, thalf=thalf, ch=ch, cb=cb):
                    cc = thalf * 4 + ch
                    return E.dma_start(out=xc3b[cb], in_=d_fm4.rearrange("(c p) t -> p c t", p=128)[:, :, cc * TC:(cc + 1) * TC])
                wa_ = (pass_start[0] if ch < 2 else teh_done[ch - 2][0]) if PIPE_B8 else None
                ld_done[ch] = p.step("sp", lx3, inc=16, wait_at=wa_)
                for f in range(16):
                    fi = ch * 16 + f

                    def te(E, f=f, cb=cb):
                        ins = None
                        for d in range(ND):
                            ins = E.matmul(cx.psA[:, (f % 4) * 512:(f % 4) * 512 + 512],
                                           wa[:, d, f * 128:(f + 1) * 128], xc3b[cb][:, d, :],
                                           start=(d == 0), stop=(d == ND - 1))
                        return ins
                    _d8 = 4 if PIPE_DEPTH == 2 else 1
                    twa = (pass_start[0] if fi < _d8 else ev_done[fi - _d8][0]) if PIPE_B8 else None
                    teh_done[ch] = p.step("te", te, wait_at=twa, dml_at=ld_done[ch][1] if PIPE_B8 else None)

                    def ev(E, f=f):
                        return E.activation(hsb[:, f, :], cx.psA[:, (f % 4) * 512:(f % 4) * 512 + 512],
                                            AF.Relu, bias=b1s[:, f:f + 1])
                    ev_done.append(p.step("act", ev, dml_at=ld_done[ch][1]))
                for tt in range(4):
                    ts_local = ch * 4 + tt
                    tglob = thalf * 2048 + ts_local * 128
                    zi = ch * 4 + tt
                    zb = zbanks[zi % 2]

                    def te2(E, e=e, tt=tt, tglob=tglob, zb=zb):
                        ins = None
                        for dc in range(2):
                            first = True
                            if e == 0:
                                ins = E.matmul(zb[dc],
                                               ctf[0:8, tglob:tglob + 128],
                                               b2s[0:8, dc * 512:(dc + 1) * 512],
                                               start=True, stop=False)
                                first = False
                            for f in range(16):
                                ins = E.matmul(zb[dc],
                                               hsb[:, f, tt * 128:(tt + 1) * 128],
                                               wb[:, f, dc * 512:(dc + 1) * 512],
                                               start=first, stop=(f == 15))
                                first = False
                        return ins
                    _z8 = 2 if PIPE_DEPTH == 2 else 1
                    zwa = (ev_done[-1][0] if zi < _z8 else max(ev_done[-1][0], dz_done[zi - _z8][0])) if PIPE_B8 else None
                    p.step("te", te2, wait_at=zwa, dml_at=ld_done[ch][1] if PIPE_B8 else None)

                    def dv(E, e=e, ts_local=ts_local, zb=zb):
                        cs = csb[:, ts_local, e:e + 1]
                        m0 = macc[:, ts_local, 0:512]
                        m1 = macc[:, ts_local, 512:1024]
                        if e == 0:
                            E.tensor_single_scalar(m0, zb[0], cs, ALU.mult)
                            return E.tensor_single_scalar(m1, zb[1], cs, ALU.mult)
                        E.scalar_tensor_tensor(m0, zb[0], cs, m0, ALU.mult, ALU.add)
                        return E.scalar_tensor_tensor(m1, zb[1], cs, m1, ALU.mult, ALU.add)
                    dz_done.append(p.step("dve", dv, dml_at=ld_done[ch][1]))
        cb_g2 = cx.pf.take(1024) if thalf == 0 else cx.cb_g2
        cb_b2 = cx.pf.take(1024) if thalf == 0 else cx.cb_b2
        if thalf == 0:
            cx.cb_g2, cx.cb_b2 = cb_g2, cb_b2
            rowb3 = cx.pb.take(1024)
            for dst, srcap in ((cb_g2, W["B_ln"][2:3, :]), (cb_b2, W["B_ln"][3:4, :])):
                def lr(E, srcap=srcap):
                    return E.dma_start(out=rowb3[0:1, :], in_=srcap)
                p.step("gp", lr, inc=16)
                for half in range(2):
                    def te_(E, half=half):
                        return E.matmul(cx.psD[:, 0:512], cx.ones[0:1, :],
                                        rowb3[0:1, half * 512:(half + 1) * 512], start=True, stop=True)
                    p.step("te", te_)

                    def dv_(E, dst=dst, half=half):
                        return E.tensor_copy(dst[:, half * 512:(half + 1) * 512], cx.psD[:, 0:512])
                    p.step("dve", dv_)
        yt2 = cx.pb.take(1024) if thalf == 0 else cx.yt2
        if thalf == 0:
            cx.yt2 = yt2
        for s in range(16):
            i = thalf * 16 + s

            def ly2(E, i=i):
                return E.dma_start(out=yt2, in_=d_tm2[i * 128:(i + 1) * 128, :])
            p.step("sp", ly2, inc=16)

            def lx1(E, i=i):
                return E.dma_start(out=cx.x32b, in_=xin[i * 128:(i + 1) * 128, :])
            p.step("sp", lx1, inc=16)

            def dvz(E, i=i, s=s):
                E.tensor_copy(cx.x32a, yt2)
                E.tensor_add(cx.x32a, cx.x32a, macc[:, s, :])
                return E.reduce_sum(cx.stat[:, 0:1], cx.x32a, axis=AX.X)
            p.step("dve", dvz)

            def scz(E, i=i):
                return E.activation(cx.x32c, cx.x32a, AF.Square, accum_out=cx.stat[:, 1:2])
            p.step("act", scz)

            def dvz2(E, i=i):
                return ln_var_ops(p, E, cx.stat)
            p.step("dve", dvz2)

            def scz2(E, i=i):
                return E.activation(cx.stat[:, 3:4], cx.stat[:, 2:3], AF.Sqrt)
            p.step("act", scz2)

            def dvz3(E, i=i):
                st_ = cx.stat
                p.fence(E, E.reciprocal(st_[:, 3:4], st_[:, 3:4]))
                E.tensor_scalar(cx.x32c, cx.x32a, st_[:, 4:5], st_[:, 3:4], ALU.subtract, ALU.mult)
                E.tensor_mul(cx.x32c, cx.x32c, cb_g2)
                E.tensor_add(cx.x32c, cx.x32c, cb_b2)
                return E.reduce_sum(st_[:, 0:1], cx.x32c, axis=AX.X)
            p.step("dve", dvz3)

            def scz3(E, i=i):
                return E.activation(cx.x32a, cx.x32c, AF.Square, accum_out=cx.stat[:, 1:2])
            p.step("act", scz3)

            def dvz4(E, i=i):
                return ln_var_ops(p, E, cx.stat)
            p.step("dve", dvz4)

            def scz4(E, i=i):
                return E.activation(cx.stat[:, 3:4], cx.stat[:, 2:3], AF.Sqrt)
            p.step("act", scz4)

            def dvz5(E, i=i):
                st_ = cx.stat
                p.fence(E, E.reciprocal(st_[:, 3:4], st_[:, 3:4]))
                E.tensor_scalar(cx.x32a, cx.x32c, st_[:, 4:5], st_[:, 3:4], ALU.subtract, ALU.mult)
                return E.tensor_add(cx.x32b, cx.x32b, cx.x32a)
            p.step("dve", dvz5)

            def stx(E, i=i):
                return E.dma_start(out=xout[i * 128:(i + 1) * 128, :], in_=cx.x32b)
            p.step("sp", stx, inc=16)
    cx.pb.pop(); cx.pf.pop()


def W_dc_rearr(d_c, thalf):
    return d_c[thalf * 2048:(thalf + 1) * 2048, 0:16].rearrange("(s p) e -> p s e", p=128)



STAGES = ("a", "b", "c", "d")
DEBUG_TAPS = []
DEBUG_A8 = False
PIPE_B4B = True
PIPE_B8 = True
PIPE_DEPTH = 1
PIPE_TE2 = False


def build_program(stages):
    nc = bass.Bass("TRN2", target_bir_lowering=False)
    W = {}
    xin = nc.dram_tensor("xin", [T, D], F32, kind="ExternalInput").ap()
    xout = nc.dram_tensor("xout", [T, D], F32, kind="ExternalOutput").ap()
    dummy = np.zeros(1)
    HKEYS = {
        "A_wqkv": [1024, 3072], "A_wo": [1024, 1024], "A_wsmall": [1024, 12],
        "A_convs": [3, 1024, 4], "A_convmix": [4, 4], "A_mixbias": [1, 4], "A_rmsw": [1, 256],
        "B_wiT": [1024, 3072], "B_bi": [1, 3072], "B_woT": [1024, 1024], "B_bo": [1, 1024],
        "B_ln": [4, 1024], "B_wge": [1024, 12], "B_bge": [1, 12],
        "B_w1": [8, 1024, 2048], "B_b1": [8, 2048], "B_w2": [8, 2048, 1024], "B_b2": [8, 1024],
        "C_qw": [1024, 1024], "C_kvw": [1024, 512], "C_ow": [1024, 1024], "C_dw": [1024, 14],
        "C_pw": [1024, 2048], "C_gw": [1024, 1024], "C_alpha": [1, 1],
        "D_g12": [2, 1024], "D_wq": [1024, 1024], "D_bq": [1, 1024], "D_wdown": [1024, 64],
        "D_bdown": [1, 64], "D_wupk": [64, 1024], "D_wupv": [64, 1024],
        "D_bupk": [1, 1024], "D_bupv": [1, 1024], "D_wo": [1024, 1024],
        "D_bo": [1, 1024], "D_wg": [1024, 8], "D_bg": [1, 8], "D_we": [8, 1024, 1024],
        "K_ident": [128, 128], "K_ones": [128, 128], "K_headmask": [128, 128],
        "K_m_sl_bd": [128, 128], "K_m_su_bd": [128, 128], "K_m_tu_bd": [128, 128],
        "K_dft_l": [128, 128], "K_idft_l": [128, 128], "K_mlow_neg": [256, 256],
        "K_m_d_neg": [128, 128], "K_rep4": [4, 16],
    }
    for k, shp in HKEYS.items():
        W[k] = nc.dram_tensor(k, shp, F32, kind="ExternalInput").ap()
    scr = {}
    for nm in ("tm0", "tm1", "tm2", "tm3"):
        scr[nm] = nc.dram_tensor("s_" + nm, [T, D], BF16, kind="Internal").ap()
    for nm in ("fm0", "fm1", "fm2", "fm3", "fm4"):
        scr[nm] = nc.dram_tensor("s_" + nm, [D, T], BF16, kind="Internal").ap()
    scr["c32"] = nc.dram_tensor("s_c32", [T, 16], F32, kind="Internal").ap()
    scr["vkb"] = nc.dram_tensor("s_vkb", [T, 2048], BF16, kind="Internal").ap()
    scr["g12"] = nc.dram_tensor("s_g12", [16, T], F32, kind="Internal").ap()
    scr["cT"] = nc.dram_tensor("s_cT", [16, T], BF16, kind="Internal").ap()
    scr["xmid"] = [nc.dram_tensor(f"s_xm{i}", [T, D], F32, kind="Internal").ap() for i in range(2)]

    p = Prog(nc)
    cx = Ctx(p, W)
    cx.sq_bf = cx.pb.take(512)
    cx.negc_bf = cx.pb.take(1)
    cx.negcT_bf = cx.pb.take(16)
    cx.rd_bf = cx.pb.take(512)
    cx.ctrow_bf = cx.pb.take(128)
    cx.negc_bc = cx.pf.take(16)
    cx.rowbf = cx.pb.take(1024)
    cx.rd32 = cx.pf.take(512)
    cx.rdb32 = cx.pf.take(512)
    cx.o32 = cx.pf.take(512)
    cx.x32a = cx.pf.take(1024)
    cx.x32b = cx.pf.take(1024)
    cx.x32c = cx.pf.take(1024)
    cx.epsc = cx.pf.take(4)
    p.start_block()
    cx.load_consts()

    def _eps(E):
        E.memset(cx.epsc[:, 0:1], 1e-6)
        return E.memset(cx.epsc[:, 1:2], 1e-5)
    p.step("dve", _eps)

    bufs = [xin]
    for i in range(len(stages) - 1):
        bufs.append(scr["xmid"][i % 2])
    bufs.append(xout)
    for si, s in enumerate(stages):
        args = (cx, bufs[si], bufs[si + 1], scr)
        if s == "b":
            emit_stage_b(*args)
        elif s == "a":
            emit_stage_a(*args)
        elif s == "c":
            emit_stage_c(*args)
        elif s == "d":
            emit_stage_d(*args)
        else:
            raise ValueError(s)
    for key in DEBUG_TAPS:
        src = scr[key]
        dbg = nc.dram_tensor("dbg_" + key, list(src.shape), src.dtype, kind="ExternalOutput").ap()

        def cp(E, src=src, dbg=dbg):
            return E.dma_start(out=dbg, in_=src)
        p.step("sp", cp, inc=16)

    final_cnt = p.cnt["flow"]
    final_d = p.dsnap()

    def fin(E):
        E.wait_ge(p.sems["flow"], final_cnt)
        for c, tgt in final_d.items():
            if tgt > 0:
                E.wait_ge(p.sems[c], tgt)
    p.block.vector(fin)
    p.close()
    return nc


def emit_stage_a(cx, xin, xout, scr):
    p = cx.p
    W = cx.W
    d_tm0 = scr["tm0"]
    d_q, d_k, d_v, d_kb = scr["fm1"], scr["fm2"], scr["fm3"], scr["fm4"]
    d_xa = scr["fm0"]
    d_vtm, d_ktm = scr["tm1"], scr["tm3"]
    d_vkb = scr["vkb"]
    d_g = scr["g12"]
    d_gtm = scr["c32"]
    d_ofm = d_xa

    emit_ln_tm(cx, xin, d_tm0)
    emit_transpose(cx, d_tm0, d_xa, T, D)
    for i, dst in ((0, d_q), (1, d_k), (2, d_v)):
        emit_fm_matmul(cx, d_xa, W["A_wqkv"][:, i * 1024:(i + 1) * 1024], dst, D, D)
    cx.pb.push()
    ws_sb = cx.pb.take3(ND, 12)

    def lws(E):
        return E.dma_start(out=ws_sb, in_=W["A_wsmall"].rearrange("(c p) f -> p c f", p=128))
    p.step("gp", lws, inc=16)
    xcs = cx.pb.take3(ND, TC)
    gout = cx.pf.take(512)
    for ch in range(NCH):
        def lx(E, ch=ch):
            return E.dma_start(out=xcs, in_=d_xa.rearrange("(c p) t -> p c t", p=128)[:, :, ch * TC:(ch + 1) * TC])
        p.step("sp", lx, inc=16)

        def te(E, ch=ch):
            ins = None
            for d in range(ND):
                ins = E.matmul(cx.psC[0:12, 0:512], ws_sb[:, d, :], xcs[:, d, :],
                               start=(d == 0), stop=(d == ND - 1))
            return ins
        p.step("te", te)

        def dv(E, ch=ch):
            return E.tensor_copy(gout[0:12, :], cx.psC[0:12, 0:512])
        p.step("dve", dv)

        def stg(E, ch=ch):
            return E.dma_start(out=d_g[0:12, ch * TC:(ch + 1) * TC], in_=gout[0:12, :])
        p.step("sp", stg, inc=16)
    cx.pb.pop()
    cx.pb.push(); cx.pf.push()
    convw = cx.pf.take(3 * 8 * 4).rearrange("p (j c k) -> p j c k", j=3, c=8)

    def lcw(E):
        return E.dma_start(out=convw,
                           in_=W["A_convs"].rearrange("j (c p) k -> p j c k", p=128),
                           allow_slow_non_contiguous=True)
    p.step("gp", lcw, inc=16)
    pad = cx.pb.take(3 + T + 1)
    acc32 = cx.pb.take(2 * T).bitcast(F32)
    t32 = cx.pb.take(2 * T).bitcast(F32)
    obf = cx.pb.take(T)
    for j, dst in ((0, d_q), (1, d_k), (2, d_v)):
        for c in range(ND):
            def ld(E, j=j, c=c, dst=dst):
                return E.dma_start(out=pad[:, 3:3 + T], in_=dst[c * 128:(c + 1) * 128, :])
            p.step("sp", ld, inc=16)

            def dvc(E, j=j, c=c):
                p.fence(E, E.memset(pad[:, 0:3], 0.0))
                E.tensor_single_scalar(acc32, pad[:, 0:T], convw[:, j, c, 0:1], ALU.mult)
                for k in range(1, 4):
                    E.scalar_tensor_tensor(acc32, pad[:, k:k + T], convw[:, j, c, k:k + 1],
                                           acc32, ALU.mult, ALU.add)
                return E.engine_nop()
            p.step("dve", dvc)

            def scc(E, j=j, c=c):
                if j < 2:
                    E.activation(t32, acc32, AF.Silu)
                    return E.activation(obf, t32, AF.Silu)
                return E.activation(obf, acc32, AF.Silu)
            p.step("act", scc)

            def stc(E, j=j, c=c, dst=dst):
                return E.dma_start(out=dst[c * 128:(c + 1) * 128, :], in_=obf)
            p.step("sp", stc, inc=16)
    cx.pb.pop(); cx.pf.pop()
    cx.pb.push(); cx.pf.push()
    w12 = cx.pf.take(4)
    mixb = cx.pf.take(1)
    gpad = cx.pb.take(2 * (4 + T)).bitcast(F32)
    gw32 = cx.pb.take(2 * T).bitcast(F32)

    def dvg0(E):
        p.fence(E, E.memset(w12[:, :], 0.0))
        return E.memset(w12[:, 3:4], 1.0)
    p.step("dve", dvg0)

    def lg(E):
        p.dma_in_step(E, gpad[0:4, 3:3 + T], d_g[8:12, :])
        p.dma_in_step(E, gpad[4:12, 3:3 + T], d_g[0:8, :])
        p.dma_in_step(E, mixb[0:4, 0:1], W["A_mixbias"][0:1, :].rearrange("o p -> p o"),
                      allow_slow_non_contiguous=True)
        return E.dma_start(out=w12[0:4, :], in_=W["A_convmix"][:, :])
    p.step("sp", lg, inc=16)

    def dvg(E):
        return E.memset(gpad[0:12, 0:3], 0.0)
    p.step("dve", dvg)

    def dvg2(E):
        E.tensor_single_scalar(gw32[0:12, :], gpad[0:12, 0:T], w12[0:12, 0:1], ALU.mult)
        for k in range(1, 4):
            E.scalar_tensor_tensor(gw32[0:12, :], gpad[0:12, k:k + T], w12[0:12, k:k + 1],
                                   gw32[0:12, :], ALU.mult, ALU.add)
        return E.engine_nop()
    p.step("dve", dvg2)

    def scg(E):
        return E.activation(gw32[0:4, :], gw32[0:4, :], AF.Silu)
    p.step("act", scg)

    def dvg3(E):
        return E.tensor_single_scalar(gw32[0:4, :], gw32[0:4, :], mixb[0:4, 0:1], ALU.add)
    p.step("dve", dvg3)

    def scg2(E):
        return E.activation(gw32[0:12, :], gw32[0:12, :], AF.Sigmoid)
    p.step("act", scg2)

    def stg2(E):
        return E.dma_start(out=d_g[0:12, :], in_=gw32[0:12, :])
    p.step("sp", stg2, inc=16)
    g12bf = cx.pb.take(T)
    gtt = cx.pf.take(16)

    def dvg4(E):
        return E.tensor_copy(g12bf[0:12, :], gw32[0:12, :])
    p.step("dve", dvg4)
    for i in range(NT):
        def teg(E, i=i):
            return E.transpose(cx.psD.bitcast(BF16)[0:128, 0:12],
                               g12bf[0:12, i * 128:(i + 1) * 128], cx.ident[0:12, 0:12])
        p.step("te", teg)

        def dvg5(E, i=i):
            return E.tensor_copy(gtt[:, 0:12], cx.psD.bitcast(BF16)[0:128, 0:12])
        p.step("dve", dvg5)

        def stg3(E, i=i):
            return E.dma_start(out=d_gtm[i * 128:(i + 1) * 128, 0:12], in_=gtt[:, 0:12])
        p.step("sp", stg3, inc=16)
    cx.pb.pop(); cx.pf.pop()

    cx.pb.push(); cx.pf.push()
    qkt = cx.pb.take3(2, TC)
    sqbf = cx.pb.take(TC)
    rst32 = cx.pf.take(TC)
    brow = cx.pb.take(TC)
    bbc32 = cx.pf.take(TC)
    for src_d, dst_kb in ((d_q, None), (d_k, d_kb)):
        for h in range(4):
            for ch in range(NCH):
                def ld(E, src_d=src_d, h=h, ch=ch):
                    return E.dma_start(out=qkt, in_=src_d.rearrange("(c p) t -> p c t", p=128)
                                       [:, 2 * h:2 * h + 2, ch * TC:(ch + 1) * TC])
                p.step("sp", ld, inc=16)
                for c2 in range(2):
                    def sq(E, c2=c2):
                        return E.activation(sqbf, qkt[:, c2, :], AF.Square)
                    p.step("act", sq)

                    def tes(E, c2=c2):
                        return E.matmul(cx.psC[:, 0:512], cx.ones, sqbf,
                                        start=(c2 == 0), stop=(c2 == 1))
                    p.step("te", tes)

                def scr_(E, h=h, ch=ch):
                    return E.activation(rst32, cx.psC[:, 0:512], AF.Sqrt, bias=cx.epsc[:, 0:1])
                p.step("act", scr_)

                def dvn(E, h=h, ch=ch):
                    E.reciprocal(rst32, rst32)
                    E.tensor_mul(qkt[:, 0, :], qkt[:, 0, :], rst32)
                    return E.tensor_mul(qkt[:, 1, :], qkt[:, 1, :], rst32)
                p.step("dve", dvn)

                def stn(E, src_d=src_d, h=h, ch=ch):
                    return E.dma_start(out=src_d.rearrange("(c p) t -> p c t", p=128)
                                       [:, 2 * h:2 * h + 2, ch * TC:(ch + 1) * TC], in_=qkt)
                p.step("sp", stn, inc=16)
                if dst_kb is not None:
                    def ldb(E, h=h, ch=ch):
                        return E.dma_start(out=brow[0:1, :], in_=d_g[4 + h:5 + h, ch * TC:(ch + 1) * TC])
                    p.step("gp", ldb, inc=16)

                    def teb(E, h=h, ch=ch):
                        return E.matmul(cx.psD[:, 0:512], cx.ones[0:1, :], brow[0:1, :],
                                        start=True, stop=True)
                    p.step("te", teb)

                    def dvb(E, h=h, ch=ch):
                        E.tensor_copy(bbc32, cx.psD[:, 0:512])
                        E.tensor_mul(qkt[:, 0, :], qkt[:, 0, :], bbc32)
                        return E.tensor_mul(qkt[:, 1, :], qkt[:, 1, :], bbc32)
                    p.step("dve", dvb)

                    def stkb(E, h=h, ch=ch):
                        return E.dma_start(out=d_kb.rearrange("(c p) t -> p c t", p=128)
                                           [:, 2 * h:2 * h + 2, ch * TC:(ch + 1) * TC], in_=qkt)
                    p.step("sp", stkb, inc=16)
    cx.pb.pop(); cx.pf.pop()
    emit_transpose(cx, d_k, d_ktm, D, T)
    emit_transpose(cx, d_v, d_vtm, D, T)
    cx.pb.push(); cx.pf.push()
    vk_sb = cx.pb.take(2048)
    vkb_sb = cx.pb.take(2048)
    bt32 = cx.pf.take(16)
    for i in range(NT):
        def ldv(E, i=i):
            p.dma_in_step(E, vk_sb[:, 0:1024], d_vtm[i * 128:(i + 1) * 128, :])
            p.dma_in_step(E, vk_sb[:, 1024:2048], d_ktm[i * 128:(i + 1) * 128, :])
            return E.dma_start(out=bt32, in_=d_gtm[i * 128:(i + 1) * 128, :])
        p.step("sp", ldv, inc=16)

        def dvv(E, i=i):
            ins = None
            for h in range(4):
                E.tensor_single_scalar(vkb_sb[:, h * 512:h * 512 + 256],
                                       vk_sb[:, h * 256:(h + 1) * 256],
                                       bt32[:, 4 + h:5 + h], ALU.mult)
                ins = E.tensor_single_scalar(vkb_sb[:, h * 512 + 256:(h + 1) * 512],
                                             vk_sb[:, 1024 + h * 256:1024 + (h + 1) * 256],
                                             bt32[:, 4 + h:5 + h], ALU.mult)
            return ins
        p.step("dve", dvv)

        def stv(E, i=i):
            return E.dma_start(out=d_vkb[i * 128:(i + 1) * 128, :], in_=vkb_sb)
        p.step("sp", stv, inc=16)
    cx.pb.pop(); cx.pf.pop()

    cx.pb.push(); cx.pf.push()
    msl_sb = cx.pb.take(128)
    msu_sb = cx.pb.take(128)
    mtu_sb = cx.pb.take(128)

    def lm(E):
        p.dma_in_step(E, msl_sb, W["K_m_sl_bd"][:, :], eng="gp")
        p.dma_in_step(E, msu_sb, W["K_m_su_bd"][:, :], eng="gp")
        return E.dma_start(out=mtu_sb, in_=W["K_m_tu_bd"][:, :])
    p.step("gp", lm, inc=16)
    q_sb = cx.pb.take(4 * 256).rearrange("p (h c t) -> p h c t", h=4, c=2)
    k_sb = cx.pb.take(4 * 256).rearrange("p (h c t) -> p h c t", h=4, c=2)
    kb_sb = cx.pb.take(4 * 256).rearrange("p (h c t) -> p h c t", h=4, c=2)
    vkbg = cx.pb.take(4 * 512).rearrange("p (h f) -> p h f", h=4)
    ktm4 = cx.pb.take(4 * 4 * 256).rearrange("p (h c f) -> p h c f", h=4, c=4)
    a_sb = cx.pb.take(4 * 128).rearrange("p (h f) -> p h f", h=4)
    c_sb = cx.pb.take(4 * 5 * 128).rearrange("p (h l f) -> p h l f", h=4, l=5)
    uw_sb = cx.pb.take(4 * 512).rearrange("p (h f) -> p h f", h=4)
    wfm_sb = cx.pb.take(4 * 2 * 128).rearrange("p (h c f) -> p h c f", h=4, c=2)
    s_sb = cx.pb.take(4 * 512).rearrange("p (h f) -> p h f", h=4)
    at_sb = cx.pb.take(4 * 32).rearrange("p (h f) -> p h f", h=4)
    ua_sb = cx.pb.take(4 * 256).rearrange("p (h f) -> p h f", h=4)
    o_sb = cx.pb.take(4 * 2 * 128).rearrange("p (h c f) -> p h c f", h=4, c=2)

    def z0(E):
        return E.memset(s_sb[:, :, :], 0.0)
    p.step("dve", z0)
    NG = 32
    for g in range(NG):
        def lg2(E, g=g):
            for dd, dst in ((d_q, q_sb), (d_k, k_sb), (d_kb, kb_sb)):
                p.dma_in_step(E, dst, dd.rearrange("(c p) t -> p c t", p=128)
                              .rearrange("p (h c) t -> p h c t", h=4)[:, :, :, g * 128:(g + 1) * 128])
            p.dma_in_step(E, vkbg, d_vkb.rearrange("(g p) f -> p g f", p=128)
                          .rearrange("p g (h f) -> p g h f", h=4)[:, g, :, :])
            return E.dma_start(out=ktm4[0:32, :, :, :],
                               in_=d_ktm.rearrange("(g c j) (h f) -> j g h c f", c=4, j=32, h=4)[:, g, :, :, :])
        p.step("sp", lg2, inc=16)
        for h in range(4):
            def teg2(E, h=h):
                for c2 in range(2):
                    E.matmul(cx.psB[:, 0:128], kb_sb[:, h, c2, :], k_sb[:, h, c2, :],
                             start=(c2 == 0), stop=(c2 == 1))
                for c2 in range(2):
                    ins = E.matmul(cx.psB[:, 128:256], k_sb[:, h, c2, :], kb_sb[:, h, c2, :],
                                   start=(c2 == 0), stop=(c2 == 1))
                return ins
            p.step("te", teg2)

            def dvm(E, h=h):
                E.scalar_tensor_tensor(a_sb[:, h, :], cx.psB[:, 0:128], -1.0, msl_sb,
                                       ALU.mult, ALU.mult)
                return E.scalar_tensor_tensor(c_sb[:, h, 0, :], cx.psB[:, 128:256], -1.0, msu_sb,
                                              ALU.mult, ALU.mult)
            p.step("dve", dvm)
            for lv in range(4):
                def tsq(E, h=h, lv=lv):
                    E.matmul(cx.psB[:, 512:640], c_sb[:, h, lv, :], a_sb[:, h, :],
                             start=True, stop=True)
                    return E.matmul(cx.psB[:, 640:768], a_sb[:, h, :], c_sb[:, h, lv, :],
                                    start=True, stop=True)
                p.step("te", tsq)

                def dsq(E, h=h, lv=lv):
                    E.tensor_copy(a_sb[:, h, :], cx.psB[:, 512:640])
                    return E.tensor_copy(c_sb[:, h, lv + 1, :], cx.psB[:, 640:768])
                p.step("dve", dsq)
            def cp0(E, h=h):
                return E.tensor_copy(uw_sb[:, h, :], vkbg[:, h, :])
            p.step("dve", cp0)
            for lv in range(5):
                def tap(E, h=h, lv=lv):
                    E.matmul(cx.psC[:, 0:512], cx.ident, uw_sb[:, h, :], start=True, stop=False)
                    return E.matmul(cx.psC[:, 0:512], c_sb[:, h, lv, :], uw_sb[:, h, :],
                                    start=False, stop=True)
                p.step("te", tap)

                def dap(E, h=h, lv=lv):
                    return E.tensor_copy(uw_sb[:, h, :], cx.psC[:, 0:512])
                p.step("dve", dap)
            def twt(E, h=h):
                E.transpose(cx.psB.bitcast(BF16)[:, 512:640], uw_sb[:, h, 256:384], cx.ident)
                return E.transpose(cx.psB.bitcast(BF16)[:, 640:768], uw_sb[:, h, 384:512], cx.ident)
            p.step("te", twt)

            def dwt(E, h=h):
                E.tensor_scalar_mul(wfm_sb[:, h, 0, :], cx.psB.bitcast(BF16)[:, 512:640], -1.0)
                return E.tensor_scalar_mul(wfm_sb[:, h, 1, :], cx.psB.bitcast(BF16)[:, 640:768], -1.0)
            p.step("dve", dwt)
        for cc in range(4):
            def tat(E, cc=cc):
                ins = None
                for h in range(4):
                    for c2 in range(2):
                        ins = E.matmul(cx.psD[0:32, h * 32:(h + 1) * 32],
                                       k_sb[:, h, c2, cc * 32:(cc + 1) * 32],
                                       q_sb[:, h, c2, cc * 32:(cc + 1) * 32],
                                       start=(c2 == 0), stop=(c2 == 1))
                return ins
            p.step("te", tat)

            def dat(E, cc=cc):
                ins = None
                for h in range(4):
                    ins = E.tensor_mul(at_sb[0:32, h, :], cx.psD[0:32, h * 32:(h + 1) * 32],
                                       mtu_sb[cc * 32:(cc + 1) * 32, cc * 32:(cc + 1) * 32])
                return ins
            p.step("dve", dat)

            def tua(E, cc=cc):
                ins = None
                for h in range(4):
                    E.matmul(cx.psB[0:32, h * 256:(h + 1) * 256], cx.ident[:, cc * 32:(cc + 1) * 32],
                             uw_sb[:, h, 0:256], start=True, stop=False)
                    for c2 in range(2):
                        ins = E.matmul(cx.psB[0:32, h * 256:(h + 1) * 256],
                                       wfm_sb[:, h, c2, cc * 32:(cc + 1) * 32],
                                       s_sb[:, h, c2 * 256:(c2 + 1) * 256],
                                       start=False, stop=(c2 == 1))
                return ins
            p.step("te", tua)

            def dua(E, cc=cc):
                ins = None
                for h in range(4):
                    ins = E.tensor_copy(ua_sb[0:32, h, :], cx.psB[0:32, h * 256:(h + 1) * 256])
                return ins
            p.step("dve", dua)

            def to_(E, cc=cc):
                ins = None
                for h in range(4):
                    for dvh in range(2):
                        for c2 in range(2):
                            E.matmul(cx.psC[:, h * 128 + dvh * 32:h * 128 + dvh * 32 + 32],
                                     s_sb[:, h, c2 * 256 + dvh * 128:c2 * 256 + dvh * 128 + 128],
                                     q_sb[:, h, c2, cc * 32:(cc + 1) * 32],
                                     start=(c2 == 0), stop=False)
                        ins = E.matmul(cx.psC[:, h * 128 + dvh * 32:h * 128 + dvh * 32 + 32],
                                       ua_sb[0:32, h, dvh * 128:(dvh + 1) * 128],
                                       at_sb[0:32, h, :], start=False, stop=True)
                return ins
            p.step("te", to_)

            def do_(E, cc=cc):
                ins = None
                for h in range(4):
                    for dvh in range(2):
                        ins = E.tensor_copy(o_sb[:, h, dvh, cc * 32:(cc + 1) * 32],
                                            cx.psC[:, h * 128 + dvh * 32:h * 128 + dvh * 32 + 32])
                return ins
            p.step("dve", do_)

            def tsu(E, g=g, cc=cc):
                ins = None
                for h in range(4):
                    for c2 in range(2):
                        ins = E.matmul(cx.psA[:, h * 512 + c2 * 256:h * 512 + (c2 + 1) * 256],
                                       ktm4[0:32, h, cc, c2 * 128:(c2 + 1) * 128],
                                       ua_sb[0:32, h, :],
                                       start=(g == 0 and cc == 0 and c2 == 0), stop=False,
                                       skip_group_check=True)
                return ins
            p.step("te", tsu)

            def dsu(E, cc=cc):
                ins = None
                for h in range(4):
                    ins = E.tensor_copy(s_sb[:, h, :], cx.psA[:, h * 512:(h + 1) * 512])
                return ins
            p.step("dve", dsu)

        def sto(E, g=g):
            ins = None
            for h in range(4):
                ins = p.dma_in_step(E, d_ofm.rearrange("(c p) t -> p c t", p=128)
                                    [:, 2 * h:2 * h + 2, g * 128:(g + 1) * 128],
                                    o_sb[:, h, :, :])
            return ins
        wait_at = p.cnt["flow"]

        def sto_body(E, sto=sto, wait_at=wait_at):
            E.wait_ge(p.sems["flow"], wait_at)
            sto(E)
        p.block.sync(sto_body)
    cx.pb.pop(); cx.pf.pop()

    if DEBUG_A8:
        def cpo(E):
            return E.dma_start(out=scr["tm2"].bitcast(BF16) if False else scr["tm2"], in_=d_ofm.rearrange("a b -> a b") if False else d_ofm)
        def cpo2(E):
            return E.dma_start(out=scr["tm2"].rearrange("a b -> (a b)"),
                               in_=d_ofm.rearrange("a b -> (a b)"))
        p.step("sp", cpo2, inc=16)

    cx.pb.push(); cx.pf.push()
    gam_bc = cx.pb.take(2 * T).bitcast(F32)
    mix_bc = cx.pb.take(2 * T).bitcast(F32)
    inv_bc = cx.pb.take(2 * T).bitcast(F32)
    vrow = cx.pb.take(T)
    orow = cx.pb.take(T)
    d1w = cx.pb.take(2 * T).bitcast(F32)
    emw = cx.pb.take(2 * T).bitcast(F32)
    omx = [cx.pb.take(2 * T).bitcast(F32) for _ in range(2)]
    sq2 = cx.pb.take(TC)
    rs2 = cx.pf.take(TC)
    rmsw = cx.pf.take(2)
    grow = cx.pb.take(TC)

    def lrw(E):
        return E.dma_start(out=rmsw, in_=W["A_rmsw"][0:1, :].rearrange("o (c p) -> p (o c)", p=128),
                           allow_slow_non_contiguous=True)
    p.step("sp", lrw, inc=16)
    for h in range(4):
        for which, dst in ((8 + h, gam_bc), (0 + h, mix_bc)):
            for ch in range(NCH):
                def ldr(E, which=which, ch=ch):
                    return E.dma_start(out=grow[0:1, :], in_=d_g[which:which + 1, ch * TC:(ch + 1) * TC])
                p.step("gp", ldr, inc=16)

                def teb2(E, ch=ch):
                    return E.matmul(cx.psD[:, 0:512], cx.ones[0:1, :], grow[0:1, :],
                                    start=True, stop=True)
                p.step("te", teb2)

                def dvb2(E, dst=dst, ch=ch):
                    return E.tensor_copy(dst[:, ch * TC:(ch + 1) * TC], cx.psD[:, 0:512])
                p.step("dve", dvb2)

        def dvi(E, h=h):
            return E.tensor_scalar(inv_bc, gam_bc, -1.0, 1.0, ALU.mult, ALU.add)
        p.step("dve", dvi)
        for dvh in range(2):
            c = 2 * h + dvh

            def ldo(E, c=c):
                p.dma_in_step(E, vrow, d_v[c * 128:(c + 1) * 128, :])
                return E.dma_start(out=orow, in_=d_ofm[c * 128:(c + 1) * 128, :])
            p.step("sp", ldo, inc=16)

            def dve1(E, dvh=dvh):
                E.tensor_mul(d1w, vrow, inv_bc)
                E.tensor_tensor_scan(emw, gam_bc, d1w, 0.0, ALU.mult, ALU.add)
                E.tensor_sub(emw, emw, orow)
                E.tensor_mul(emw, emw, mix_bc)
                return E.tensor_add(omx[dvh], emw, orow)
            p.step("dve", dve1)
        for ch in range(NCH):
            def sqh2(E, ch=ch):
                E.activation(sq2, omx[0][:, ch * TC:(ch + 1) * TC], AF.Square)
                return E.activation(cx.sq_bf, omx[1][:, ch * TC:(ch + 1) * TC], AF.Square)
            p.step("act", sqh2)

            def teh(E, ch=ch):
                E.matmul(cx.psC[:, 0:512], cx.ones, sq2, start=True, stop=False)
                return E.matmul(cx.psC[:, 0:512], cx.ones, cx.sq_bf, start=False, stop=True)
            p.step("te", teh)

            def sch(E, ch=ch):
                return E.activation(rs2, cx.psC[:, 0:512], AF.Sqrt, bias=cx.epsc[:, 1:2], scale=1.0 / 256.0)
            p.step("act", sch)

            def dvh2(E, h=h, ch=ch):
                E.reciprocal(rs2, rs2)
                for dvh in range(2):
                    E.tensor_mul(omx[dvh][:, ch * TC:(ch + 1) * TC],
                                 omx[dvh][:, ch * TC:(ch + 1) * TC], rs2)
                ins = None
                for dvh in range(2):
                    ins = E.tensor_single_scalar(omx[dvh][:, ch * TC:(ch + 1) * TC],
                                                 omx[dvh][:, ch * TC:(ch + 1) * TC],
                                                 rmsw[:, dvh:dvh + 1], ALU.mult)
                return ins
            p.step("dve", dvh2)

        def dvc2(E, h=h):
            E.tensor_copy(vrow, omx[0])
            return E.tensor_copy(orow, omx[1])
        p.step("dve", dvc2)

        def sto2(E, h=h):
            p.dma_in_step(E, d_ofm[(2 * h) * 128:(2 * h + 1) * 128, :], vrow)
            return E.dma_start(out=d_ofm[(2 * h + 1) * 128:(2 * h + 2) * 128, :], in_=orow)
        p.step("sp", sto2, inc=16)
    cx.pb.pop(); cx.pf.pop()

    cx.pb.push(); cx.pf.push()
    wo_sb = cx.pb.take3(ND, 1024)

    def lwo(E):
        return E.dma_start(out=wo_sb, in_=W["A_wo"].rearrange("(c p) f -> p c f", p=128))
    p.step("gp", lwo, inc=16)
    oc = cx.pb.take3(ND, TC)
    st = cx.stat
    for ch in range(NCH):
        def lo(E, ch=ch):
            return E.dma_start(out=oc, in_=d_ofm.rearrange("(c p) t -> p c t", p=128)[:, :, ch * TC:(ch + 1) * TC])
        p.step("sp", lo, inc=16)
        for tt in range(4):
            i = ch * 4 + tt

            def te2(E, tt=tt):
                ins = None
                for fc in range(2):
                    for d in range(ND):
                        ins = E.matmul(cx.psA[:, fc * 512:fc * 512 + 512],
                                       oc[:, d, tt * 128:(tt + 1) * 128],
                                       wo_sb[:, d, fc * 512:(fc + 1) * 512],
                                       start=(d == 0), stop=(d == ND - 1))
                return ins
            p.step("te", te2)

            def lxi(E, i=i):
                return E.dma_start(out=cx.x32b, in_=xin[i * 128:(i + 1) * 128, :])
            p.step("sp", lxi, inc=16)

            def dvf(E, i=i):
                E.tensor_copy(cx.x32a[:, 0:512], cx.psA[:, 0:512])
                E.tensor_copy(cx.x32a[:, 512:1024], cx.psA[:, 512:1024])
                return E.reduce_sum(st[:, 0:1], cx.x32a, axis=AX.X)
            p.step("dve", dvf)

            def scf(E, i=i):
                return E.activation(cx.x32c, cx.x32a, AF.Square, accum_out=st[:, 1:2])
            p.step("act", scf)

            def dvf2(E, i=i):
                return ln_var_ops(p, E, cx.stat)
            p.step("dve", dvf2)

            def scf2(E, i=i):
                return E.activation(st[:, 3:4], st[:, 2:3], AF.Sqrt)
            p.step("act", scf2)

            def dvf3(E, i=i):
                p.fence(E, E.reciprocal(st[:, 3:4], st[:, 3:4]))
                E.tensor_scalar(cx.x32c, cx.x32a, st[:, 4:5], st[:, 3:4], ALU.subtract, ALU.mult)
                return E.tensor_add(cx.x32b, cx.x32b, cx.x32c)
            p.step("dve", dvf3)

            def stf(E, i=i):
                return E.dma_start(out=xout[i * 128:(i + 1) * 128, :], in_=cx.x32b)
            p.step("sp", stf, inc=16)
    cx.pb.pop(); cx.pf.pop()


def emit_stage_c(cx, xin, xout, scr):
    p = cx.p
    W = cx.W
    d_tm0, d_xf_tm = scr["tm0"], scr["tm1"]
    d_xf_fm, d_hg, d_hp = scr["fm1"], scr["fm2"], scr["fm3"]
    d_yhat = scr["fm4"]
    st = cx.stat
    NC64 = 64
    HL = 29
    SLOT = 36

    emit_ln_tm(cx, xin, d_tm0)
    cx.pb.push()
    dft_sb = cx.pb.take(128)

    def ldft(E):
        return E.dma_start(out=dft_sb, in_=W["K_dft_l"][:, :])
    p.step("gp", ldft, inc=16)
    xt = cx.pb.take(1024)
    ft = cx.pb.take(1024)
    for i in range(NT):
        def lx(E, i=i):
            return E.dma_start(out=xt, in_=d_tm0[i * 128:(i + 1) * 128, :])
        p.step("sp", lx, inc=16)

        def te(E, i=i):
            E.matmul(cx.psA[:, 0:512], dft_sb, xt[:, 0:512], start=True, stop=True)
            return E.matmul(cx.psA[:, 512:1024], dft_sb, xt[:, 512:1024], start=True, stop=True)
        p.step("te", te)

        def dv(E, i=i):
            E.tensor_copy(ft[:, 0:512], cx.psA[:, 0:512])
            return E.tensor_copy(ft[:, 512:1024], cx.psA[:, 512:1024])
        p.step("dve", dv)

        def stf(E, i=i):
            return E.dma_start(out=d_xf_tm[i * 128:(i + 1) * 128, :], in_=ft)
        p.step("sp", stf, inc=16)
    cx.pb.pop()
    emit_transpose(cx, d_xf_tm, d_xf_fm, T, D)

    cx.pb.push(); cx.pf.push()
    wq_sb = cx.pb.take3(ND, 1024)
    wkv_sb = cx.pb.take3(ND, 512)
    wo_sb = cx.pb.take3(ND, 1024)
    hm_sb = cx.pb.take(128)
    rep_sb = cx.pb.take(16)
    mlow_sb = cx.pb.take(2 * 256).rearrange("p (t q) -> p t q", t=2)

    def lw(E):
        p.dma_in_step(E, wq_sb, W["C_qw"].rearrange("(c p) f -> p c f", p=128), eng="gp")
        p.dma_in_step(E, wkv_sb, W["C_kvw"].rearrange("(c p) f -> p c f", p=128), eng="gp")
        p.dma_in_step(E, wo_sb, W["C_ow"].rearrange("(c p) f -> p c f", p=128), eng="gp")
        p.dma_in_step(E, hm_sb, W["K_headmask"][:, :], eng="gp")
        p.dma_in_step(E, rep_sb[0:4, :], W["K_rep4"][:, :], eng="gp")
        return E.dma_start(out=mlow_sb, in_=W["K_mlow_neg"].rearrange("(t p) q -> p t q", p=128))
    p.step("gp", lw, inc=16)
    xlow = [cx.pb.take3(ND, 256) for _ in range(2)]
    xfrow = cx.pb.take(4096)
    for d in range(ND):
        def lxr(E, d=d):
            return E.dma_start(out=xfrow, in_=d_xf_fm[d * 128:(d + 1) * 128, :])
        p.step("sp", lxr, inc=16)

        def dvl(E, d=d):
            v3 = xfrow.rearrange("p (c f) -> p c f", f=64)
            E.tensor_copy(xlow[0][:, d, :].rearrange("p (c f) -> p c f", f=4), v3[:, :, 0:4])
            p.fence(E, E.memset(xlow[1][:, d, :], 0.0))
            return E.tensor_copy(xlow[1][:, d, :].rearrange("p (c f) -> p c f", f=4)[:, :, 1:4],
                                 v3[:, :, 33:36])
        p.step("dve", dvl)
    qf = [cx.pb.take3(ND, 256) for _ in range(2)]
    kvf = [cx.pb.take3(4, 256) for _ in range(2)]
    vtm = [cx.pb.take(2 * 4 * 65).rearrange("p (t g e) -> p t g e", t=2, g=4) for _ in range(2)]
    alow = [cx.pb.take3(ND, 256) for _ in range(2)]
    es = cx.pb.take(512)
    qst = cx.pb.take(256)
    kst = cx.pb.take(256)
    ptb = cx.pb.take(256)
    for wi in range(2):
        for f in range(ND):
            def teq(E, wi=wi, f=f):
                ins = None
                for d in range(ND):
                    ins = E.matmul(cx.psA[:, 0:256], wq_sb[:, d, f * 128:(f + 1) * 128],
                                   xlow[wi][:, d, :], start=(d == 0), stop=(d == ND - 1))
                return ins
            p.step("te", teq)

            def dvq(E, wi=wi, f=f):
                return E.tensor_scalar_mul(qf[wi][:, f, :], cx.psA[:, 0:256], 0.125)
            p.step("dve", dvq)
        for f in range(4):
            def tek(E, wi=wi, f=f):
                ins = None
                for d in range(ND):
                    ins = E.matmul(cx.psA[:, 0:256], wkv_sb[:, d, f * 128:(f + 1) * 128],
                                   xlow[wi][:, d, :], start=(d == 0), stop=(d == ND - 1))
                return ins
            p.step("te", tek)

            def dvk(E, wi=wi, f=f):
                return E.tensor_copy(kvf[wi][:, f, :], cx.psA[:, 0:256])
            p.step("dve", dvk)
        def mset(E, wi=wi):
            return E.memset(vtm[wi][:, :, :, :], 1.0)
        p.step("dve", mset)
        for tt in range(2):
            def tev(E, wi=wi, tt=tt):
                ins = None
                for gc in range(2):
                    for d in range(ND):
                        ins = E.matmul(cx.psB[:, gc * 128:(gc + 1) * 128],
                                       xlow[wi][:, d, tt * 128:(tt + 1) * 128],
                                       wkv_sb[:, d, 256 + gc * 128:256 + (gc + 1) * 128],
                                       start=(d == 0), stop=(d == ND - 1))
                return ins
            p.step("te", tev)

            def dvv(E, wi=wi, tt=tt):
                ins = None
                for g in range(4):
                    ins = E.tensor_copy(vtm[wi][:, tt, g, 0:64],
                                        cx.psB[:, g * 64:(g + 1) * 64])
                return ins
            p.step("dve", dvv)
        for h in range(16):
            g = h // 4
            qb, qt = 64 * (h % 2), h // 2
            kb, kt = 64 * (g % 2), g // 2

            def dvst(E, wi=wi, qb=qb, qt=qt, kb=kb, kt=kt):
                E.tensor_copy(qst[0:64, :], qf[wi][qb:qb + 64, qt, :])
                return E.tensor_copy(kst[0:64, :], kvf[wi][kb:kb + 64, kt, :])
            p.step("dve", dvst)
            for qi in range(2):
                def tes(E, wi=wi, qi=qi):
                    E.matmul(cx.psA[:, 0:256], cx.ident, mlow_sb[:, qi, :],
                             start=True, stop=False)
                    return E.matmul(cx.psA[:, 0:256], qst[0:64, qi * 128:(qi + 1) * 128],
                                    kst[0:64, :], start=False, stop=True)
                p.step("te", tes)

                def dvm(E, wi=wi, qi=qi):
                    return E.reduce_max(st[:, 8:9], cx.psA[:, 0:256], axis=AX.X, negate=True)
                p.step("dve", dvm)

                def sce(E, wi=wi, qi=qi):
                    return E.activation(es[:, 0:256], cx.psA[:, 0:256], AF.Exp,
                                        bias=st[:, 8:9], accum_out=st[:, 9:10])
                p.step("act", sce)

                def dvn(E, wi=wi, qi=qi):
                    p.fence(E, E.reciprocal(st[:, 10:11], st[:, 9:10]))
                    return E.tensor_single_scalar(es[:, 0:256], es[:, 0:256],
                                                  st[:, 10:11], ALU.mult)
                p.step("dve", dvn)

                def tet(E, wi=wi, qi=qi):
                    E.transpose(cx.psB.bitcast(BF16)[:, 0:128], es[:, 0:128], cx.ident)
                    return E.transpose(cx.psB.bitcast(BF16)[:, 128:256], es[:, 128:256], cx.ident)
                p.step("te", tet)

                def dvt(E, wi=wi, qi=qi):
                    return E.tensor_copy(ptb, cx.psB.bitcast(BF16)[:, 0:256])
                p.step("dve", dvt)

                def tpv(E, wi=wi, g=g, qi=qi):
                    ins = None
                    for kj in range(2):
                        ins = E.matmul(cx.psC[0:64, 0:128], vtm[wi][:, kj, g, 0:64],
                                       ptb[:, kj * 128:(kj + 1) * 128],
                                       start=(kj == 0), stop=(kj == 1))
                    return ins
                p.step("te", tpv)

                def dvo2(E, wi=wi, h=h, qi=qi):
                    return E.tensor_copy(
                        alow[wi][(h % 2) * 64:(h % 2) * 64 + 64, h // 2, qi * 128:(qi + 1) * 128],
                        cx.psC[0:64, 0:128])
                p.step("dve", dvo2)
        ofm = alow[wi]
        res = qf[wi]
        for f in range(ND):
            def teo(E, wi=wi, f=f, ofm=ofm):
                ins = None
                for d in range(ND):
                    ins = E.matmul(cx.psA[:, 0:256], wo_sb[:, d, f * 128:(f + 1) * 128],
                                   ofm[:, d, :], start=(d == 0), stop=(d == ND - 1))
                return ins
            p.step("te", teo)

            def dvp(E, wi=wi, f=f, res=res):
                return E.tensor_copy(res[:, f, :], cx.psA[:, 0:256])
            p.step("dve", dvp)
    alf = cx.pf.take(2)

    def lal(E):
        return E.dma_start(out=cx.rowbf[0:1, 0:1], in_=W["C_alpha"][0:1, 0:1])
    p.step("gp", lal, inc=16)

    def scal(E):
        return E.activation(st[0:1, 8:9], cx.rowbf[0:1, 0:1], AF.Sigmoid)
    p.step("act", scal)

    def dval(E):
        return E.tensor_copy(cx.negc_bf[0:1, 0:1], st[0:1, 8:9])
    p.step("dve", dval)

    def teal(E):
        return E.matmul(cx.psD[:, 0:1], cx.ones[0:1, :], cx.negc_bf[0:1, 0:1],
                        start=True, stop=True)
    p.step("te", teal)

    def dval2(E):
        return E.tensor_copy(alf[:, 0:1], cx.psD[:, 0:1])
    p.step("dve", dval2)
    yl = cx.pb.take(256)
    yh_row = cx.pb.take(4096)
    for d in range(ND):
        def dvbl(E, d=d):
            for wi in range(2):
                p.fence(E, E.tensor_sub(cx.o32[:, 0:256], qf[wi][:, d, :], xlow[wi][:, d, :]))
                p.fence(E, E.tensor_single_scalar(cx.o32[:, 0:256], cx.o32[:, 0:256], alf[:, 0:1], ALU.mult))
                p.fence(E, E.tensor_add(cx.rdb32[:, 0:256], cx.o32[:, 0:256], xlow[wi][:, d, :]))
                p.fence(E, E.tensor_copy(yl, cx.rdb32[:, 0:256]))
                ylv = yl.rearrange("p (c f) -> p c f", f=4)
                if wi == 0:
                    p.fence(E, E.tensor_copy(yh_row.rearrange("p (c f) -> p c f", f=64)[:, :, 0:4], ylv))
                else:
                    p.fence(E, E.tensor_copy(yh_row.rearrange("p (c f) -> p c f", f=64)[:, :, 33:36],
                                             ylv[:, :, 1:4]))
            return E.engine_nop()
        p.step("dve", dvbl)

        def styl(E, d=d):
            return E.dma_start(out=d_yhat[d * 128:(d + 1) * 128, :], in_=yh_row)
        p.step("sp", styl, inc=16)
    cx.pb.pop(); cx.pf.pop()

    cx.pb.push(); cx.pf.push()
    dwc_sb = cx.pf.take(8 * 14).rearrange("p (c k) -> p c k", c=8)

    def ldw(E):
        return E.dma_start(out=dwc_sb, in_=W["C_dw"].rearrange("(c p) k -> p c k", p=128))
    p.step("sp", ldw, inc=16)
    hpadbuf = cx.pb.take(NC64 * SLOT)
    hacc = cx.pb.take(2 * NC64 * HL).bitcast(F32)
    hg = cx.pb.take(NC64 * HL)
    xfr2 = cx.pb.take(4096)
    for wi in range(2):
        for d in range(ND):
            def lx2(E, wi=wi, d=d):
                return E.dma_start(out=xfr2, in_=d_xf_fm[d * 128:(d + 1) * 128, :])
            p.step("sp", lx2, inc=16)

            def dvp1(E, wi=wi, d=d):
                p.fence(E, E.memset(hpadbuf, 0.0))
                v3 = xfr2.rearrange("p (c f) -> p c f", f=64)
                src2 = v3[:, :, 4:33] if wi == 0 else v3[:, :, 36:64]
                dst = hpadbuf.rearrange("p (c s) -> p c s", s=SLOT)
                if wi == 0:
                    return E.tensor_copy(dst[:, :, 6:35], src2)
                return E.tensor_copy(dst[:, :, 6:34], src2)
            p.step("dve", dvp1)

            def dvp2(E, wi=wi, d=d):
                hv = hacc.rearrange("p (c l) -> p c l", l=HL)
                pv = hpadbuf.rearrange("p (c s) -> p c s", s=SLOT)
                E.tensor_single_scalar(hv, pv[:, :, 0:HL], dwc_sb[:, d, 7 * wi:7 * wi + 1], ALU.mult)
                for k in range(1, 7):
                    E.scalar_tensor_tensor(hv, pv[:, :, k:k + HL], dwc_sb[:, d, 7 * wi + k:7 * wi + k + 1],
                                           hv, ALU.mult, ALU.add)
                return E.engine_nop()
            p.step("dve", dvp2)

            def scp(E, wi=wi, d=d):
                return E.activation(hg, hacc, AF.Gelu)
            p.step("act", scp)

            def sth(E, wi=wi, d=d):
                return E.dma_start(out=d_hg[d * 128:(d + 1) * 128, wi * 2048:wi * 2048 + NC64 * HL],
                                   in_=hg)
            p.step("sp", sth, inc=16)
    pw_sb = cx.pb.take3(ND, 1024)
    hgc = cx.pb.take3(ND, 512)
    hp_out = cx.pb.take3(ND, 512)
    CW = [512, 512, 512, 320]
    for wi in range(2):
        def lpw(E, wi=wi):
            return E.dma_start(out=pw_sb, in_=W["C_pw"].rearrange("(c p) f -> p c f", p=128)
                               [:, :, wi * 1024:(wi + 1) * 1024])
        p.step("gp", lpw, inc=16)
        off = 0
        for ci, cw in enumerate(CW):
            def lhg(E, wi=wi, off=off, cw=cw):
                return E.dma_start(out=hgc[:, :, 0:cw],
                                   in_=d_hg.rearrange("(c p) f -> p c f", p=128)
                                   [:, :, wi * 2048 + off:wi * 2048 + off + cw])
            p.step("sp", lhg, inc=16)
            for f in range(ND):
                def tep(E, wi=wi, f=f, cw=cw):
                    ins = None
                    for d in range(ND):
                        ins = E.matmul(cx.psA[:, 0:cw], pw_sb[:, d, f * 128:(f + 1) * 128],
                                       hgc[:, d, 0:cw], start=(d == 0), stop=(d == ND - 1))
                    return ins
                p.step("te", tep)

                def dvpp(E, wi=wi, f=f, cw=cw):
                    return E.tensor_copy(hp_out[:, f, 0:cw], cx.psA[:, 0:cw])
                p.step("dve", dvpp)

            def stp(E, wi=wi, off=off, cw=cw):
                return E.dma_start(out=d_hp.rearrange("(c p) f -> p c f", p=128)
                                   [:, :, wi * 2048 + off:wi * 2048 + off + cw],
                                   in_=hp_out[:, :, 0:cw])
            p.step("sp", stp, inc=16)
            off += cw
    cx.pb.pop(); cx.pf.pop()

    cx.pb.push(); cx.pf.push()
    gw_sb = cx.pb.take3(ND, 1024)
    cm_sb = cx.pb.take3(ND, 64)
    g_sb = cx.pb.take3(ND, 64)
    xfr3 = cx.pb.take(4096)

    def lgw(E):
        return E.dma_start(out=gw_sb, in_=W["C_gw"].rearrange("(c p) f -> p c f", p=128))
    p.step("gp", lgw, inc=16)
    for d in range(ND):
        def lx3(E, d=d):
            return E.dma_start(out=xfr3, in_=d_xf_fm[d * 128:(d + 1) * 128, :])
        p.step("sp", lx3, inc=16)

        def dvcm(E, d=d):
            return E.tensor_copy(cm_sb[:, d, :], xfr3.rearrange("p (c f) -> p c f", f=64)[:, :, 0])
        p.step("dve", dvcm)
    for f in range(ND):
        def teg(E, f=f):
            ins = None
            for d in range(ND):
                ins = E.matmul(cx.psC[:, 0:64], gw_sb[:, d, f * 128:(f + 1) * 128],
                               cm_sb[:, d, :], start=(d == 0), stop=(d == ND - 1))
            return ins
        p.step("te", teg)

        def scg(E, f=f):
            return E.activation(g_sb[:, f, :], cx.psC[:, 0:64], AF.Sigmoid, scale=1.0 / 64.0)
        p.step("act", scg)
    hpr = cx.pb.take3(2, NC64 * HL)
    yhrow = cx.pb.take(4096)
    xfr4 = cx.pb.take(4096)
    hi32 = cx.pb.take(2 * NC64 * HL).bitcast(F32)
    for d in range(ND):
        def lhp(E, d=d):
            p.dma_in_step(E, hpr[:, 0, :], d_hp[d * 128:(d + 1) * 128, 0:NC64 * HL])
            p.dma_in_step(E, hpr[:, 1, :], d_hp[d * 128:(d + 1) * 128, 2048:2048 + NC64 * HL])
            p.dma_in_step(E, xfr4, d_xf_fm[d * 128:(d + 1) * 128, :])
            return E.dma_start(out=yhrow, in_=d_yhat[d * 128:(d + 1) * 128, :])
        p.step("sp", lhp, inc=16)

        def dvh(E, d=d):
            xv = xfr4.rearrange("p (c f) -> p c f", f=64)
            yv = yhrow.rearrange("p (c f) -> p c f", f=64)
            gb = g_sb[:, d, :][:, :, None].broadcast_to([128, NC64, HL])
            for wi in range(2):
                hv = hpr[:, wi, :].rearrange("p (c l) -> p c l", l=HL)
                if wi == 0:
                    E.tensor_tensor(hi32.rearrange("p (c l) -> p c l", l=HL), hv, gb, ALU.mult)
                    E.tensor_tensor(yv[:, :, 4:33], hi32.rearrange("p (c l) -> p c l", l=HL),
                                    xv[:, :, 4:33], ALU.add)
                else:
                    E.tensor_tensor(hi32.rearrange("p (c l) -> p c l", l=HL),
                                    hv, gb, ALU.mult)
                    E.tensor_tensor(yv[:, :, 36:64],
                                    hi32.rearrange("p (c l) -> p c l", l=HL)[:, :, 0:28],
                                    xv[:, :, 36:64], ALU.add)
            return E.engine_nop()
        p.step("dve", dvh)

        def sty(E, d=d):
            return E.dma_start(out=d_yhat[d * 128:(d + 1) * 128, :], in_=yhrow)
        p.step("sp", sty, inc=16)
    cx.pb.pop(); cx.pf.pop()

    emit_transpose(cx, d_yhat, d_xf_tm, D, T)
    cx.pb.push()
    idft_sb = cx.pb.take(128)

    def lidft(E):
        return E.dma_start(out=idft_sb, in_=W["K_idft_l"][:, :])
    p.step("gp", lidft, inc=16)
    yt2 = cx.pb.take(1024)
    for i in range(NT):
        def ly(E, i=i):
            return E.dma_start(out=yt2, in_=d_xf_tm[i * 128:(i + 1) * 128, :])
        p.step("sp", ly, inc=16)

        def tei(E, i=i):
            E.matmul(cx.psA[:, 0:512], idft_sb, yt2[:, 0:512], start=True, stop=True)
            return E.matmul(cx.psA[:, 512:1024], idft_sb, yt2[:, 512:1024], start=True, stop=True)
        p.step("te", tei)

        def lxi(E, i=i):
            return E.dma_start(out=cx.x32b, in_=xin[i * 128:(i + 1) * 128, :])
        p.step("sp", lxi, inc=16)

        def dvf(E, i=i):
            E.tensor_copy(cx.x32a[:, 0:512], cx.psA[:, 0:512])
            E.tensor_copy(cx.x32a[:, 512:1024], cx.psA[:, 512:1024])
            return E.reduce_sum(st[:, 0:1], cx.x32a, axis=AX.X)
        p.step("dve", dvf)

        def scf(E, i=i):
            return E.activation(cx.x32c, cx.x32a, AF.Square, accum_out=st[:, 1:2])
        p.step("act", scf)

        def dvf2(E, i=i):
            return ln_var_ops(p, E, cx.stat)
        p.step("dve", dvf2)

        def scf2(E, i=i):
            return E.activation(st[:, 3:4], st[:, 2:3], AF.Sqrt)
        p.step("act", scf2)

        def dvf3(E, i=i):
            p.fence(E, E.reciprocal(st[:, 3:4], st[:, 3:4]))
            E.tensor_scalar(cx.x32c, cx.x32a, st[:, 4:5], st[:, 3:4], ALU.subtract, ALU.mult)
            return E.tensor_add(cx.x32b, cx.x32b, cx.x32c)
        p.step("dve", dvf3)

        def stf(E, i=i):
            return E.dma_start(out=xout[i * 128:(i + 1) * 128, :], in_=cx.x32b)
        p.step("sp", stf, inc=16)
    cx.pb.pop()


def emit_stage_d(cx, xin, xout, scr):
    p = cx.p
    W = cx.W
    d_tm0, d_tm1, d_tm2 = scr["tm0"], scr["tm1"], scr["tm2"]
    d_fm0, d_fm1 = scr["fm0"], scr["fm1"]
    d_vsc = scr["tm3"]

    emit_ln_tm(cx, xin, d_tm0)
    cx.pf.push()
    cb_g = cx.pf.take(1024)
    emit_row_bcast(cx, W["D_g12"][0:1, :], cb_g, 1024)
    emit_ln_tm(cx, d_tm0, d_tm1, gbc=cb_g, rms=True, eps=1e-6, ld_eng="gp")
    cx.pf.pop()
    emit_transpose(cx, d_tm1, d_fm0, T, D)

    cx.pb.push(); cx.pf.push()
    wq_sb = cx.pb.take3(ND, 1024)
    wdn_sb = cx.pb.take3(ND, 64)
    wupk_sb = cx.pb.take(1024)
    wupv_sb = cx.pb.take(1024)
    wo_sb = cx.pb.take(16 * 1024).rearrange("p (h f) -> p h f", h=16)
    mneg_sb = cx.pb.take(128)

    def lw(E):
        p.dma_in_step(E, wq_sb, W["D_wq"].rearrange("(c p) f -> p c f", p=128), eng="gp")
        p.dma_in_step(E, wdn_sb, W["D_wdown"].rearrange("(c p) f -> p c f", p=128), eng="gp")
        p.dma_in_step(E, wupk_sb[0:64, :], W["D_wupk"][:, :], eng="gp")
        p.dma_in_step(E, wupv_sb[0:64, :], W["D_wupv"][:, :], eng="gp")
        p.dma_in_step(E, wo_sb[0:64, :, :],
                      W["D_wo"].rearrange("(h p) f -> p h f", p=64), eng="gp")
        return E.dma_start(out=mneg_sb, in_=W["K_m_d_neg"][:, :])
    p.step("gp", lw, inc=16)
    bq_sb = cx.pf.take(16)
    bupk_sb = cx.pf.take(16)
    bdn_sb = cx.pf.take(1)
    bupv_bc = cx.pf.take(1024)
    bo_bc = cx.pf.take(1024)

    def lb(E):
        p.dma_in_step(E, bq_sb[0:64, :], W["D_bq"][0:1, :].rearrange("o (h p) -> p (o h)", p=64),
                      allow_slow_non_contiguous=True)
        p.dma_in_step(E, bupk_sb[0:64, :], W["D_bupk"][0:1, :].rearrange("o (h p) -> p (o h)", p=64),
                      allow_slow_non_contiguous=True)
        return E.dma_start(out=bdn_sb[0:64, :], in_=W["D_bdown"][0:1, :].rearrange("o p -> p o"),
                           allow_slow_non_contiguous=True)
    p.step("sp", lb, inc=16)

    def sc8(E):
        return E.tensor_scalar_mul(bq_sb[0:64, :], bq_sb[0:64, :], 0.125)
    p.step("dve", sc8)
    emit_row_bcast(cx, W["D_bupv"][0:1, :], bupv_bc, 1024)
    emit_row_bcast(cx, W["D_bo"][0:1, :], bo_bc, 1024)

    xc = cx.pb.take3(ND, TC)
    qdm = cx.pb.take(8192).rearrange("p (t h) -> p t h", h=16)
    kdm = cx.pb.take(8192).rearrange("p (t h) -> p t h", h=16)
    dn_sb = cx.pb.take(512)
    vtm_sb = cx.pb.take3(4, 1024)
    v8_sb = cx.pb.take(64 * 64).rearrange("p (g e) -> p g e", g=64)
    es8 = cx.pb.take(128)
    p8_sb = cx.pb.take(128)
    odhm = cx.pb.take(16 * 512).rearrange("p (h t) -> p h t", h=16)
    xd_t = cx.pb.take(1024)
    x2_t = cx.pb.take(1024)
    st = cx.stat
    for ch in range(NCH):
        def lx(E, ch=ch):
            return E.dma_start(out=xc, in_=d_fm0.rearrange("(c p) t -> p c t", p=128)[:, :, ch * TC:(ch + 1) * TC])
        p.step("sp", lx, inc=16)

        def te_dn(E, ch=ch):
            ins = None
            for d in range(ND):
                ins = E.matmul(cx.psC[0:64, 0:512], wdn_sb[:, d, :], xc[:, d, :],
                               start=(d == 0), stop=(d == ND - 1))
            return ins
        p.step("te", te_dn)

        def ev_dn(E, ch=ch):
            return E.activation(dn_sb[0:64, :], cx.psC[0:64, 0:512], AF.Identity,
                                bias=bdn_sb[0:64, 0:1])
        p.step("act", ev_dn)
        for h in range(16):
            def te_q(E, h=h):
                ins = None
                for d in range(ND):
                    ins = E.matmul(cx.psA[0:64, 0:512], wq_sb[:, d, h * 64:(h + 1) * 64],
                                   xc[:, d, :], start=(d == 0), stop=(d == ND - 1))
                return ins
            p.step("te", te_q)

            def ev_q(E, h=h):
                return E.activation(qdm[0:64, :, h:h + 1], cx.psA[0:64, 0:512],
                                    AF.Identity, bias=bq_sb[0:64, h:h + 1], scale=0.125)
            p.step("act", ev_q)

            def te_k(E, h=h):
                return E.matmul(cx.psA[0:64, 512:1024], wupk_sb[0:64, h * 64:(h + 1) * 64],
                                dn_sb[0:64, :], start=True, stop=True)
            p.step("te", te_k)

            def ev_k(E, h=h):
                return E.activation(kdm[0:64, :, h:h + 1], cx.psA[0:64, 512:1024],
                                    AF.Identity, bias=bupk_sb[0:64, h:h + 1])
            p.step("act", ev_k)
        for ts in range(4):
            def te_v(E, ts=ts):
                for gc in range(2):
                    ins = E.matmul(cx.psB[:, gc * 512:(gc + 1) * 512],
                                   dn_sb[0:64, ts * 128:(ts + 1) * 128],
                                   wupv_sb[0:64, gc * 512:(gc + 1) * 512],
                                   start=True, stop=True)
                return ins
            p.step("te", te_v)

            def ev_v(E, ts=ts):
                E.scalar_tensor_tensor(vtm_sb[:, ts, 0:512], cx.psB[:, 0:512], 1.0,
                                       bupv_bc[:, 0:512], ALU.mult, ALU.add)
                return E.scalar_tensor_tensor(vtm_sb[:, ts, 512:1024], cx.psB[:, 512:1024], 1.0,
                                              bupv_bc[:, 512:1024], ALU.mult, ALU.add)
            p.step("dve", ev_v)

            def st_v(E, ch=ch, ts=ts):
                r = ch * 512 + ts * 128
                return E.dma_start(out=d_vsc[r:r + 128, :], in_=vtm_sb[:, ts, :])
            p.step("sp", st_v, inc=16)

        def ld_v8(E, ch=ch):
            src = d_vsc.rearrange("(r j) (g e) -> r (j g) e", j=8, g=16)
            src = src[ch * 64:(ch + 1) * 64].transpose([1, 0, 2])
            return E.dma_start(out=v8_sb, in_=src)
        p.step("sp", ld_v8, inc=16)
        for grp in range(64):
            def te_s(E, grp=grp):
                E.matmul(cx.psA[:, 0:128], cx.ident, mneg_sb, start=True, stop=False)
                return E.matmul(cx.psA[:, 0:128],
                                qdm[0:64, :, :].rearrange("p t h -> p (t h)")[:, grp * 128:(grp + 1) * 128],
                                kdm[0:64, :, :].rearrange("p t h -> p (t h)")[:, grp * 128:(grp + 1) * 128],
                                start=False, stop=True)
            p.step("te", te_s)

            def dv_m(E, grp=grp):
                return E.reduce_max(st[:, 8:9], cx.psA[:, 0:128], axis=AX.X, negate=True)
            p.step("dve", dv_m)

            def sc_e(E, grp=grp):
                return E.activation(es8, cx.psA[:, 0:128], AF.Exp, bias=st[:, 8:9],
                                    accum_out=st[:, 9:10])
            p.step("act", sc_e)

            def dv_n(E, grp=grp):
                p.fence(E, E.reciprocal(st[:, 10:11], st[:, 9:10]))
                return E.tensor_single_scalar(es8, es8, st[:, 10:11], ALU.mult)
            p.step("dve", dv_n)

            def te_t(E, grp=grp):
                return E.transpose(cx.psB.bitcast(BF16)[:, 0:128], es8, cx.ident)
            p.step("te", te_t)

            def dv_p(E, grp=grp):
                return E.tensor_copy(p8_sb, cx.psB.bitcast(BF16)[:, 0:128])
            p.step("dve", dv_p)

            def te_pv(E, grp=grp):
                return E.matmul(cx.psC[0:64, 0:128], v8_sb[:, grp, :], p8_sb,
                                start=True, stop=True)
            p.step("te", te_pv)

            def dv_o(E, grp=grp):
                src = cx.psC[0:64, 0:128].rearrange("p (j h) -> p j h", j=8)
                dst = odhm[0:64, :, grp * 8:grp * 8 + 8].transpose([0, 2, 1])
                return E.tensor_copy(dst, src)
            p.step("dve", dv_o)
        for ts in range(4):
            i = ch * 4 + ts

            def ld_xd(E, i=i):
                return E.dma_start(out=xd_t, in_=d_tm0[i * 128:(i + 1) * 128, :])
            p.step("sp", ld_xd, inc=16)

            def te_o(E, ts=ts):
                ins = None
                for fc in range(2):
                    for h in range(16):
                        ins = E.matmul(cx.psA[:, fc * 512:(fc + 1) * 512],
                                       odhm[0:64, h, ts * 128:(ts + 1) * 128],
                                       wo_sb[0:64, h, fc * 512:(fc + 1) * 512],
                                       start=(h == 0), stop=(h == 15))
                return ins
            p.step("te", te_o)

            def dv_x2(E, i=i):
                E.tensor_copy(cx.x32a, xd_t)
                E.tensor_add(cx.x32a, cx.x32a, bo_bc)
                E.tensor_add(cx.x32a[:, 0:512], cx.x32a[:, 0:512], cx.psA[:, 0:512])
                E.tensor_add(cx.x32a[:, 512:1024], cx.x32a[:, 512:1024], cx.psA[:, 512:1024])
                return E.tensor_copy(x2_t, cx.x32a)
            p.step("dve", dv_x2)

            def st_x2(E, i=i):
                return E.dma_start(out=d_tm2[i * 128:(i + 1) * 128, :], in_=x2_t)
            p.step("sp", st_x2, inc=16)
    cx.pb.pop(); cx.pf.pop()

    cx.pf.push()
    cb_g2 = cx.pf.take(1024)
    emit_row_bcast(cx, W["D_g12"][1:2, :], cb_g2, 1024)
    emit_ln_tm(cx, d_tm2, d_tm1, gbc=cb_g2, rms=True, eps=1e-6, ld_eng="gp")
    cx.pf.pop()
    emit_transpose(cx, d_tm1, d_fm1, T, D)

    cx.pb.push(); cx.pf.push()
    wg_sb = cx.pb.take3(ND, 8)
    gate_all = cx.pf.take(NT * 8).rearrange("p (t e) -> p t e", t=NT)
    bg_bc = cx.pf.take(8)
    g32 = cx.pf.take(8)

    def lwg(E):
        return E.dma_start(out=wg_sb, in_=W["D_wg"].rearrange("(c p) f -> p c f", p=128))
    p.step("gp", lwg, inc=16)
    emit_row_bcast(cx, W["D_bg"][0:1, :], bg_bc, 8)
    xc2 = cx.pb.take3(ND, TC)
    for ch in range(NCH):
        def lx2(E, ch=ch):
            return E.dma_start(out=xc2, in_=d_fm1.rearrange("(c p) t -> p c t", p=128)[:, :, ch * TC:(ch + 1) * TC])
        p.step("sp", lx2, inc=16)
        for tt in range(4):
            i = ch * 4 + tt

            def te_g(E, tt=tt):
                ins = None
                for d in range(ND):
                    ins = E.matmul(cx.psC[:, 0:8], xc2[:, d, tt * 128:(tt + 1) * 128],
                                   wg_sb[:, d, :], start=(d == 0), stop=(d == ND - 1))
                return ins
            p.step("te", te_g)

            def dv_g(E, i=i):
                p.fence(E, E.tensor_add(g32, cx.psC[:, 0:8], bg_bc))
                return E.reduce_max(st[:, 8:9], g32, axis=AX.X, negate=True)
            p.step("dve", dv_g)

            def sc_g(E, i=i):
                return E.activation(g32, g32, AF.Exp, bias=st[:, 8:9], accum_out=st[:, 9:10])
            p.step("act", sc_g)

            def dv_g2(E, i=i):
                p.fence(E, E.reciprocal(st[:, 10:11], st[:, 9:10]))
                return E.tensor_single_scalar(gate_all[:, i, :], g32, st[:, 10:11], ALU.mult)
            p.step("dve", dv_g2)
    cx.pb.pop()

    cx.pb.push()
    we_sb = cx.pb.take3(ND, 1024)
    macc2 = cx.pb.take(16 * 1024).rearrange("p (s f) -> p s f", s=16)
    xc3b = [cx.pb.take3(ND, TC) for _ in range(2)]
    x2b = cx.pb.take(1024)
    zbanks = [(cx.psB[:, 0:512], cx.psB[:, 512:1024]), (cx.psC, cx.psD)]
    for thalf in range(2):
        for e in range(8):
            def lwe(E, e=e):
                return E.dma_start(out=we_sb, in_=W["D_we"][e].rearrange("(c p) f -> p c f", p=128))
            p.step("gp", lwe, inc=16)
            e_start = p.mark()
            ld_done = [None] * 4
            tz_done = [None] * 4
            dz_done = []
            for ch in range(4):
                cc = thalf * 4 + ch
                cb = ch % 2

                def lx3(E, cc=cc, cb=cb):
                    return E.dma_start(out=xc3b[cb], in_=d_fm1.rearrange("(c p) t -> p c t", p=128)[:, :, cc * TC:(cc + 1) * TC])
                wa_ = e_start[0] if ch < 2 else tz_done[ch - 2][0]
                ld_done[ch] = p.step("sp", lx3, inc=16, wait_at=wa_)
                for ts in range(4):
                    s_loc = ch * 4 + ts
                    i = cc * 4 + ts
                    zi = ch * 4 + ts
                    zb = zbanks[zi % 2]

                    def te_z(E, ts=ts, cb=cb, zb=zb):
                        ins = None
                        for dc in range(2):
                            for d in range(ND):
                                ins = E.matmul(zb[dc],
                                               xc3b[cb][:, d, ts * 128:(ts + 1) * 128],
                                               we_sb[:, d, dc * 512:(dc + 1) * 512],
                                               start=(d == 0), stop=(d == ND - 1))
                        return ins
                    zwa = e_start[0] if zi < 2 else dz_done[zi - 2][0]
                    tz_done[ch] = p.step("te", te_z, wait_at=zwa, dml_at=ld_done[ch][1])

                    def dv_z(E, e=e, s_loc=s_loc, i=i, zb=zb):
                        gs = gate_all[:, i, e:e + 1]
                        m0 = macc2[:, s_loc, 0:512]
                        m1 = macc2[:, s_loc, 512:1024]
                        if e == 0:
                            E.tensor_single_scalar(m0, zb[0], gs, ALU.mult)
                            return E.tensor_single_scalar(m1, zb[1], gs, ALU.mult)
                        E.scalar_tensor_tensor(m0, zb[0], gs, m0, ALU.mult, ALU.add)
                        return E.scalar_tensor_tensor(m1, zb[1], gs, m1, ALU.mult, ALU.add)
                    dz_done.append(p.step("dve", dv_z, dml_at=ld_done[ch][1]))
        for s in range(16):
            i = thalf * 16 + s

            def ld_x2(E, i=i):
                return E.dma_start(out=x2b, in_=d_tm2[i * 128:(i + 1) * 128, :])
            p.step("sp", ld_x2, inc=16)

            def ld_xi(E, i=i):
                return E.dma_start(out=cx.x32b, in_=xin[i * 128:(i + 1) * 128, :])
            p.step("sp", ld_xi, inc=16)

            def dvz(E, i=i, s=s):
                E.tensor_copy(cx.x32a, x2b)
                E.tensor_add(cx.x32a, cx.x32a, macc2[:, s, :])
                return E.reduce_sum(st[:, 0:1], cx.x32a, axis=AX.X)
            p.step("dve", dvz)

            def scz(E, i=i):
                return E.activation(cx.x32c, cx.x32a, AF.Square, accum_out=st[:, 1:2])
            p.step("act", scz)

            def dvz2(E, i=i):
                return ln_var_ops(p, E, cx.stat)
            p.step("dve", dvz2)

            def scz2(E, i=i):
                return E.activation(st[:, 3:4], st[:, 2:3], AF.Sqrt)
            p.step("act", scz2)

            def dvz3(E, i=i):
                p.fence(E, E.reciprocal(st[:, 3:4], st[:, 3:4]))
                E.tensor_scalar(cx.x32c, cx.x32a, st[:, 4:5], st[:, 3:4], ALU.subtract, ALU.mult)
                return E.tensor_add(cx.x32b, cx.x32b, cx.x32c)
            p.step("dve", dvz3)

            def stx(E, i=i):
                return E.dma_start(out=xout[i * 128:(i + 1) * 128, :], in_=cx.x32b)
            p.step("sp", stx, inc=16)
    cx.pb.pop(); cx.pf.pop()


def run_stages(stage_list, x_full, inputs):
    H = host_prep(inputs)
    nc = build_program(stage_list)
    in_maps = []
    for c in range(B):
        m = {"xin": np.ascontiguousarray(x_full[c])}
        m.update(H)
        in_maps.append(m)
    res = run_bass_kernel_spmd(nc, in_maps, core_ids=list(range(B)))
    out = np.stack([res.results[c]["xout"] for c in range(B)])
    if DEBUG_TAPS:
        return out, res.results
    return out


def kernel(**inputs):
    x = np.asarray(inputs["L0_attn_hidden_states"])
    return run_stages(("a", "b", "c", "d"), x, inputs)


# version 04
# speedup vs baseline: 1.1094x; 1.0877x over previous
import numpy as np
from contextlib import ExitStack

import concourse.bass as bass
import concourse.mybir as mybir
from concourse.bass_utils import run_bass_kernel_spmd

F32 = mybir.dt.float32
BF16 = mybir.dt.bfloat16
AF = mybir.ActivationFunctionType
ALU = mybir.AluOpType
AX = mybir.AxisListType

B, T, D = 8, 4096, 1024
NT = T // 128
ND = D // 128
TC = 512
NCH = T // TC
NEG_BIG = -1e30

ENGMAP = {"te": "tensor", "dve": "vector", "act": "scalar", "sp": "sync", "gp": "gpsimd"}



class Prog:
    def __init__(self, nc):
        self.nc = nc
        self.es = ExitStack()
        self.block = None
        self.sems = {}
        self.cnt = {}
        self.sem("flow")

    def sem(self, name):
        if name not in self.sems:
            self.sems[name] = self.es.enter_context(self.nc.semaphore(name))
            self.cnt[name] = 0
        return self.sems[name]

    def sbuf(self, name, shape, dt):
        return self.es.enter_context(self.nc.sbuf_tensor(name, shape, dt))

    def psum(self, name, shape, dt=F32):
        return self.es.enter_context(self.nc.psum_tensor(name, shape, dt))

    def start_block(self):
        self.block = self.es.enter_context(self.nc.Block())

    def close(self):
        self.es.close()

    DCH = ("dsp", "dgp", "dact")

    def _init_sync(self):
        if "dsp" not in self.sems:
            for c in self.DCH:
                self.sem(c)
            self.last_fw = {}
            self.last_dw = {}

    def dsnap(self):
        return {c: self.cnt[c] for c in self.DCH}

    def mark(self):
        self._init_sync()
        return (self.cnt["flow"], self.dsnap())

    def step(self, eng, fn, inc=1, wait_at=None, dml_at=None):
        self._init_sync()
        flow_tgt = self.cnt["flow"] if wait_at is None else wait_at
        dml_tgt = self.dsnap() if dml_at is None else dml_at
        need_flow = flow_tgt > self.last_fw.get(eng, -1)
        if need_flow:
            self.last_fw[eng] = flow_tgt
        need_d = []
        for c in self.DCH:
            tgt = dml_tgt.get(c, 0)
            if tgt > self.last_dw.get((eng, c), -1) and tgt > 0:
                self.last_dw[(eng, c)] = tgt
                need_d.append((c, tgt))
        sem_name = ("d" + ("sp" if eng == "sp" else ("gp" if eng == "gp" else "act"))) \
            if inc == 16 else "flow"

        def body(E):
            if need_flow:
                E.wait_ge(self.sems["flow"], flow_tgt)
            for c, tgt in need_d:
                E.wait_ge(self.sems[c], tgt)
            ins = fn(E)
            assert ins is not None, "step fn must return its last instruction"
            ins.then_inc(self.sems[sem_name], inc)
        getattr(self.block, ENGMAP[eng])(body)
        self.cnt[sem_name] += inc
        return (self.cnt["flow"], self.dsnap())

    def fence(self, E, ins):
        ins.then_inc(self.sems["flow"], 1)
        self.cnt["flow"] += 1
        E.wait_ge(self.sems["flow"], self.cnt["flow"])
        return ins

    def dma_in_step(self, E, out, in_, eng="sp", **kw):
        self._init_sync()
        ch = "d" + ("sp" if eng == "sp" else ("gp" if eng == "gp" else "act"))
        ins = E.dma_start(out=out, in_=in_, **kw)
        ins.then_inc(self.sems[ch], 16)
        self.cnt[ch] += 16
        return ins


class SbAlloc:
    def __init__(self, tensor, total):
        self.t = tensor
        self.total = total
        self.off = 0
        self.marks = []

    def take(self, n):
        self.off = (self.off + 3) & ~3
        assert self.off + n <= self.total, f"sbuf pool overflow {self.off}+{n}>{self.total}"
        ap = self.t[:, self.off:self.off + n]
        self.off += n
        return ap

    def take3(self, a, b):
        return self.take(a * b).rearrange("p (a b) -> p a b", a=a)

    def push(self):
        self.marks.append(self.off)

    def pop(self):
        self.off = self.marks.pop()



def _dft_matrices():
    tau = np.arange(64)
    rows = [np.cos(2 * np.pi * f * tau / 64) for f in range(33)]
    rows += [-np.sin(2 * np.pi * f * tau / 64) for f in range(1, 32)]
    D64 = np.stack(rows).astype(np.float32)
    G = np.zeros((64, 64), np.float32)
    for f in range(33):
        w = 1.0 if f in (0, 32) else 2.0
        G[:, f] = w * np.cos(2 * np.pi * f * tau / 64) / 64.0
    for f in range(1, 32):
        G[:, 32 + f] = -2.0 * np.sin(2 * np.pi * f * tau / 64) / 64.0
    return D64, G


def make_consts():
    c = {}
    c["ident"] = np.eye(128, dtype=np.float32)
    c["ones"] = np.ones((128, 128), np.float32)
    hm = np.zeros((128, 8, 16), np.float32)
    for tile in range(8):
        for p in range(128):
            hm[p, tile, (tile * 128 + p) // 64] = 1.0
    c["headmask"] = hm.reshape(128, 128)
    sl = np.zeros((128, 128), np.float32)
    su = np.zeros((128, 128), np.float32)
    tu = np.zeros((128, 128), np.float32)
    for ch in range(4):
        o = ch * 32
        blk = np.tril(np.ones((32, 32), np.float32), -1)
        sl[o:o + 32, o:o + 32] = blk
        su[o:o + 32, o:o + 32] = blk.T
        tu[o:o + 32, o:o + 32] = np.triu(np.ones((32, 32), np.float32))
    c["m_sl_bd"] = sl
    c["m_su_bd"] = su
    c["m_tu_bd"] = tu
    D64, G64 = _dft_matrices()
    dft_l = np.zeros((128, 128), np.float32)
    idft_l = np.zeros((128, 128), np.float32)
    for g in range(2):
        o = g * 64
        dft_l[o:o + 64, o:o + 64] = D64.T
        idft_l[o:o + 64, o:o + 64] = G64.T
    c["dft_l"] = dft_l
    c["idft_l"] = idft_l
    ft = np.arange(256)
    mlow = (ft[:, None] // 4 >= ft[None, :] // 4).astype(np.float32)
    c["mlow_neg"] = ((1.0 - mlow) * NEG_BIG).astype(np.float32)
    bd = np.kron(np.eye(8, dtype=np.float32), np.ones((16, 16), np.float32))
    c["m_d_neg"] = ((1.0 - bd) * NEG_BIG).astype(np.float32)
    rep = np.zeros((4, 16), np.float32)
    for h in range(16):
        rep[h // 4, h] = 1.0
    c["rep4"] = rep
    return c


def host_prep(I):
    H = {}
    A = lambda x: np.ascontiguousarray(np.asarray(x))
    H["A_wqkv"] = A(np.concatenate([I["L0_attn_Wq"], I["L0_attn_Wk"], I["L0_attn_Wv"]], axis=1))
    H["A_wo"] = A(I["L0_attn_Wo"])
    H["A_wsmall"] = A(np.concatenate([I["L0_attn_Wb"], I["L0_attn_Wdec"], I["L0_attn_Wmix"]], axis=1))
    H["A_convs"] = A(np.stack([I["L0_attn_conv_q"], I["L0_attn_conv_k"], I["L0_attn_conv_v"]]))
    H["A_convmix"] = A(I["L0_attn_conv_mix"])
    H["A_mixbias"] = A(I["L0_attn_mix_bias"]).reshape(1, 4)
    H["A_rmsw"] = A(I["L0_attn_rms_w"]).reshape(1, 256)
    H["B_wiT"] = A(I["L0_ffn_in_proj_w"].T)
    H["B_bi"] = A(I["L0_ffn_in_proj_b"]).reshape(1, 3072)
    H["B_woT"] = A(I["L0_ffn_out_proj_w"].T)
    H["B_bo"] = A(I["L0_ffn_out_proj_b"]).reshape(1, 1024)
    H["B_ln"] = A(np.stack([I["L0_ffn_ln1_g"], I["L0_ffn_ln1_b"], I["L0_ffn_ln2_g"], I["L0_ffn_ln2_b"]]))
    H["B_wge"] = A(np.concatenate([I["L0_ffn_w_group"],
                                   I["L0_ffn_w_expert"].transpose(1, 0, 2).reshape(1024, 8)], axis=1))
    H["B_bge"] = A(np.concatenate([I["L0_ffn_b_group"], I["L0_ffn_b_expert"].reshape(8)])).reshape(1, 12)
    H["B_w1"] = A(I["L0_ffn_w1"])
    H["B_b1"] = A(I["L0_ffn_b1"])
    H["B_w2"] = A(I["L0_ffn_w2"])
    H["B_b2"] = A(I["L0_ffn_b2"])
    H["C_qw"] = A(I["L1_attn_q_w"].T)
    H["C_kvw"] = A(np.concatenate([I["L1_attn_k_w"].T, I["L1_attn_v_w"].T], axis=1))
    H["C_ow"] = A(I["L1_attn_o_w"].T)
    H["C_dw"] = A(np.concatenate([I["L1_attn_dw_r"][:, 0, :], I["L1_attn_dw_i"][:, 0, :]], axis=1))
    H["C_pw"] = A(np.concatenate([I["L1_attn_pw_r"].T, I["L1_attn_pw_i"].T], axis=1))
    H["C_gw"] = A(I["L1_attn_gate_w"].T)
    H["C_alpha"] = A(I["L1_attn_alpha_raw"]).reshape(1, 1)
    H["D_g12"] = A(np.stack([I["L1_ffn_gamma1"], I["L1_ffn_gamma2"]]))
    H["D_wq"] = A(I["L1_ffn_Wq"])
    H["D_bq"] = A(I["L1_ffn_bq"]).reshape(1, 1024)
    H["D_wdown"] = A(I["L1_ffn_Wdown"])
    H["D_bdown"] = A(I["L1_ffn_bdown"]).reshape(1, 64)
    wup = np.asarray(I["L1_ffn_Wup"]).reshape(64, 16, 2, 64)
    H["D_wupk"] = A(wup[:, :, 0, :].reshape(64, 1024))
    H["D_wupv"] = A(wup[:, :, 1, :].reshape(64, 1024))
    bup = np.asarray(I["L1_ffn_bup"]).reshape(16, 2, 64)
    H["D_bupk"] = A(bup[:, 0, :].reshape(1, 1024))
    H["D_bupv"] = A(bup[:, 1, :].reshape(1, 1024))
    H["D_wo"] = A(I["L1_ffn_Wo"])
    H["D_bo"] = A(I["L1_ffn_bo"]).reshape(1, 1024)
    H["D_wg"] = A(I["L1_ffn_Wg"])
    H["D_bg"] = A(I["L1_ffn_bg"]).reshape(1, 8)
    H["D_we"] = A(I["L1_ffn_We"])
    for k, v in make_consts().items():
        H["K_" + k] = v
    return H


def ln_var_ops(p, E, st, eps=1e-5):
    E.tensor_scalar_mul(st[:, 4:5], st[:, 0:1], 1.0 / D)
    E.tensor_scalar_mul(st[:, 2:3], st[:, 1:2], 1.0 / D)
    p.fence(E, E.tensor_mul(st[:, 5:6], st[:, 4:5], st[:, 4:5]))
    p.fence(E, E.tensor_sub(st[:, 2:3], st[:, 2:3], st[:, 5:6]))
    return E.tensor_scalar_add(st[:, 2:3], st[:, 2:3], eps)



class Ctx:
    def __init__(self, p, W):
        self.p = p
        self.W = W
        self.pb = SbAlloc(p.sbuf("poolb", [128, 75776], BF16), 75776)
        self.pf = SbAlloc(p.sbuf("poolf", [128, 9216], F32), 9216)
        self.ident = self.pb.take(128)
        self.ones = self.pb.take(128)
        self.stat = self.pf.take(32)
        ps = p.psum("ps", [128, 4096], F32)
        self.psA = ps[:, 0:2048]
        self.psB = ps[:, 2048:3072]
        self.psC = ps[:, 3072:3584]
        self.psD = ps[:, 3584:4096]
        self.ps = ps

    def load_consts(self):
        p = self.p

        def g(E):
            p.dma_in_step(E, self.ident, self.W["K_ident"][:, :], eng="gp")
            return E.dma_start(out=self.ones, in_=self.W["K_ones"][:, :])
        p.step("gp", g, inc=16)


def bcast_row(cx, src_row_bf, dst_f32, n):
    p = cx.p

    def te(E):
        ins = None
        for ci in range((n + 511) // 512):
            w = min(512, n - ci * 512)
            ins = E.matmul(cx.psD[:, :w], cx.ones[0:1, :], src_row_bf[0:1, ci * 512:ci * 512 + w],
                           start=True, stop=True)
        return ins
    assert n <= 512
    p.step("te", te)

    def dv(E):
        return E.tensor_copy(dst_f32[:, :n], cx.psD[:, :n])
    p.step("dve", dv)


def emit_ln_tm(cx, xin, xout_tm, n_rows=T, gbc=None, bbc=None, rms=False, eps=1e-5,
               res_in=None, res_out=None, prenorm_cb=None, ld_eng="sp"):
    p = cx.p
    cx.pf.push(); cx.pb.push()
    sbx = [cx.pb.take(2048).bitcast(F32) for _ in range(2)]
    sbq = cx.pf.take(1024)
    sbr = [cx.pb.take(2048).bitcast(F32) for _ in range(2)] if res_in is not None else None
    sbo = [cx.pb.take(1024) for _ in range(2)]
    st = cx.stat
    n = n_rows // 128
    comp_done = [None] * n
    store_done = [None] * n
    pass_start = p.mark()
    st_eng = "gp" if ld_eng == "sp" else "sp"
    for i in range(n):
        b = i % 2

        def ld(E, i=i, b=b):
            if res_in is not None:
                p.dma_in_step(E, sbr[b], res_in[i * 128:(i + 1) * 128, :], eng=ld_eng)
            return E.dma_start(out=sbx[b], in_=xin[i * 128:(i + 1) * 128, :])
        wa, da = (pass_start if i < 2 else (comp_done[i - 2][0], store_done[i - 2][1]))
        p.step(ld_eng, ld, inc=16, wait_at=wa, dml_at=da)

        def dv(E, i=i, b=b):
            if res_in is not None:
                E.tensor_add(sbx[b], sbx[b], sbr[b])
            if prenorm_cb is not None:
                prenorm_cb(E, sbx[b], i)
            if rms:
                return E.engine_nop()
            return E.reduce_sum(st[:, 0:1], sbx[b], axis=AX.X)
        p.step("dve", dv)
        if res_out is not None:
            def st_res(E, i=i, b=b):
                return E.dma_start(out=res_out[i * 128:(i + 1) * 128, :], in_=sbx[b])
            p.step(st_eng, st_res, inc=16)

        def sc(E, i=i, b=b):
            return E.scalar_tensor_tensor(sbq, sbx[b], 1.0, sbx[b], ALU.mult, ALU.mult,
                                          accum_out=st[:, 1:2])
        p.step("dve", sc)

        def dv2(E, i=i):
            if rms:
                E.tensor_scalar(st[:, 2:3], st[:, 1:2], 1.0 / D, eps, ALU.mult, ALU.add)
                return E.memset(st[:, 4:5], 0.0)
            return ln_var_ops(p, E, st, eps)
        p.step("dve", dv2)

        def sc2(E, i=i):
            return E.activation(st[:, 3:4], st[:, 2:3], AF.Sqrt)
        p.step("act", sc2)

        def dv3(E, i=i, b=b):
            p.fence(E, E.reciprocal(st[:, 3:4], st[:, 3:4]))
            tgt = sbq if gbc is not None else sbo[b]
            E.tensor_scalar(tgt, sbx[b], st[:, 4:5], st[:, 3:4], ALU.subtract, ALU.mult)
            if gbc is not None:
                E.tensor_mul(sbq, sbq, gbc)
                if bbc is not None:
                    return E.tensor_add(sbo[b], sbq, bbc)
                return E.tensor_copy(sbo[b], sbq)
            return E.engine_nop()
        comp_done[i] = p.step("dve", dv3)

        def stt(E, i=i, b=b):
            return E.dma_start(out=xout_tm[i * 128:(i + 1) * 128, :], in_=sbo[b])
        p.step(st_eng, stt, inc=16)
        store_done[i] = p.mark()
    cx.pf.pop(); cx.pb.pop()


def emit_transpose(cx, src_tm, dst_fm, n_rows, n_cols):
    p = cx.p
    cx.pb.push()
    sbi = cx.pb.take(n_cols)
    sbo = cx.pb.take(n_cols)
    for i in range(n_rows // 128):
        def ld(E, i=i):
            return E.dma_start(out=sbi, in_=src_tm[i * 128:(i + 1) * 128, :])
        p.step("sp", ld, inc=16)
        for h in range(0, n_cols // 128, 4):
            nj = min(4, n_cols // 128 - h)

            def te(E, i=i, h=h, nj=nj):
                ins = None
                for j in range(nj):
                    ins = E.transpose(cx.psA.bitcast(BF16)[:, (j * 1024):(j * 1024) + 128],
                                      sbi[:, (h + j) * 128:(h + j + 1) * 128], cx.ident)
                return ins
            p.step("te", te)

            def dv(E, i=i, h=h, nj=nj):
                ins = None
                for j in range(nj):
                    ins = E.tensor_copy(sbo[:, (h + j) * 128:(h + j + 1) * 128],
                                        cx.psA.bitcast(BF16)[:, (j * 1024):(j * 1024) + 128])
                return ins
            p.step("dve", dv)

        def stt(E, i=i):
            ins = None
            for j in range(n_cols // 128):
                ins = p.dma_in_step(E, dst_fm[j * 128:(j + 1) * 128, i * 128:(i + 1) * 128],
                                    sbo[:, j * 128:(j + 1) * 128])
            return ins
        wait_at = p.cnt["flow"]

        def body(E, stt=stt, wait_at=wait_at):
            E.wait_ge(p.sems["flow"], wait_at)
            stt(E)
        p.block.sync(body)
    cx.pb.pop()


def emit_fm_matmul(cx, x_fm, w_dram, y_fm, din, dout, bias_dram=None, bias_off=0,
                   func=AF.Identity, w_sb=None, out_f32_cb=None):
    p = cx.p
    cx.pb.push()
    ndt, nft = din // 128, dout // 128
    if w_sb is None:
        w_sb = cx.pb.take3(ndt, dout)

        def lw(E):
            return E.dma_start(out=w_sb, in_=w_dram.rearrange("(c p) f -> p c f", p=128))
        p.step("gp", lw, inc=16)
    bsb = None
    if bias_dram is not None:
        bsb = cx.pf.take(nft)

        def lb(E):
            return E.dma_start(out=bsb, in_=bias_dram[0:1, bias_off:bias_off + dout]
                               .rearrange("o (c p) -> p (o c)", p=128),
                               allow_slow_non_contiguous=True)
        p.step("sp", lb, inc=16)
    xc = [cx.pb.take3(ndt, TC) for _ in range(2)]
    yc = [cx.pb.take3(nft, TC) for _ in range(2)]
    pass_start = p.mark()
    ev_done = []
    ld_done = [None] * NCH
    ch_done = [None] * NCH
    for ch in range(NCH):
        b = ch % 2

        def lx(E, ch=ch, b=b):
            return E.dma_start(out=xc[b], in_=x_fm.rearrange("(c p) t -> p c t", p=128)[:, :, ch * TC:(ch + 1) * TC])
        wa, da = (pass_start if ch < 2 else (ch_done[ch - 2]))
        ld_done[ch] = p.step("sp", lx, inc=16, wait_at=wa, dml_at=da)
        for f in range(nft):
            fi = ch * nft + f

            def te(E, f=f, ch=ch, b=b):
                ins = None
                for d in range(ndt):
                    ins = E.matmul(cx.psA[:, (f % 4) * 512:(f % 4) * 512 + 512],
                                   w_sb[:, d, f * 128:(f + 1) * 128], xc[b][:, d, :],
                                   start=(d == 0), stop=(d == ndt - 1))
                return ins
            twa = pass_start[0] if fi < 4 else ev_done[fi - 4][0]
            p.step("te", te, wait_at=twa, dml_at=ld_done[ch][1])

            def ev(E, f=f, ch=ch, b=b):
                pa = cx.psA[:, (f % 4) * 512:(f % 4) * 512 + 512]
                if func == AF.Identity:
                    if bsb is not None:
                        return E.tensor_scalar_add(yc[b][:, f, :], pa, bsb[:, f:f + 1])
                    return E.tensor_copy(yc[b][:, f, :], pa)
                if bsb is not None:
                    return E.activation(yc[b][:, f, :], pa, func, bias=bsb[:, f:f + 1])
                return E.activation(yc[b][:, f, :], pa, func)
            eng_ev = "dve" if func == AF.Identity else "act"
            ev_done.append(p.step(eng_ev, ev, dml_at=ld_done[ch][1]))

        def st(E, ch=ch, b=b):
            return E.dma_start(out=y_fm.rearrange("(c p) t -> p c t", p=128)[:, :, ch * TC:(ch + 1) * TC],
                               in_=yc[b])
        ch_done[ch] = p.step("gp", st, inc=16)
    cx.pb.pop()


def emit_row_bcast(cx, src_row_ap, dst_f32, n):
    p = cx.p

    def lr(E):
        return E.dma_start(out=cx.rowbf[0:1, 0:n], in_=src_row_ap)
    p.step("gp", lr, inc=16)
    for c0 in range(0, n, 512):
        w = min(512, n - c0)

        def te(E, c0=c0, w=w):
            return E.matmul(cx.psD[:, 0:w], cx.ones[0:1, :], cx.rowbf[0:1, c0:c0 + w],
                            start=True, stop=True)
        p.step("te", te)

        def dv(E, c0=c0, w=w):
            return E.tensor_copy(dst_f32[:, c0:c0 + w], cx.psD[:, 0:w])
        p.step("dve", dv)



def emit_stage_b(cx, xin, xout, scr):
    p = cx.p
    W = cx.W
    d_tm0, d_fm0, d_fm1, d_fm2 = scr["tm0"], scr["fm0"], scr["fm1"], scr["fm2"]
    d_tm1, d_fm3, d_tm2, d_fm4 = scr["tm1"], scr["fm3"], scr["tm2"], scr["fm4"]
    d_c, d_cT = scr["c32"], scr["cT"]

    emit_ln_tm(cx, xin, d_tm0)
    emit_transpose(cx, d_tm0, d_fm0, T, D)
    cx.pb.push()
    wqk_sb = cx.pb.take3(ND, 2048)

    def lw(E):
        return E.dma_start(out=wqk_sb, in_=W["B_wiT"].rearrange("(c p) f -> p c f", p=128)[:, :, 0:2048])
    p.step("gp", lw, inc=16)
    emit_fm_matmul(cx, d_fm0, None, d_fm1, D, D, bias_dram=W["B_bi"], bias_off=0,
                   w_sb=wqk_sb[:, :, 0:1024])
    emit_fm_matmul(cx, d_fm0, None, d_fm2, D, D, w_sb=wqk_sb[:, :, 1024:2048])
    cx.pb.pop()
    cx.pb.push()
    wv_sb = cx.pb.take3(ND, 1024)

    def lwv(E):
        return E.dma_start(out=wv_sb, in_=W["B_wiT"].rearrange("(c p) f -> p c f", p=128)[:, :, 2048:3072])
    p.step("gp", lwv, inc=16)
    xc = cx.pb.take3(ND, TC)
    vo = cx.pb.take(1024)
    for ch in range(NCH):
        def lx(E, ch=ch):
            return E.dma_start(out=xc, in_=d_fm0.rearrange("(c p) t -> p c t", p=128)[:, :, ch * TC:(ch + 1) * TC])
        p.step("sp", lx, inc=16)
        for tt in range(4):
            def te(E, ch=ch, tt=tt):
                ins = None
                for fc in range(2):
                    for d in range(ND):
                        ins = E.matmul(cx.psA[:, fc * 512:fc * 512 + 512],
                                       xc[:, d, tt * 128:(tt + 1) * 128],
                                       wv_sb[:, d, fc * 512:(fc + 1) * 512],
                                       start=(d == 0), stop=(d == ND - 1))
                return ins
            p.step("te", te)

            def ev(E, ch=ch, tt=tt):
                E.tensor_copy(vo[:, 0:512], cx.psA[:, 0:512])
                return E.tensor_copy(vo[:, 512:1024], cx.psA[:, 512:1024])
            p.step("dve", ev)

            def st(E, ch=ch, tt=tt):
                r = ch * 512 + tt * 128
                return E.dma_start(out=d_tm1[r:r + 128, :], in_=vo)
            p.step("sp", st, inc=16)
    cx.pb.pop()

    cx.pb.push(); cx.pf.push()
    hm_sb = cx.pb.take(128)

    def lhm(E):
        return E.dma_start(out=hm_sb, in_=W["K_headmask"][:, :])
    p.step("gp", lhm, inc=16)
    sq = cx.pf.take(512)
    nrm = cx.pf.take(16)
    negc_bc = cx.negc_bc
    xc2 = cx.pb.take3(ND, TC)
    for which, src in ((0, d_fm1), (1, d_fm2)):
        for ch in range(NCH):
            def lx(E, ch=ch, src=src):
                return E.dma_start(out=xc2, in_=src.rearrange("(c p) t -> p c t", p=128)[:, :, ch * TC:(ch + 1) * TC])
            p.step("sp", lx, inc=16)
            for d in range(ND):
                def sc_fix(E, ch=ch, d=d):
                    return E.tensor_mul(sq, xc2[:, d, :], xc2[:, d, :])
                p.step("dve", sc_fix)

                def dv_cast(E, ch=ch, d=d):
                    return E.tensor_copy(cx.sq_bf, sq)
                p.step("dve", dv_cast)

                def te(E, ch=ch, d=d):
                    return E.matmul(cx.psC[0:16, :], hm_sb[:, d * 16:(d + 1) * 16],
                                    cx.sq_bf, start=(d == 0), stop=(d == ND - 1))
                p.step("te", te)

            def dv(E, ch=ch, which=which):
                p.fence(E, E.reduce_max(cx.stat[0:16, 8:9], cx.psC[0:16, :], axis=AX.X))
                if ch == 0:
                    return E.tensor_copy(nrm[0:16, which:which + 1], cx.stat[0:16, 8:9])
                return E.tensor_max(nrm[0:16, which:which + 1], nrm[0:16, which:which + 1],
                                    cx.stat[0:16, 8:9])
            p.step("dve", dv)

    def sc3(E):
        E.activation(nrm[0:16, 2:3], nrm[0:16, 0:1], AF.Sqrt)
        return E.activation(nrm[0:16, 3:4], nrm[0:16, 1:2], AF.Sqrt)
    p.step("act", sc3)

    def dv4(E):
        p.fence(E, E.tensor_mul(nrm[0:16, 4:5], nrm[0:16, 2:3], nrm[0:16, 3:4]))
        p.fence(E, E.tensor_scalar_mul(nrm[0:16, 4:5], nrm[0:16, 4:5], -1.0 / 8.0))
        p.fence(E, E.memset(cx.negc_bf, 0.0))
        return E.tensor_copy(cx.negc_bf[0:16, 0:1], nrm[0:16, 4:5])
    p.step("dve", dv4)

    def te4(E):
        return E.transpose(cx.psD.bitcast(BF16)[0:1, 0:128], cx.negc_bf, cx.ident)
    p.step("te", te4)

    def dv5(E):
        return E.tensor_copy(cx.negcT_bf[0:1, 0:16], cx.psD.bitcast(BF16)[0:1, 0:16])
    p.step("dve", dv5)

    def te5(E):
        return E.matmul(cx.psD[:, 0:16], cx.ones[0:1, :], cx.negcT_bf[0:1, 0:16], start=True, stop=True)
    p.step("te", te5)

    def dv6(E):
        return E.tensor_copy(negc_bc, cx.psD[:, 0:16])
    p.step("dve", dv6)
    cx.pb.pop(); cx.pf.pop()

    cx.pb.push()
    qh = [cx.pb.take(4096) for _ in range(2)]
    kh = [cx.pb.take(4096) for _ in range(2)]
    vaug = [cx.pb.take(NT * 2 * 66).rearrange("p (t h e) -> p t h e", t=NT, h=2) for _ in range(2)]
    es = cx.pb.take(2048)
    ob = cx.pb.take(512)
    bvs = cx.pf.take(8)

    def lbv(E):
        return E.dma_start(out=bvs, in_=W["B_bi"][0:1, 2048:3072].rearrange("o (c p) -> p (o c)", p=128),
                           allow_slow_non_contiguous=True)
    p.step("sp", lbv, inc=16)
    b4_start = p.mark()
    hp_done = [None] * 8
    for hp in range(8):
        hb = hp % 2

        def mset(E, hp=hp, hb=hb):
            return E.memset(vaug[hb], 1.0)
        mset_mark = p.step("dve", mset, wait_at=(b4_start[0] if hp < 2 else hp_done[hp - 2][0]),
                           dml_at=b4_start[1])

        def lqk(E, hp=hp, hb=hb):
            p.dma_in_step(E, qh[hb], d_fm1[hp * 128:(hp + 1) * 128, :])
            p.dma_in_step(E, kh[hb], d_fm2[hp * 128:(hp + 1) * 128, :])
            src = d_tm1.rearrange("(t p) (h e) -> p t h e", p=128, h=16)
            p.dma_in_step(E, vaug[hb][:, :, 0, 0:64], src[:, :, 2 * hp, :])
            return E.dma_start(out=vaug[hb][:, :, 1, 0:64], in_=src[:, :, 2 * hp + 1, :])
        ld_hp = p.step("sp", lqk, inc=16, wait_at=mset_mark[0], dml_at=b4_start[1])
        for h01 in range(2):
            h = 2 * hp + h01
            qb = 64 * h01
            for qi in range(8):
                exp_done = []
                for pr in range(17):
                    if pr < 16:
                        def te(E, h01=h01, qi=qi, pr=pr, qb=qb, hb=hb):
                            ins = None
                            for j in range(2):
                                kt = pr * 2 + j
                                ins = E.matmul(cx.psA[:, (pr % 2) * 1024 + j * 512:(pr % 2) * 1024 + (j + 1) * 512],
                                               kh[hb][qb:qb + 64, kt * 128:(kt + 1) * 128],
                                               qh[hb][qb:qb + 64, qi * 512:(qi + 1) * 512],
                                               start=True, stop=True)
                            return ins
                        p.step("te", te, wait_at=(exp_done[pr - PIPE_DEPTH][0] if pr >= PIPE_DEPTH else None) if PIPE_B4B else None,
                               dml_at=ld_hp[1] if PIPE_B4B else None)

                        def sc(E, h=h, pr=pr):
                            return E.activation(es[:, (pr % 2) * 1024:(pr % 2 + 1) * 1024],
                                                cx.psA[:, (pr % 2) * 1024:(pr % 2 + 1) * 1024],
                                                AF.Exp, bias=negc_bc[:, h:h + 1], scale=0.125)
                        exp_done.append(p.step("act", sc, dml_at=ld_hp[1]))
                    if pr >= 1:
                        pp = pr - 1

                        def te2(E, h01=h01, qi=qi, pp=pp, hb=hb):
                            ins = None
                            for j in range(2):
                                kt = pp * 2 + j
                                ins = E.matmul(cx.psB[0:65, 0:512],
                                               vaug[hb][:, kt, h01, 0:65],
                                               es[:, (pp % 2) * 1024 + j * 512:(pp % 2) * 1024 + (j + 1) * 512],
                                               start=(pp == 0 and j == 0), stop=(pp == 15 and j == 1))
                            return ins
                        p.step("te", te2, wait_at=exp_done[pp][0] if (PIPE_B4B and PIPE_TE2) else None,
                               dml_at=ld_hp[1] if PIPE_B4B else None)
                def dv_b(E):
                    p.fence(E, E.reciprocal(cx.rd32[0:1, :], cx.psB[64:65, 0:512]))
                    return E.tensor_copy(cx.rd_bf[0:1, :], cx.rd32[0:1, :])
                p.step("dve", dv_b)

                def te3(E):
                    return E.matmul(cx.psC[0:64, 0:512], cx.ones[0:1, 0:64], cx.rd_bf[0:1, 0:512],
                                    start=True, stop=True)
                p.step("te", te3)

                def dv_c(E):
                    return E.tensor_copy(cx.rdb32[0:64, :], cx.psC[0:64, 0:512])
                p.step("dve", dv_c)

                def dv_d(E, h01=h01, qb=qb):
                    return E.tensor_mul(cx.o32[0:64, :], cx.psB[0:64, 0:512], cx.rdb32[0:64, :])
                p.step("dve", dv_d)

                def sc2(E, h=h, h01=h01, qb=qb):
                    return E.tensor_scalar_add(ob[0:64, :], cx.o32[0:64, :],
                                               bvs[qb:qb + 64, (h // 2):(h // 2) + 1])
                p.step("dve", sc2)

                def st(E, h=h, qi=qi):
                    return E.dma_start(out=d_fm3[h * 64:(h + 1) * 64, qi * 512:(qi + 1) * 512],
                                       in_=ob[0:64, :])
                p.step("gp", st, inc=16)
        hp_done[hp] = p.mark()
    cx.pb.pop()

    cx.pb.push(); cx.pf.push()
    wo_sb = cx.pb.take3(ND, 1024)

    def lwo(E):
        return E.dma_start(out=wo_sb, in_=W["B_woT"].rearrange("(c p) f -> p c f", p=128))
    p.step("gp", lwo, inc=16)
    rowbf = cx.pb.take(1024)
    cb_bo = cx.pf.take(1024)
    cb_g1 = cx.pf.take(1024)
    cb_b1 = cx.pf.take(1024)
    for dst, srcap in ((cb_bo, W["B_bo"][0:1, :]), (cb_g1, W["B_ln"][0:1, :]), (cb_b1, W["B_ln"][1:2, :])):
        def lr(E, srcap=srcap):
            return E.dma_start(out=rowbf[0:1, :], in_=srcap)
        p.step("gp", lr, inc=16)
        for half in range(2):
            def te(E, half=half):
                return E.matmul(cx.psD[:, 0:512], cx.ones[0:1, :], rowbf[0:1, half * 512:(half + 1) * 512],
                                start=True, stop=True)
            p.step("te", te)

            def dv(E, dst=dst, half=half):
                return E.tensor_copy(dst[:, half * 512:(half + 1) * 512], cx.psD[:, 0:512])
            p.step("dve", dv)
    oc = cx.pb.take3(ND, TC)
    xb_t = cx.pb.take(1024)
    y_t = cx.pb.take(1024)
    for ch in range(NCH):
        def lo(E, ch=ch):
            return E.dma_start(out=oc, in_=d_fm3.rearrange("(c p) t -> p c t", p=128)[:, :, ch * TC:(ch + 1) * TC])
        p.step("sp", lo, inc=16)
        for tt in range(4):
            i = ch * 4 + tt

            def te(E, ch=ch, tt=tt):
                ins = None
                for fc in range(2):
                    for d in range(ND):
                        ins = E.matmul(cx.psA[:, fc * 512:fc * 512 + 512],
                                       oc[:, d, tt * 128:(tt + 1) * 128],
                                       wo_sb[:, d, fc * 512:(fc + 1) * 512],
                                       start=(d == 0), stop=(d == ND - 1))
                return ins
            p.step("te", te)

            def lxb(E, i=i):
                return E.dma_start(out=xb_t, in_=d_tm0[i * 128:(i + 1) * 128, :])
            p.step("sp", lxb, inc=16)

            def dv(E, i=i):
                E.tensor_copy(cx.x32a, xb_t)
                E.tensor_add(cx.x32a, cx.x32a, cb_bo)
                E.tensor_add(cx.x32a[:, 0:512], cx.x32a[:, 0:512], cx.psA[:, 0:512])
                E.tensor_add(cx.x32a[:, 512:1024], cx.x32a[:, 512:1024], cx.psA[:, 512:1024])
                return E.reduce_sum(cx.stat[:, 0:1], cx.x32a, axis=AX.X)
            p.step("dve", dv)

            def sc(E, i=i):
                return E.scalar_tensor_tensor(cx.x32b, cx.x32a, 1.0, cx.x32a, ALU.mult, ALU.mult,
                                              accum_out=cx.stat[:, 1:2])
            p.step("dve", sc)

            def dv2(E, i=i):
                return ln_var_ops(p, E, cx.stat)
            p.step("dve", dv2)

            def sc2(E, i=i):
                return E.activation(cx.stat[:, 3:4], cx.stat[:, 2:3], AF.Sqrt)
            p.step("act", sc2)

            def dv3(E, i=i):
                p.fence(E, E.reciprocal(cx.stat[:, 3:4], cx.stat[:, 3:4]))
                E.tensor_scalar(cx.x32b, cx.x32a, cx.stat[:, 4:5], cx.stat[:, 3:4],
                                ALU.subtract, ALU.mult)
                E.tensor_mul(cx.x32b, cx.x32b, cb_g1)
                E.tensor_add(cx.x32b, cx.x32b, cb_b1)
                return E.tensor_copy(y_t, cx.x32b)
            p.step("dve", dv3)

            def sty(E, i=i):
                return E.dma_start(out=d_tm2[i * 128:(i + 1) * 128, :], in_=y_t)
            p.step("sp", sty, inc=16)
    cx.pb.pop(); cx.pf.pop()

    emit_transpose(cx, d_tm2, d_fm4, T, D)

    cx.pb.push(); cx.pf.push()
    wge_sb = cx.pb.take3(ND, 12)

    def lwge(E):
        return E.dma_start(out=wge_sb, in_=W["B_wge"].rearrange("(c p) f -> p c f", p=128))
    p.step("gp", lwge, inc=16)
    rowb2 = cx.pb.take(16)

    def lbge(E):
        return E.dma_start(out=rowb2[0:1, 0:12], in_=W["B_bge"][0:1, :])
    p.step("gp", lbge, inc=16)
    cbge = cx.pf.take(12)

    def te(E):
        return E.matmul(cx.psD[:, 0:12], cx.ones[0:1, :], rowb2[0:1, 0:12], start=True, stop=True)
    p.step("te", te)

    def dvge(E):
        return E.tensor_copy(cbge, cx.psD[:, 0:12])
    p.step("dve", dvge)
    yc2 = cx.pb.take3(ND, TC)
    r32 = cx.pf.take(12)
    w32 = cx.pf.take(32)
    cmb_bf = cx.pb.take(8)
    for ch in range(NCH):
        def ly(E, ch=ch):
            return E.dma_start(out=yc2, in_=d_fm4.rearrange("(c p) t -> p c t", p=128)[:, :, ch * TC:(ch + 1) * TC])
        p.step("sp", ly, inc=16)
        for tt in range(4):
            i = ch * 4 + tt

            def te2(E, ch=ch, tt=tt):
                ins = None
                for d in range(ND):
                    ins = E.matmul(cx.psC[:, 0:12], yc2[:, d, tt * 128:(tt + 1) * 128],
                                   wge_sb[:, d, :], start=(d == 0), stop=(d == ND - 1))
                return ins
            p.step("te", te2)

            def dvr(E, i=i):
                p.fence(E, E.tensor_add(r32, cx.psC[:, 0:12], cbge))
                lg = r32[:, 0:4]
                return E.reduce_max(cx.stat[:, 8:9], lg, axis=AX.X, negate=True)
            p.step("dve", dvr)

            def scr_(E, i=i):
                return E.activation(w32[:, 0:4], r32[:, 0:4], AF.Exp,
                                    bias=cx.stat[:, 8:9], accum_out=cx.stat[:, 9:10])
            p.step("act", scr_)

            def dvr2(E, i=i):
                st_ = cx.stat
                E.reciprocal(st_[:, 10:11], st_[:, 9:10])
                p.fence(E, E.tensor_scalar_mul(st_[:, 11:12], st_[:, 8:9], -1.0))
                lg = r32[:, 0:4]
                ohg = w32[:, 4:8]
                p.fence(E, E.tensor_tensor(ohg, lg, st_[:, 11:12].broadcast_to([128, 4]), ALU.is_ge))
                pref = w32[:, 8:12]
                p.fence(E, E.tensor_copy(pref[:, 0:1], ohg[:, 0:1]))
                p.fence(E, E.tensor_tensor(pref[:, 1:2], pref[:, 0:1], ohg[:, 1:2], ALU.max))
                p.fence(E, E.tensor_tensor(pref[:, 2:3], pref[:, 1:2], ohg[:, 2:3], ALU.max))
                inv = w32[:, 12:16]
                p.fence(E, E.tensor_scalar(inv[:, 0:3], pref[:, 0:3], -1.0, 1.0, ALU.mult, ALU.add))
                p.fence(E, E.tensor_mul(ohg[:, 1:4], ohg[:, 1:4], inv[:, 0:3]))
                le = r32[:, 4:12].rearrange("p (g e) -> p g e", e=2)
                p.fence(E, E.reduce_max(w32[:, 16:20], le, axis=AX.X, negate=True))
                les = w32[:, 20:28].rearrange("p (g e) -> p g e", e=2)
                return E.tensor_tensor(les, le, w32[:, 16:20][:, :, None].broadcast_to([128, 4, 2]), ALU.add)
            p.step("dve", dvr2)

            def scr2(E, i=i):
                return E.activation(w32[:, 20:28], w32[:, 20:28], AF.Exp)
            p.step("act", scr2)

            def dvr3(E, i=i):
                st_ = cx.stat
                ee = w32[:, 20:28].rearrange("p (g e) -> p g e", e=2)
                p.fence(E, E.reduce_sum(w32[:, 16:20], ee, axis=AX.X))
                p.fence(E, E.reciprocal(w32[:, 16:20], w32[:, 16:20]))
                p.fence(E, E.tensor_tensor(ee, ee, w32[:, 16:20][:, :, None].broadcast_to([128, 4, 2]), ALU.mult))
                ohg = w32[:, 4:8]
                p.fence(E, E.tensor_tensor(ee, ee, ohg[:, :, None].broadcast_to([128, 4, 2]), ALU.mult))
                sel = w32[:, 28:30]
                p.fence(E, E.tensor_reduce(sel, w32[:, 20:28].rearrange("p (g e) -> p e g", e=2), AX.X, ALU.add))
                p.fence(E, E.reduce_max(st_[:, 12:13], sel, axis=AX.X))
                ohe = w32[:, 30:32]
                p.fence(E, E.tensor_tensor(ohe, sel, st_[:, 12:13].broadcast_to([128, 2]), ALU.is_ge))
                p.fence(E, E.tensor_scalar(st_[:, 13:14], ohe[:, 0:1], -1.0, 1.0, ALU.mult, ALU.add))
                p.fence(E, E.tensor_mul(ohe[:, 1:2], ohe[:, 1:2], st_[:, 13:14]))
                p.fence(E, E.tensor_mul(st_[:, 14:15], st_[:, 10:11], st_[:, 12:13]))
                cmb = w32[:, 16:24].rearrange("p (g e) -> p g e", e=2)
                p.fence(E, E.tensor_tensor(cmb, w32[:, 4:8][:, :, None].broadcast_to([128, 4, 2]),
                                ohe[:, None, :].broadcast_to([128, 4, 2]), ALU.mult))
                p.fence(E, E.tensor_single_scalar(w32[:, 16:24], w32[:, 16:24], st_[:, 14:15], ALU.mult))
                return E.tensor_copy(cmb_bf, w32[:, 16:24])
            p.step("dve", dvr3)

            def stc(E, i=i):
                return E.dma_start(out=d_c[i * 128:(i + 1) * 128, 0:8], in_=w32[:, 16:24])
            p.step("sp", stc, inc=16)

            def te3(E, i=i):
                return E.transpose(cx.psD.bitcast(BF16)[0:8, 0:128], cmb_bf, cx.ident)
            p.step("te", te3)

            def dvr4(E, i=i):
                return E.tensor_copy(cx.ctrow_bf[0:8, 0:128], cx.psD.bitcast(BF16)[0:8, 0:128])
            p.step("dve", dvr4)

            def stct(E, i=i):
                return E.dma_start(out=d_cT[0:8, i * 128:(i + 1) * 128], in_=cx.ctrow_bf[0:8, 0:128])
            p.step("sp", stct, inc=16)
    cx.pb.pop(); cx.pf.pop()

    cx.pb.push(); cx.pf.push()
    wa = cx.pb.take3(ND, 2048)
    wb = cx.pb.take3(16, 1024)
    macc = cx.pb.take(16 * 1024).rearrange("p (s f) -> p s f", s=16)
    hsb = cx.pb.take3(16, 512)
    xc3 = cx.pb.take3(ND, TC)
    ctf = cx.pb.take(4096)
    b1s = cx.pf.take(16)
    csb = cx.pf.take(256).rearrange("p (s e) -> p s e", s=16)
    b2s = cx.pb.take(1024)

    def lct(E):
        return E.dma_start(out=ctf[0:8, :], in_=d_cT[0:8, :])
    p.step("sp", lct, inc=16)

    def lb2(E):
        return E.dma_start(out=b2s[0:8, :], in_=W["B_b2"][:, :])
    p.step("gp", lb2, inc=16)
    xc3b = [xc3, cx.pb.take3(ND, TC)]
    zbanks = [(cx.psB[:, 0:512], cx.psB[:, 512:1024]), (cx.psC, cx.psD)]
    for thalf in range(2):
        def lc(E, thalf=thalf):
            return E.dma_start(out=csb,
                               in_=W_dc_rearr(d_c, thalf))
        p.step("sp", lc, inc=16)
        for e in range(8):
            def lw1(E, e=e):
                p.dma_in_step(E, wa, W["B_w1"][e].rearrange("(c p) f -> p c f", p=128), eng="gp")
                p.dma_in_step(E, wb, W["B_w2"][e].rearrange("(c p) f -> p c f", p=128), eng="gp")
                return E.dma_start(out=b1s, in_=W["B_b1"][e:e + 1, :].rearrange("o (c p) -> p (o c)", p=128),
                                   allow_slow_non_contiguous=True)
            p.step("gp", lw1, inc=16)
            ev_done = []
            teh_done = [None] * 4
            ld_done = [None] * 4
            dz_done = []
            pass_start = p.mark()
            for ch in range(4):
                cb = ch % 2

                def lx3(E, thalf=thalf, ch=ch, cb=cb):
                    cc = thalf * 4 + ch
                    return E.dma_start(out=xc3b[cb], in_=d_fm4.rearrange("(c p) t -> p c t", p=128)[:, :, cc * TC:(cc + 1) * TC])
                wa_ = (pass_start[0] if ch < 2 else teh_done[ch - 2][0]) if PIPE_B8 else None
                ld_done[ch] = p.step("sp", lx3, inc=16, wait_at=wa_)
                for f in range(16):
                    fi = ch * 16 + f

                    def te(E, f=f, cb=cb):
                        ins = None
                        for d in range(ND):
                            ins = E.matmul(cx.psA[:, (f % 4) * 512:(f % 4) * 512 + 512],
                                           wa[:, d, f * 128:(f + 1) * 128], xc3b[cb][:, d, :],
                                           start=(d == 0), stop=(d == ND - 1))
                        return ins
                    _d8 = 4 if PIPE_DEPTH == 2 else 1
                    twa = (pass_start[0] if fi < _d8 else ev_done[fi - _d8][0]) if PIPE_B8 else None
                    teh_done[ch] = p.step("te", te, wait_at=twa, dml_at=ld_done[ch][1] if PIPE_B8 else None)

                    def ev(E, f=f):
                        return E.tensor_scalar(hsb[:, f, :], cx.psA[:, (f % 4) * 512:(f % 4) * 512 + 512],
                                               b1s[:, f:f + 1], 0.0, ALU.add, ALU.max)
                    ev_done.append(p.step("dve", ev, dml_at=ld_done[ch][1]))
                for tt in range(4):
                    ts_local = ch * 4 + tt
                    tglob = thalf * 2048 + ts_local * 128
                    zi = ch * 4 + tt
                    zb = zbanks[zi % 2]

                    def te2(E, e=e, tt=tt, tglob=tglob, zb=zb):
                        ins = None
                        for dc in range(2):
                            first = True
                            if e == 0:
                                ins = E.matmul(zb[dc],
                                               ctf[0:8, tglob:tglob + 128],
                                               b2s[0:8, dc * 512:(dc + 1) * 512],
                                               start=True, stop=False)
                                first = False
                            for f in range(16):
                                ins = E.matmul(zb[dc],
                                               hsb[:, f, tt * 128:(tt + 1) * 128],
                                               wb[:, f, dc * 512:(dc + 1) * 512],
                                               start=first, stop=(f == 15))
                                first = False
                        return ins
                    _z8 = 2 if PIPE_DEPTH == 2 else 1
                    zwa = (ev_done[-1][0] if zi < _z8 else max(ev_done[-1][0], dz_done[zi - _z8][0])) if PIPE_B8 else None
                    p.step("te", te2, wait_at=zwa, dml_at=ld_done[ch][1] if PIPE_B8 else None)

                    def dv(E, e=e, ts_local=ts_local, zb=zb):
                        cs = csb[:, ts_local, e:e + 1]
                        m0 = macc[:, ts_local, 0:512]
                        m1 = macc[:, ts_local, 512:1024]
                        if e == 0:
                            E.tensor_single_scalar(m0, zb[0], cs, ALU.mult)
                            return E.tensor_single_scalar(m1, zb[1], cs, ALU.mult)
                        E.scalar_tensor_tensor(m0, zb[0], cs, m0, ALU.mult, ALU.add)
                        return E.scalar_tensor_tensor(m1, zb[1], cs, m1, ALU.mult, ALU.add)
                    dz_done.append(p.step("dve", dv, dml_at=ld_done[ch][1]))
        cb_g2 = cx.pf.take(1024) if thalf == 0 else cx.cb_g2
        cb_b2 = cx.pf.take(1024) if thalf == 0 else cx.cb_b2
        if thalf == 0:
            cx.cb_g2, cx.cb_b2 = cb_g2, cb_b2
            rowb3 = cx.pb.take(1024)
            for dst, srcap in ((cb_g2, W["B_ln"][2:3, :]), (cb_b2, W["B_ln"][3:4, :])):
                def lr(E, srcap=srcap):
                    return E.dma_start(out=rowb3[0:1, :], in_=srcap)
                p.step("gp", lr, inc=16)
                for half in range(2):
                    def te_(E, half=half):
                        return E.matmul(cx.psD[:, 0:512], cx.ones[0:1, :],
                                        rowb3[0:1, half * 512:(half + 1) * 512], start=True, stop=True)
                    p.step("te", te_)

                    def dv_(E, dst=dst, half=half):
                        return E.tensor_copy(dst[:, half * 512:(half + 1) * 512], cx.psD[:, 0:512])
                    p.step("dve", dv_)
        yt2 = cx.pb.take(1024) if thalf == 0 else cx.yt2
        if thalf == 0:
            cx.yt2 = yt2
        for s in range(16):
            i = thalf * 16 + s

            def ly2(E, i=i):
                return E.dma_start(out=yt2, in_=d_tm2[i * 128:(i + 1) * 128, :])
            p.step("sp", ly2, inc=16)

            def lx1(E, i=i):
                return E.dma_start(out=cx.x32b, in_=xin[i * 128:(i + 1) * 128, :])
            p.step("sp", lx1, inc=16)

            def dvz(E, i=i, s=s):
                E.tensor_copy(cx.x32a, yt2)
                E.tensor_add(cx.x32a, cx.x32a, macc[:, s, :])
                return E.reduce_sum(cx.stat[:, 0:1], cx.x32a, axis=AX.X)
            p.step("dve", dvz)

            def scz(E, i=i):
                return E.scalar_tensor_tensor(cx.x32c, cx.x32a, 1.0, cx.x32a, ALU.mult, ALU.mult,
                                              accum_out=cx.stat[:, 1:2])
            p.step("dve", scz)

            def dvz2(E, i=i):
                return ln_var_ops(p, E, cx.stat)
            p.step("dve", dvz2)

            def scz2(E, i=i):
                return E.activation(cx.stat[:, 3:4], cx.stat[:, 2:3], AF.Sqrt)
            p.step("act", scz2)

            def dvz3(E, i=i):
                st_ = cx.stat
                p.fence(E, E.reciprocal(st_[:, 3:4], st_[:, 3:4]))
                E.tensor_scalar(cx.x32c, cx.x32a, st_[:, 4:5], st_[:, 3:4], ALU.subtract, ALU.mult)
                E.tensor_mul(cx.x32c, cx.x32c, cb_g2)
                E.tensor_add(cx.x32c, cx.x32c, cb_b2)
                return E.reduce_sum(st_[:, 0:1], cx.x32c, axis=AX.X)
            p.step("dve", dvz3)

            def scz3(E, i=i):
                return E.scalar_tensor_tensor(cx.x32a, cx.x32c, 1.0, cx.x32c, ALU.mult, ALU.mult,
                                              accum_out=cx.stat[:, 1:2])
            p.step("dve", scz3)

            def dvz4(E, i=i):
                return ln_var_ops(p, E, cx.stat)
            p.step("dve", dvz4)

            def scz4(E, i=i):
                return E.activation(cx.stat[:, 3:4], cx.stat[:, 2:3], AF.Sqrt)
            p.step("act", scz4)

            def dvz5(E, i=i):
                st_ = cx.stat
                p.fence(E, E.reciprocal(st_[:, 3:4], st_[:, 3:4]))
                E.tensor_scalar(cx.x32a, cx.x32c, st_[:, 4:5], st_[:, 3:4], ALU.subtract, ALU.mult)
                return E.tensor_add(cx.x32b, cx.x32b, cx.x32a)
            p.step("dve", dvz5)

            def stx(E, i=i):
                return E.dma_start(out=xout[i * 128:(i + 1) * 128, :], in_=cx.x32b)
            p.step("sp", stx, inc=16)
    cx.pb.pop(); cx.pf.pop()


def W_dc_rearr(d_c, thalf):
    return d_c[thalf * 2048:(thalf + 1) * 2048, 0:16].rearrange("(s p) e -> p s e", p=128)



STAGES = ("a", "b", "c", "d")
DEBUG_TAPS = []
DEBUG_A8 = False
PIPE_B4B = True
PIPE_B8 = True
PIPE_DEPTH = 1
PIPE_TE2 = False


def build_program(stages):
    nc = bass.Bass("TRN2", target_bir_lowering=False)
    W = {}
    xin = nc.dram_tensor("xin", [T, D], F32, kind="ExternalInput").ap()
    xout = nc.dram_tensor("xout", [T, D], F32, kind="ExternalOutput").ap()
    dummy = np.zeros(1)
    HKEYS = {
        "A_wqkv": [1024, 3072], "A_wo": [1024, 1024], "A_wsmall": [1024, 12],
        "A_convs": [3, 1024, 4], "A_convmix": [4, 4], "A_mixbias": [1, 4], "A_rmsw": [1, 256],
        "B_wiT": [1024, 3072], "B_bi": [1, 3072], "B_woT": [1024, 1024], "B_bo": [1, 1024],
        "B_ln": [4, 1024], "B_wge": [1024, 12], "B_bge": [1, 12],
        "B_w1": [8, 1024, 2048], "B_b1": [8, 2048], "B_w2": [8, 2048, 1024], "B_b2": [8, 1024],
        "C_qw": [1024, 1024], "C_kvw": [1024, 512], "C_ow": [1024, 1024], "C_dw": [1024, 14],
        "C_pw": [1024, 2048], "C_gw": [1024, 1024], "C_alpha": [1, 1],
        "D_g12": [2, 1024], "D_wq": [1024, 1024], "D_bq": [1, 1024], "D_wdown": [1024, 64],
        "D_bdown": [1, 64], "D_wupk": [64, 1024], "D_wupv": [64, 1024],
        "D_bupk": [1, 1024], "D_bupv": [1, 1024], "D_wo": [1024, 1024],
        "D_bo": [1, 1024], "D_wg": [1024, 8], "D_bg": [1, 8], "D_we": [8, 1024, 1024],
        "K_ident": [128, 128], "K_ones": [128, 128], "K_headmask": [128, 128],
        "K_m_sl_bd": [128, 128], "K_m_su_bd": [128, 128], "K_m_tu_bd": [128, 128],
        "K_dft_l": [128, 128], "K_idft_l": [128, 128], "K_mlow_neg": [256, 256],
        "K_m_d_neg": [128, 128], "K_rep4": [4, 16],
    }
    for k, shp in HKEYS.items():
        W[k] = nc.dram_tensor(k, shp, F32, kind="ExternalInput").ap()
    scr = {}
    for nm in ("tm0", "tm1", "tm2", "tm3"):
        scr[nm] = nc.dram_tensor("s_" + nm, [T, D], BF16, kind="Internal").ap()
    for nm in ("fm0", "fm1", "fm2", "fm3", "fm4"):
        scr[nm] = nc.dram_tensor("s_" + nm, [D, T], BF16, kind="Internal").ap()
    scr["c32"] = nc.dram_tensor("s_c32", [T, 16], F32, kind="Internal").ap()
    scr["vkb"] = nc.dram_tensor("s_vkb", [T, 2048], BF16, kind="Internal").ap()
    scr["g12"] = nc.dram_tensor("s_g12", [16, T], F32, kind="Internal").ap()
    scr["cT"] = nc.dram_tensor("s_cT", [16, T], BF16, kind="Internal").ap()
    scr["xmid"] = [nc.dram_tensor(f"s_xm{i}", [T, D], F32, kind="Internal").ap() for i in range(2)]

    p = Prog(nc)
    cx = Ctx(p, W)
    cx.sq_bf = cx.pb.take(512)
    cx.negc_bf = cx.pb.take(1)
    cx.negcT_bf = cx.pb.take(16)
    cx.rd_bf = cx.pb.take(512)
    cx.ctrow_bf = cx.pb.take(128)
    cx.negc_bc = cx.pf.take(16)
    cx.rowbf = cx.pb.take(1024)
    cx.rd32 = cx.pf.take(512)
    cx.rdb32 = cx.pf.take(512)
    cx.o32 = cx.pf.take(512)
    cx.x32a = cx.pf.take(1024)
    cx.x32b = cx.pf.take(1024)
    cx.x32c = cx.pf.take(1024)
    cx.epsc = cx.pf.take(4)
    p.start_block()
    cx.load_consts()

    def _eps(E):
        E.memset(cx.epsc[:, 0:1], 1e-6)
        return E.memset(cx.epsc[:, 1:2], 1e-5)
    p.step("dve", _eps)

    bufs = [xin]
    for i in range(len(stages) - 1):
        bufs.append(scr["xmid"][i % 2])
    bufs.append(xout)
    for si, s in enumerate(stages):
        args = (cx, bufs[si], bufs[si + 1], scr)
        if s == "b":
            emit_stage_b(*args)
        elif s == "a":
            emit_stage_a(*args)
        elif s == "c":
            emit_stage_c(*args)
        elif s == "d":
            emit_stage_d(*args)
        else:
            raise ValueError(s)
    for key in DEBUG_TAPS:
        src = scr[key]
        dbg = nc.dram_tensor("dbg_" + key, list(src.shape), src.dtype, kind="ExternalOutput").ap()

        def cp(E, src=src, dbg=dbg):
            return E.dma_start(out=dbg, in_=src)
        p.step("sp", cp, inc=16)

    final_cnt = p.cnt["flow"]
    final_d = p.dsnap()

    def fin(E):
        E.wait_ge(p.sems["flow"], final_cnt)
        for c, tgt in final_d.items():
            if tgt > 0:
                E.wait_ge(p.sems[c], tgt)
    p.block.vector(fin)
    p.close()
    return nc


def emit_stage_a(cx, xin, xout, scr):
    p = cx.p
    W = cx.W
    d_tm0 = scr["tm0"]
    d_q, d_k, d_v, d_kb = scr["fm1"], scr["fm2"], scr["fm3"], scr["fm4"]
    d_xa = scr["fm0"]
    d_vtm, d_ktm = scr["tm1"], scr["tm3"]
    d_vkb = scr["vkb"]
    d_g = scr["g12"]
    d_gtm = scr["c32"]
    d_ofm = d_xa

    emit_ln_tm(cx, xin, d_tm0)
    emit_transpose(cx, d_tm0, d_xa, T, D)
    for i, dst in ((0, d_q), (1, d_k), (2, d_v)):
        emit_fm_matmul(cx, d_xa, W["A_wqkv"][:, i * 1024:(i + 1) * 1024], dst, D, D)
    cx.pb.push()
    ws_sb = cx.pb.take3(ND, 12)

    def lws(E):
        return E.dma_start(out=ws_sb, in_=W["A_wsmall"].rearrange("(c p) f -> p c f", p=128))
    p.step("gp", lws, inc=16)
    xcs = cx.pb.take3(ND, TC)
    gout = cx.pf.take(512)
    for ch in range(NCH):
        def lx(E, ch=ch):
            return E.dma_start(out=xcs, in_=d_xa.rearrange("(c p) t -> p c t", p=128)[:, :, ch * TC:(ch + 1) * TC])
        p.step("sp", lx, inc=16)

        def te(E, ch=ch):
            ins = None
            for d in range(ND):
                ins = E.matmul(cx.psC[0:12, 0:512], ws_sb[:, d, :], xcs[:, d, :],
                               start=(d == 0), stop=(d == ND - 1))
            return ins
        p.step("te", te)

        def dv(E, ch=ch):
            return E.tensor_copy(gout[0:12, :], cx.psC[0:12, 0:512])
        p.step("dve", dv)

        def stg(E, ch=ch):
            return E.dma_start(out=d_g[0:12, ch * TC:(ch + 1) * TC], in_=gout[0:12, :])
        p.step("sp", stg, inc=16)
    cx.pb.pop()
    cx.pb.push(); cx.pf.push()
    convw = cx.pf.take(3 * 8 * 4).rearrange("p (j c k) -> p j c k", j=3, c=8)

    def lcw(E):
        return E.dma_start(out=convw,
                           in_=W["A_convs"].rearrange("j (c p) k -> p j c k", p=128),
                           allow_slow_non_contiguous=True)
    p.step("gp", lcw, inc=16)
    pad = cx.pb.take(3 + T + 1)
    acc32 = cx.pb.take(2 * T).bitcast(F32)
    t32 = cx.pb.take(2 * T).bitcast(F32)
    obf = cx.pb.take(T)
    for j, dst in ((0, d_q), (1, d_k), (2, d_v)):
        for c in range(ND):
            def ld(E, j=j, c=c, dst=dst):
                return E.dma_start(out=pad[:, 3:3 + T], in_=dst[c * 128:(c + 1) * 128, :])
            p.step("sp", ld, inc=16)

            def dvc(E, j=j, c=c):
                p.fence(E, E.memset(pad[:, 0:3], 0.0))
                E.tensor_single_scalar(acc32, pad[:, 0:T], convw[:, j, c, 0:1], ALU.mult)
                for k in range(1, 4):
                    E.scalar_tensor_tensor(acc32, pad[:, k:k + T], convw[:, j, c, k:k + 1],
                                           acc32, ALU.mult, ALU.add)
                return E.engine_nop()
            p.step("dve", dvc)

            def scc(E, j=j, c=c):
                if j < 2:
                    E.activation(t32, acc32, AF.Silu)
                    return E.activation(obf, t32, AF.Silu)
                return E.activation(obf, acc32, AF.Silu)
            p.step("act", scc)

            def stc(E, j=j, c=c, dst=dst):
                return E.dma_start(out=dst[c * 128:(c + 1) * 128, :], in_=obf)
            p.step("sp", stc, inc=16)
    cx.pb.pop(); cx.pf.pop()
    cx.pb.push(); cx.pf.push()
    w12 = cx.pf.take(4)
    mixb = cx.pf.take(1)
    gpad = cx.pb.take(2 * (4 + T)).bitcast(F32)
    gw32 = cx.pb.take(2 * T).bitcast(F32)

    def dvg0(E):
        p.fence(E, E.memset(w12[:, :], 0.0))
        return E.memset(w12[:, 3:4], 1.0)
    p.step("dve", dvg0)

    def lg(E):
        p.dma_in_step(E, gpad[0:4, 3:3 + T], d_g[8:12, :])
        p.dma_in_step(E, gpad[4:12, 3:3 + T], d_g[0:8, :])
        p.dma_in_step(E, mixb[0:4, 0:1], W["A_mixbias"][0:1, :].rearrange("o p -> p o"),
                      allow_slow_non_contiguous=True)
        return E.dma_start(out=w12[0:4, :], in_=W["A_convmix"][:, :])
    p.step("sp", lg, inc=16)

    def dvg(E):
        return E.memset(gpad[0:12, 0:3], 0.0)
    p.step("dve", dvg)

    def dvg2(E):
        E.tensor_single_scalar(gw32[0:12, :], gpad[0:12, 0:T], w12[0:12, 0:1], ALU.mult)
        for k in range(1, 4):
            E.scalar_tensor_tensor(gw32[0:12, :], gpad[0:12, k:k + T], w12[0:12, k:k + 1],
                                   gw32[0:12, :], ALU.mult, ALU.add)
        return E.engine_nop()
    p.step("dve", dvg2)

    def scg(E):
        return E.activation(gw32[0:4, :], gw32[0:4, :], AF.Silu)
    p.step("act", scg)

    def dvg3(E):
        return E.tensor_single_scalar(gw32[0:4, :], gw32[0:4, :], mixb[0:4, 0:1], ALU.add)
    p.step("dve", dvg3)

    def scg2(E):
        return E.activation(gw32[0:12, :], gw32[0:12, :], AF.Sigmoid)
    p.step("act", scg2)

    def stg2(E):
        return E.dma_start(out=d_g[0:12, :], in_=gw32[0:12, :])
    p.step("sp", stg2, inc=16)
    g12bf = cx.pb.take(T)
    gtt = cx.pf.take(16)

    def dvg4(E):
        return E.tensor_copy(g12bf[0:12, :], gw32[0:12, :])
    p.step("dve", dvg4)
    for i in range(NT):
        def teg(E, i=i):
            return E.transpose(cx.psD.bitcast(BF16)[0:128, 0:12],
                               g12bf[0:12, i * 128:(i + 1) * 128], cx.ident[0:12, 0:12])
        p.step("te", teg)

        def dvg5(E, i=i):
            return E.tensor_copy(gtt[:, 0:12], cx.psD.bitcast(BF16)[0:128, 0:12])
        p.step("dve", dvg5)

        def stg3(E, i=i):
            return E.dma_start(out=d_gtm[i * 128:(i + 1) * 128, 0:12], in_=gtt[:, 0:12])
        p.step("sp", stg3, inc=16)
    cx.pb.pop(); cx.pf.pop()

    cx.pb.push(); cx.pf.push()
    qkt = cx.pb.take3(2, TC)
    sqbf = cx.pb.take(TC)
    rst32 = cx.pf.take(TC)
    brow = cx.pb.take(TC)
    bbc32 = cx.pf.take(TC)
    for src_d, dst_kb in ((d_q, None), (d_k, d_kb)):
        for h in range(4):
            for ch in range(NCH):
                def ld(E, src_d=src_d, h=h, ch=ch):
                    return E.dma_start(out=qkt, in_=src_d.rearrange("(c p) t -> p c t", p=128)
                                       [:, 2 * h:2 * h + 2, ch * TC:(ch + 1) * TC])
                p.step("sp", ld, inc=16)
                for c2 in range(2):
                    def sq(E, c2=c2):
                        return E.tensor_mul(sqbf, qkt[:, c2, :], qkt[:, c2, :])
                    p.step("dve", sq)

                    def tes(E, c2=c2):
                        return E.matmul(cx.psC[:, 0:512], cx.ones, sqbf,
                                        start=(c2 == 0), stop=(c2 == 1))
                    p.step("te", tes)

                def scr_(E, h=h, ch=ch):
                    return E.activation(rst32, cx.psC[:, 0:512], AF.Sqrt, bias=cx.epsc[:, 0:1])
                p.step("act", scr_)

                def dvn(E, h=h, ch=ch):
                    E.reciprocal(rst32, rst32)
                    E.tensor_mul(qkt[:, 0, :], qkt[:, 0, :], rst32)
                    return E.tensor_mul(qkt[:, 1, :], qkt[:, 1, :], rst32)
                p.step("dve", dvn)

                def stn(E, src_d=src_d, h=h, ch=ch):
                    return E.dma_start(out=src_d.rearrange("(c p) t -> p c t", p=128)
                                       [:, 2 * h:2 * h + 2, ch * TC:(ch + 1) * TC], in_=qkt)
                p.step("sp", stn, inc=16)
                if dst_kb is not None:
                    def ldb(E, h=h, ch=ch):
                        return E.dma_start(out=brow[0:1, :], in_=d_g[4 + h:5 + h, ch * TC:(ch + 1) * TC])
                    p.step("gp", ldb, inc=16)

                    def teb(E, h=h, ch=ch):
                        return E.matmul(cx.psD[:, 0:512], cx.ones[0:1, :], brow[0:1, :],
                                        start=True, stop=True)
                    p.step("te", teb)

                    def dvb(E, h=h, ch=ch):
                        E.tensor_copy(bbc32, cx.psD[:, 0:512])
                        E.tensor_mul(qkt[:, 0, :], qkt[:, 0, :], bbc32)
                        return E.tensor_mul(qkt[:, 1, :], qkt[:, 1, :], bbc32)
                    p.step("dve", dvb)

                    def stkb(E, h=h, ch=ch):
                        return E.dma_start(out=d_kb.rearrange("(c p) t -> p c t", p=128)
                                           [:, 2 * h:2 * h + 2, ch * TC:(ch + 1) * TC], in_=qkt)
                    p.step("sp", stkb, inc=16)
    cx.pb.pop(); cx.pf.pop()
    emit_transpose(cx, d_k, d_ktm, D, T)
    emit_transpose(cx, d_v, d_vtm, D, T)
    cx.pb.push(); cx.pf.push()
    vk_sb = cx.pb.take(2048)
    vkb_sb = cx.pb.take(2048)
    bt32 = cx.pf.take(16)
    for i in range(NT):
        def ldv(E, i=i):
            p.dma_in_step(E, vk_sb[:, 0:1024], d_vtm[i * 128:(i + 1) * 128, :])
            p.dma_in_step(E, vk_sb[:, 1024:2048], d_ktm[i * 128:(i + 1) * 128, :])
            return E.dma_start(out=bt32, in_=d_gtm[i * 128:(i + 1) * 128, :])
        p.step("sp", ldv, inc=16)

        def dvv(E, i=i):
            ins = None
            for h in range(4):
                E.tensor_single_scalar(vkb_sb[:, h * 512:h * 512 + 256],
                                       vk_sb[:, h * 256:(h + 1) * 256],
                                       bt32[:, 4 + h:5 + h], ALU.mult)
                ins = E.tensor_single_scalar(vkb_sb[:, h * 512 + 256:(h + 1) * 512],
                                             vk_sb[:, 1024 + h * 256:1024 + (h + 1) * 256],
                                             bt32[:, 4 + h:5 + h], ALU.mult)
            return ins
        p.step("dve", dvv)

        def stv(E, i=i):
            return E.dma_start(out=d_vkb[i * 128:(i + 1) * 128, :], in_=vkb_sb)
        p.step("sp", stv, inc=16)
    cx.pb.pop(); cx.pf.pop()

    cx.pb.push(); cx.pf.push()
    msl_sb = cx.pb.take(128)
    msu_sb = cx.pb.take(128)
    mtu_sb = cx.pb.take(128)

    def lm(E):
        p.dma_in_step(E, msl_sb, W["K_m_sl_bd"][:, :], eng="gp")
        p.dma_in_step(E, msu_sb, W["K_m_su_bd"][:, :], eng="gp")
        return E.dma_start(out=mtu_sb, in_=W["K_m_tu_bd"][:, :])
    p.step("gp", lm, inc=16)
    q_sb = cx.pb.take(4 * 256).rearrange("p (h c t) -> p h c t", h=4, c=2)
    k_sb = cx.pb.take(4 * 256).rearrange("p (h c t) -> p h c t", h=4, c=2)
    kb_sb = cx.pb.take(4 * 256).rearrange("p (h c t) -> p h c t", h=4, c=2)
    vkbg = cx.pb.take(4 * 512).rearrange("p (h f) -> p h f", h=4)
    ktm4 = cx.pb.take(4 * 4 * 256).rearrange("p (h c f) -> p h c f", h=4, c=4)
    a_sb = cx.pb.take(4 * 128).rearrange("p (h f) -> p h f", h=4)
    c_sb = cx.pb.take(4 * 5 * 128).rearrange("p (h l f) -> p h l f", h=4, l=5)
    uw_sb = cx.pb.take(4 * 512).rearrange("p (h f) -> p h f", h=4)
    wfm_sb = cx.pb.take(4 * 2 * 128).rearrange("p (h c f) -> p h c f", h=4, c=2)
    s_sb = cx.pb.take(4 * 512).rearrange("p (h f) -> p h f", h=4)
    at_sb = cx.pb.take(4 * 32).rearrange("p (h f) -> p h f", h=4)
    ua_sb = cx.pb.take(4 * 256).rearrange("p (h f) -> p h f", h=4)
    o_sb = cx.pb.take(4 * 2 * 128).rearrange("p (h c f) -> p h c f", h=4, c=2)

    def z0(E):
        return E.memset(s_sb[:, :, :], 0.0)
    p.step("dve", z0)
    NG = 32
    for g in range(NG):
        def lg2(E, g=g):
            for dd, dst in ((d_q, q_sb), (d_k, k_sb), (d_kb, kb_sb)):
                p.dma_in_step(E, dst, dd.rearrange("(c p) t -> p c t", p=128)
                              .rearrange("p (h c) t -> p h c t", h=4)[:, :, :, g * 128:(g + 1) * 128])
            p.dma_in_step(E, vkbg, d_vkb.rearrange("(g p) f -> p g f", p=128)
                          .rearrange("p g (h f) -> p g h f", h=4)[:, g, :, :])
            return E.dma_start(out=ktm4[0:32, :, :, :],
                               in_=d_ktm.rearrange("(g c j) (h f) -> j g h c f", c=4, j=32, h=4)[:, g, :, :, :])
        p.step("sp", lg2, inc=16)
        for h in range(4):
            def teg2(E, h=h):
                for c2 in range(2):
                    E.matmul(cx.psB[:, 0:128], kb_sb[:, h, c2, :], k_sb[:, h, c2, :],
                             start=(c2 == 0), stop=(c2 == 1))
                for c2 in range(2):
                    ins = E.matmul(cx.psB[:, 128:256], k_sb[:, h, c2, :], kb_sb[:, h, c2, :],
                                   start=(c2 == 0), stop=(c2 == 1))
                return ins
            p.step("te", teg2)

            def dvm(E, h=h):
                E.scalar_tensor_tensor(a_sb[:, h, :], cx.psB[:, 0:128], -1.0, msl_sb,
                                       ALU.mult, ALU.mult)
                return E.scalar_tensor_tensor(c_sb[:, h, 0, :], cx.psB[:, 128:256], -1.0, msu_sb,
                                              ALU.mult, ALU.mult)
            p.step("dve", dvm)
            for lv in range(4):
                def tsq(E, h=h, lv=lv):
                    E.matmul(cx.psB[:, 512:640], c_sb[:, h, lv, :], a_sb[:, h, :],
                             start=True, stop=True)
                    return E.matmul(cx.psB[:, 640:768], a_sb[:, h, :], c_sb[:, h, lv, :],
                                    start=True, stop=True)
                p.step("te", tsq)

                def dsq(E, h=h, lv=lv):
                    E.tensor_copy(a_sb[:, h, :], cx.psB[:, 512:640])
                    return E.tensor_copy(c_sb[:, h, lv + 1, :], cx.psB[:, 640:768])
                p.step("dve", dsq)
            def cp0(E, h=h):
                return E.tensor_copy(uw_sb[:, h, :], vkbg[:, h, :])
            p.step("dve", cp0)
            for lv in range(5):
                def tap(E, h=h, lv=lv):
                    E.matmul(cx.psC[:, 0:512], cx.ident, uw_sb[:, h, :], start=True, stop=False)
                    return E.matmul(cx.psC[:, 0:512], c_sb[:, h, lv, :], uw_sb[:, h, :],
                                    start=False, stop=True)
                p.step("te", tap)

                def dap(E, h=h, lv=lv):
                    return E.tensor_copy(uw_sb[:, h, :], cx.psC[:, 0:512])
                p.step("dve", dap)
            def twt(E, h=h):
                E.transpose(cx.psB.bitcast(BF16)[:, 512:640], uw_sb[:, h, 256:384], cx.ident)
                return E.transpose(cx.psB.bitcast(BF16)[:, 640:768], uw_sb[:, h, 384:512], cx.ident)
            p.step("te", twt)

            def dwt(E, h=h):
                E.tensor_scalar_mul(wfm_sb[:, h, 0, :], cx.psB.bitcast(BF16)[:, 512:640], -1.0)
                return E.tensor_scalar_mul(wfm_sb[:, h, 1, :], cx.psB.bitcast(BF16)[:, 640:768], -1.0)
            p.step("dve", dwt)
        for cc in range(4):
            def tat(E, cc=cc):
                ins = None
                for h in range(4):
                    for c2 in range(2):
                        ins = E.matmul(cx.psD[0:32, h * 32:(h + 1) * 32],
                                       k_sb[:, h, c2, cc * 32:(cc + 1) * 32],
                                       q_sb[:, h, c2, cc * 32:(cc + 1) * 32],
                                       start=(c2 == 0), stop=(c2 == 1))
                return ins
            p.step("te", tat)

            def dat(E, cc=cc):
                ins = None
                for h in range(4):
                    ins = E.tensor_mul(at_sb[0:32, h, :], cx.psD[0:32, h * 32:(h + 1) * 32],
                                       mtu_sb[cc * 32:(cc + 1) * 32, cc * 32:(cc + 1) * 32])
                return ins
            p.step("dve", dat)

            def tua(E, cc=cc):
                ins = None
                for h in range(4):
                    E.matmul(cx.psB[0:32, h * 256:(h + 1) * 256], cx.ident[:, cc * 32:(cc + 1) * 32],
                             uw_sb[:, h, 0:256], start=True, stop=False)
                    for c2 in range(2):
                        ins = E.matmul(cx.psB[0:32, h * 256:(h + 1) * 256],
                                       wfm_sb[:, h, c2, cc * 32:(cc + 1) * 32],
                                       s_sb[:, h, c2 * 256:(c2 + 1) * 256],
                                       start=False, stop=(c2 == 1))
                return ins
            p.step("te", tua)

            def dua(E, cc=cc):
                ins = None
                for h in range(4):
                    ins = E.tensor_copy(ua_sb[0:32, h, :], cx.psB[0:32, h * 256:(h + 1) * 256])
                return ins
            p.step("dve", dua)

            def to_(E, cc=cc):
                ins = None
                for h in range(4):
                    for dvh in range(2):
                        for c2 in range(2):
                            E.matmul(cx.psC[:, h * 128 + dvh * 32:h * 128 + dvh * 32 + 32],
                                     s_sb[:, h, c2 * 256 + dvh * 128:c2 * 256 + dvh * 128 + 128],
                                     q_sb[:, h, c2, cc * 32:(cc + 1) * 32],
                                     start=(c2 == 0), stop=False)
                        ins = E.matmul(cx.psC[:, h * 128 + dvh * 32:h * 128 + dvh * 32 + 32],
                                       ua_sb[0:32, h, dvh * 128:(dvh + 1) * 128],
                                       at_sb[0:32, h, :], start=False, stop=True)
                return ins
            p.step("te", to_)

            def do_(E, cc=cc):
                ins = None
                for h in range(4):
                    for dvh in range(2):
                        ins = E.tensor_copy(o_sb[:, h, dvh, cc * 32:(cc + 1) * 32],
                                            cx.psC[:, h * 128 + dvh * 32:h * 128 + dvh * 32 + 32])
                return ins
            p.step("dve", do_)

            def tsu(E, g=g, cc=cc):
                ins = None
                for h in range(4):
                    for c2 in range(2):
                        ins = E.matmul(cx.psA[:, h * 512 + c2 * 256:h * 512 + (c2 + 1) * 256],
                                       ktm4[0:32, h, cc, c2 * 128:(c2 + 1) * 128],
                                       ua_sb[0:32, h, :],
                                       start=(g == 0 and cc == 0 and c2 == 0), stop=False,
                                       skip_group_check=True)
                return ins
            p.step("te", tsu)

            def dsu(E, cc=cc):
                ins = None
                for h in range(4):
                    ins = E.tensor_copy(s_sb[:, h, :], cx.psA[:, h * 512:(h + 1) * 512])
                return ins
            p.step("dve", dsu)

        def sto(E, g=g):
            ins = None
            for h in range(4):
                ins = p.dma_in_step(E, d_ofm.rearrange("(c p) t -> p c t", p=128)
                                    [:, 2 * h:2 * h + 2, g * 128:(g + 1) * 128],
                                    o_sb[:, h, :, :])
            return ins
        wait_at = p.cnt["flow"]

        def sto_body(E, sto=sto, wait_at=wait_at):
            E.wait_ge(p.sems["flow"], wait_at)
            sto(E)
        p.block.sync(sto_body)
    cx.pb.pop(); cx.pf.pop()

    if DEBUG_A8:
        def cpo(E):
            return E.dma_start(out=scr["tm2"].bitcast(BF16) if False else scr["tm2"], in_=d_ofm.rearrange("a b -> a b") if False else d_ofm)
        def cpo2(E):
            return E.dma_start(out=scr["tm2"].rearrange("a b -> (a b)"),
                               in_=d_ofm.rearrange("a b -> (a b)"))
        p.step("sp", cpo2, inc=16)

    cx.pb.push(); cx.pf.push()
    gam_bc = cx.pb.take(2 * T).bitcast(F32)
    mix_bc = cx.pb.take(2 * T).bitcast(F32)
    inv_bc = cx.pb.take(2 * T).bitcast(F32)
    vrow = cx.pb.take(T)
    orow = cx.pb.take(T)
    d1w = cx.pb.take(2 * T).bitcast(F32)
    emw = cx.pb.take(2 * T).bitcast(F32)
    omx = [cx.pb.take(2 * T).bitcast(F32) for _ in range(2)]
    sq2 = cx.pb.take(TC)
    rs2 = cx.pf.take(TC)
    rmsw = cx.pf.take(2)
    grow = cx.pb.take(TC)

    def lrw(E):
        return E.dma_start(out=rmsw, in_=W["A_rmsw"][0:1, :].rearrange("o (c p) -> p (o c)", p=128),
                           allow_slow_non_contiguous=True)
    p.step("sp", lrw, inc=16)
    for h in range(4):
        for which, dst in ((8 + h, gam_bc), (0 + h, mix_bc)):
            for ch in range(NCH):
                def ldr(E, which=which, ch=ch):
                    return E.dma_start(out=grow[0:1, :], in_=d_g[which:which + 1, ch * TC:(ch + 1) * TC])
                p.step("gp", ldr, inc=16)

                def teb2(E, ch=ch):
                    return E.matmul(cx.psD[:, 0:512], cx.ones[0:1, :], grow[0:1, :],
                                    start=True, stop=True)
                p.step("te", teb2)

                def dvb2(E, dst=dst, ch=ch):
                    return E.tensor_copy(dst[:, ch * TC:(ch + 1) * TC], cx.psD[:, 0:512])
                p.step("dve", dvb2)

        def dvi(E, h=h):
            return E.tensor_scalar(inv_bc, gam_bc, -1.0, 1.0, ALU.mult, ALU.add)
        p.step("dve", dvi)
        for dvh in range(2):
            c = 2 * h + dvh

            def ldo(E, c=c):
                p.dma_in_step(E, vrow, d_v[c * 128:(c + 1) * 128, :])
                return E.dma_start(out=orow, in_=d_ofm[c * 128:(c + 1) * 128, :])
            p.step("sp", ldo, inc=16)

            def dve1(E, dvh=dvh):
                E.tensor_mul(d1w, vrow, inv_bc)
                E.tensor_tensor_scan(emw, gam_bc, d1w, 0.0, ALU.mult, ALU.add)
                E.tensor_sub(emw, emw, orow)
                E.tensor_mul(emw, emw, mix_bc)
                return E.tensor_add(omx[dvh], emw, orow)
            p.step("dve", dve1)
        for ch in range(NCH):
            def sqh2(E, ch=ch):
                E.tensor_mul(sq2, omx[0][:, ch * TC:(ch + 1) * TC], omx[0][:, ch * TC:(ch + 1) * TC])
                return E.tensor_mul(cx.sq_bf, omx[1][:, ch * TC:(ch + 1) * TC],
                                    omx[1][:, ch * TC:(ch + 1) * TC])
            p.step("dve", sqh2)

            def teh(E, ch=ch):
                E.matmul(cx.psC[:, 0:512], cx.ones, sq2, start=True, stop=False)
                return E.matmul(cx.psC[:, 0:512], cx.ones, cx.sq_bf, start=False, stop=True)
            p.step("te", teh)

            def sch(E, ch=ch):
                return E.activation(rs2, cx.psC[:, 0:512], AF.Sqrt, bias=cx.epsc[:, 1:2], scale=1.0 / 256.0)
            p.step("act", sch)

            def dvh2(E, h=h, ch=ch):
                E.reciprocal(rs2, rs2)
                for dvh in range(2):
                    E.tensor_mul(omx[dvh][:, ch * TC:(ch + 1) * TC],
                                 omx[dvh][:, ch * TC:(ch + 1) * TC], rs2)
                ins = None
                for dvh in range(2):
                    ins = E.tensor_single_scalar(omx[dvh][:, ch * TC:(ch + 1) * TC],
                                                 omx[dvh][:, ch * TC:(ch + 1) * TC],
                                                 rmsw[:, dvh:dvh + 1], ALU.mult)
                return ins
            p.step("dve", dvh2)

        def dvc2(E, h=h):
            E.tensor_copy(vrow, omx[0])
            return E.tensor_copy(orow, omx[1])
        p.step("dve", dvc2)

        def sto2(E, h=h):
            p.dma_in_step(E, d_ofm[(2 * h) * 128:(2 * h + 1) * 128, :], vrow)
            return E.dma_start(out=d_ofm[(2 * h + 1) * 128:(2 * h + 2) * 128, :], in_=orow)
        p.step("sp", sto2, inc=16)
    cx.pb.pop(); cx.pf.pop()

    cx.pb.push(); cx.pf.push()
    wo_sb = cx.pb.take3(ND, 1024)

    def lwo(E):
        return E.dma_start(out=wo_sb, in_=W["A_wo"].rearrange("(c p) f -> p c f", p=128))
    p.step("gp", lwo, inc=16)
    oc = cx.pb.take3(ND, TC)
    st = cx.stat
    for ch in range(NCH):
        def lo(E, ch=ch):
            return E.dma_start(out=oc, in_=d_ofm.rearrange("(c p) t -> p c t", p=128)[:, :, ch * TC:(ch + 1) * TC])
        p.step("sp", lo, inc=16)
        for tt in range(4):
            i = ch * 4 + tt

            def te2(E, tt=tt):
                ins = None
                for fc in range(2):
                    for d in range(ND):
                        ins = E.matmul(cx.psA[:, fc * 512:fc * 512 + 512],
                                       oc[:, d, tt * 128:(tt + 1) * 128],
                                       wo_sb[:, d, fc * 512:(fc + 1) * 512],
                                       start=(d == 0), stop=(d == ND - 1))
                return ins
            p.step("te", te2)

            def lxi(E, i=i):
                return E.dma_start(out=cx.x32b, in_=xin[i * 128:(i + 1) * 128, :])
            p.step("sp", lxi, inc=16)

            def dvf(E, i=i):
                E.tensor_copy(cx.x32a[:, 0:512], cx.psA[:, 0:512])
                E.tensor_copy(cx.x32a[:, 512:1024], cx.psA[:, 512:1024])
                return E.reduce_sum(st[:, 0:1], cx.x32a, axis=AX.X)
            p.step("dve", dvf)

            def scf(E, i=i):
                return E.scalar_tensor_tensor(cx.x32c, cx.x32a, 1.0, cx.x32a, ALU.mult, ALU.mult,
                                              accum_out=st[:, 1:2])
            p.step("dve", scf)

            def dvf2(E, i=i):
                return ln_var_ops(p, E, cx.stat)
            p.step("dve", dvf2)

            def scf2(E, i=i):
                return E.activation(st[:, 3:4], st[:, 2:3], AF.Sqrt)
            p.step("act", scf2)

            def dvf3(E, i=i):
                p.fence(E, E.reciprocal(st[:, 3:4], st[:, 3:4]))
                E.tensor_scalar(cx.x32c, cx.x32a, st[:, 4:5], st[:, 3:4], ALU.subtract, ALU.mult)
                return E.tensor_add(cx.x32b, cx.x32b, cx.x32c)
            p.step("dve", dvf3)

            def stf(E, i=i):
                return E.dma_start(out=xout[i * 128:(i + 1) * 128, :], in_=cx.x32b)
            p.step("sp", stf, inc=16)
    cx.pb.pop(); cx.pf.pop()


def emit_stage_c(cx, xin, xout, scr):
    p = cx.p
    W = cx.W
    d_tm0, d_xf_tm = scr["tm0"], scr["tm1"]
    d_xf_fm, d_hg, d_hp = scr["fm1"], scr["fm2"], scr["fm3"]
    d_yhat = scr["fm4"]
    st = cx.stat
    NC64 = 64
    HL = 29
    SLOT = 36

    emit_ln_tm(cx, xin, d_tm0)
    cx.pb.push()
    dft_sb = cx.pb.take(128)

    def ldft(E):
        return E.dma_start(out=dft_sb, in_=W["K_dft_l"][:, :])
    p.step("gp", ldft, inc=16)
    xt = cx.pb.take(1024)
    ft = cx.pb.take(1024)
    for i in range(NT):
        def lx(E, i=i):
            return E.dma_start(out=xt, in_=d_tm0[i * 128:(i + 1) * 128, :])
        p.step("sp", lx, inc=16)

        def te(E, i=i):
            E.matmul(cx.psA[:, 0:512], dft_sb, xt[:, 0:512], start=True, stop=True)
            return E.matmul(cx.psA[:, 512:1024], dft_sb, xt[:, 512:1024], start=True, stop=True)
        p.step("te", te)

        def dv(E, i=i):
            E.tensor_copy(ft[:, 0:512], cx.psA[:, 0:512])
            return E.tensor_copy(ft[:, 512:1024], cx.psA[:, 512:1024])
        p.step("dve", dv)

        def stf(E, i=i):
            return E.dma_start(out=d_xf_tm[i * 128:(i + 1) * 128, :], in_=ft)
        p.step("sp", stf, inc=16)
    cx.pb.pop()
    emit_transpose(cx, d_xf_tm, d_xf_fm, T, D)

    cx.pb.push(); cx.pf.push()
    wq_sb = cx.pb.take3(ND, 1024)
    wkv_sb = cx.pb.take3(ND, 512)
    wo_sb = cx.pb.take3(ND, 1024)
    hm_sb = cx.pb.take(128)
    rep_sb = cx.pb.take(16)
    mlow_sb = cx.pb.take(2 * 256).rearrange("p (t q) -> p t q", t=2)

    def lw(E):
        p.dma_in_step(E, wq_sb, W["C_qw"].rearrange("(c p) f -> p c f", p=128), eng="gp")
        p.dma_in_step(E, wkv_sb, W["C_kvw"].rearrange("(c p) f -> p c f", p=128), eng="gp")
        p.dma_in_step(E, wo_sb, W["C_ow"].rearrange("(c p) f -> p c f", p=128), eng="gp")
        p.dma_in_step(E, hm_sb, W["K_headmask"][:, :], eng="gp")
        p.dma_in_step(E, rep_sb[0:4, :], W["K_rep4"][:, :], eng="gp")
        return E.dma_start(out=mlow_sb, in_=W["K_mlow_neg"].rearrange("(t p) q -> p t q", p=128))
    p.step("gp", lw, inc=16)
    xlow = [cx.pb.take3(ND, 256) for _ in range(2)]
    xfrow = cx.pb.take(4096)
    for d in range(ND):
        def lxr(E, d=d):
            return E.dma_start(out=xfrow, in_=d_xf_fm[d * 128:(d + 1) * 128, :])
        p.step("sp", lxr, inc=16)

        def dvl(E, d=d):
            v3 = xfrow.rearrange("p (c f) -> p c f", f=64)
            E.tensor_copy(xlow[0][:, d, :].rearrange("p (c f) -> p c f", f=4), v3[:, :, 0:4])
            p.fence(E, E.memset(xlow[1][:, d, :], 0.0))
            return E.tensor_copy(xlow[1][:, d, :].rearrange("p (c f) -> p c f", f=4)[:, :, 1:4],
                                 v3[:, :, 33:36])
        p.step("dve", dvl)
    qf = [cx.pb.take3(ND, 256) for _ in range(2)]
    kvf = [cx.pb.take3(4, 256) for _ in range(2)]
    vtm = [cx.pb.take(2 * 4 * 65).rearrange("p (t g e) -> p t g e", t=2, g=4) for _ in range(2)]
    alow = [cx.pb.take3(ND, 256) for _ in range(2)]
    es = cx.pb.take(512)
    qst = cx.pb.take(256)
    kst = cx.pb.take(256)
    ptb = cx.pb.take(256)
    for wi in range(2):
        for f in range(ND):
            def teq(E, wi=wi, f=f):
                ins = None
                for d in range(ND):
                    ins = E.matmul(cx.psA[:, 0:256], wq_sb[:, d, f * 128:(f + 1) * 128],
                                   xlow[wi][:, d, :], start=(d == 0), stop=(d == ND - 1))
                return ins
            p.step("te", teq)

            def dvq(E, wi=wi, f=f):
                return E.tensor_scalar_mul(qf[wi][:, f, :], cx.psA[:, 0:256], 0.125)
            p.step("dve", dvq)
        for f in range(4):
            def tek(E, wi=wi, f=f):
                ins = None
                for d in range(ND):
                    ins = E.matmul(cx.psA[:, 0:256], wkv_sb[:, d, f * 128:(f + 1) * 128],
                                   xlow[wi][:, d, :], start=(d == 0), stop=(d == ND - 1))
                return ins
            p.step("te", tek)

            def dvk(E, wi=wi, f=f):
                return E.tensor_copy(kvf[wi][:, f, :], cx.psA[:, 0:256])
            p.step("dve", dvk)
        def mset(E, wi=wi):
            return E.memset(vtm[wi][:, :, :, :], 1.0)
        p.step("dve", mset)
        for tt in range(2):
            def tev(E, wi=wi, tt=tt):
                ins = None
                for gc in range(2):
                    for d in range(ND):
                        ins = E.matmul(cx.psB[:, gc * 128:(gc + 1) * 128],
                                       xlow[wi][:, d, tt * 128:(tt + 1) * 128],
                                       wkv_sb[:, d, 256 + gc * 128:256 + (gc + 1) * 128],
                                       start=(d == 0), stop=(d == ND - 1))
                return ins
            p.step("te", tev)

            def dvv(E, wi=wi, tt=tt):
                ins = None
                for g in range(4):
                    ins = E.tensor_copy(vtm[wi][:, tt, g, 0:64],
                                        cx.psB[:, g * 64:(g + 1) * 64])
                return ins
            p.step("dve", dvv)
        for h in range(16):
            g = h // 4
            qb, qt = 64 * (h % 2), h // 2
            kb, kt = 64 * (g % 2), g // 2

            def dvst(E, wi=wi, qb=qb, qt=qt, kb=kb, kt=kt):
                E.tensor_copy(qst[0:64, :], qf[wi][qb:qb + 64, qt, :])
                return E.tensor_copy(kst[0:64, :], kvf[wi][kb:kb + 64, kt, :])
            p.step("dve", dvst)
            for qi in range(2):
                def tes(E, wi=wi, qi=qi):
                    E.matmul(cx.psA[:, 0:256], cx.ident, mlow_sb[:, qi, :],
                             start=True, stop=False)
                    return E.matmul(cx.psA[:, 0:256], qst[0:64, qi * 128:(qi + 1) * 128],
                                    kst[0:64, :], start=False, stop=True)
                p.step("te", tes)

                def dvm(E, wi=wi, qi=qi):
                    return E.reduce_max(st[:, 8:9], cx.psA[:, 0:256], axis=AX.X, negate=True)
                p.step("dve", dvm)

                def sce(E, wi=wi, qi=qi):
                    return E.activation(es[:, 0:256], cx.psA[:, 0:256], AF.Exp,
                                        bias=st[:, 8:9], accum_out=st[:, 9:10])
                p.step("act", sce)

                def dvn(E, wi=wi, qi=qi):
                    p.fence(E, E.reciprocal(st[:, 10:11], st[:, 9:10]))
                    return E.tensor_single_scalar(es[:, 0:256], es[:, 0:256],
                                                  st[:, 10:11], ALU.mult)
                p.step("dve", dvn)

                def tet(E, wi=wi, qi=qi):
                    E.transpose(cx.psB.bitcast(BF16)[:, 0:128], es[:, 0:128], cx.ident)
                    return E.transpose(cx.psB.bitcast(BF16)[:, 128:256], es[:, 128:256], cx.ident)
                p.step("te", tet)

                def dvt(E, wi=wi, qi=qi):
                    return E.tensor_copy(ptb, cx.psB.bitcast(BF16)[:, 0:256])
                p.step("dve", dvt)

                def tpv(E, wi=wi, g=g, qi=qi):
                    ins = None
                    for kj in range(2):
                        ins = E.matmul(cx.psC[0:64, 0:128], vtm[wi][:, kj, g, 0:64],
                                       ptb[:, kj * 128:(kj + 1) * 128],
                                       start=(kj == 0), stop=(kj == 1))
                    return ins
                p.step("te", tpv)

                def dvo2(E, wi=wi, h=h, qi=qi):
                    return E.tensor_copy(
                        alow[wi][(h % 2) * 64:(h % 2) * 64 + 64, h // 2, qi * 128:(qi + 1) * 128],
                        cx.psC[0:64, 0:128])
                p.step("dve", dvo2)
        ofm = alow[wi]
        res = qf[wi]
        for f in range(ND):
            def teo(E, wi=wi, f=f, ofm=ofm):
                ins = None
                for d in range(ND):
                    ins = E.matmul(cx.psA[:, 0:256], wo_sb[:, d, f * 128:(f + 1) * 128],
                                   ofm[:, d, :], start=(d == 0), stop=(d == ND - 1))
                return ins
            p.step("te", teo)

            def dvp(E, wi=wi, f=f, res=res):
                return E.tensor_copy(res[:, f, :], cx.psA[:, 0:256])
            p.step("dve", dvp)
    alf = cx.pf.take(2)

    def lal(E):
        return E.dma_start(out=cx.rowbf[0:1, 0:1], in_=W["C_alpha"][0:1, 0:1])
    p.step("gp", lal, inc=16)

    def scal(E):
        return E.activation(st[0:1, 8:9], cx.rowbf[0:1, 0:1], AF.Sigmoid)
    p.step("act", scal)

    def dval(E):
        return E.tensor_copy(cx.negc_bf[0:1, 0:1], st[0:1, 8:9])
    p.step("dve", dval)

    def teal(E):
        return E.matmul(cx.psD[:, 0:1], cx.ones[0:1, :], cx.negc_bf[0:1, 0:1],
                        start=True, stop=True)
    p.step("te", teal)

    def dval2(E):
        return E.tensor_copy(alf[:, 0:1], cx.psD[:, 0:1])
    p.step("dve", dval2)
    yl = cx.pb.take(256)
    yh_row = cx.pb.take(4096)
    for d in range(ND):
        def dvbl(E, d=d):
            for wi in range(2):
                p.fence(E, E.tensor_sub(cx.o32[:, 0:256], qf[wi][:, d, :], xlow[wi][:, d, :]))
                p.fence(E, E.tensor_single_scalar(cx.o32[:, 0:256], cx.o32[:, 0:256], alf[:, 0:1], ALU.mult))
                p.fence(E, E.tensor_add(cx.rdb32[:, 0:256], cx.o32[:, 0:256], xlow[wi][:, d, :]))
                p.fence(E, E.tensor_copy(yl, cx.rdb32[:, 0:256]))
                ylv = yl.rearrange("p (c f) -> p c f", f=4)
                if wi == 0:
                    p.fence(E, E.tensor_copy(yh_row.rearrange("p (c f) -> p c f", f=64)[:, :, 0:4], ylv))
                else:
                    p.fence(E, E.tensor_copy(yh_row.rearrange("p (c f) -> p c f", f=64)[:, :, 33:36],
                                             ylv[:, :, 1:4]))
            return E.engine_nop()
        p.step("dve", dvbl)

        def styl(E, d=d):
            return E.dma_start(out=d_yhat[d * 128:(d + 1) * 128, :], in_=yh_row)
        p.step("sp", styl, inc=16)
    cx.pb.pop(); cx.pf.pop()

    cx.pb.push(); cx.pf.push()
    dwc_sb = cx.pf.take(8 * 14).rearrange("p (c k) -> p c k", c=8)

    def ldw(E):
        return E.dma_start(out=dwc_sb, in_=W["C_dw"].rearrange("(c p) k -> p c k", p=128))
    p.step("sp", ldw, inc=16)
    hpadbuf = cx.pb.take(NC64 * SLOT)
    hacc = cx.pb.take(2 * NC64 * HL).bitcast(F32)
    hg = cx.pb.take(NC64 * HL)
    xfr2 = cx.pb.take(4096)
    for wi in range(2):
        for d in range(ND):
            def lx2(E, wi=wi, d=d):
                return E.dma_start(out=xfr2, in_=d_xf_fm[d * 128:(d + 1) * 128, :])
            p.step("sp", lx2, inc=16)

            def dvp1(E, wi=wi, d=d):
                p.fence(E, E.memset(hpadbuf, 0.0))
                v3 = xfr2.rearrange("p (c f) -> p c f", f=64)
                src2 = v3[:, :, 4:33] if wi == 0 else v3[:, :, 36:64]
                dst = hpadbuf.rearrange("p (c s) -> p c s", s=SLOT)
                if wi == 0:
                    return E.tensor_copy(dst[:, :, 6:35], src2)
                return E.tensor_copy(dst[:, :, 6:34], src2)
            p.step("dve", dvp1)

            def dvp2(E, wi=wi, d=d):
                hv = hacc.rearrange("p (c l) -> p c l", l=HL)
                pv = hpadbuf.rearrange("p (c s) -> p c s", s=SLOT)
                E.tensor_single_scalar(hv, pv[:, :, 0:HL], dwc_sb[:, d, 7 * wi:7 * wi + 1], ALU.mult)
                for k in range(1, 7):
                    E.scalar_tensor_tensor(hv, pv[:, :, k:k + HL], dwc_sb[:, d, 7 * wi + k:7 * wi + k + 1],
                                           hv, ALU.mult, ALU.add)
                return E.engine_nop()
            p.step("dve", dvp2)

            def scp(E, wi=wi, d=d):
                return E.activation(hg, hacc, AF.Gelu)
            p.step("act", scp)

            def sth(E, wi=wi, d=d):
                return E.dma_start(out=d_hg[d * 128:(d + 1) * 128, wi * 2048:wi * 2048 + NC64 * HL],
                                   in_=hg)
            p.step("sp", sth, inc=16)
    pw_sb = cx.pb.take3(ND, 1024)
    hgc = cx.pb.take3(ND, 512)
    hp_out = cx.pb.take3(ND, 512)
    CW = [512, 512, 512, 320]
    for wi in range(2):
        def lpw(E, wi=wi):
            return E.dma_start(out=pw_sb, in_=W["C_pw"].rearrange("(c p) f -> p c f", p=128)
                               [:, :, wi * 1024:(wi + 1) * 1024])
        p.step("gp", lpw, inc=16)
        off = 0
        for ci, cw in enumerate(CW):
            def lhg(E, wi=wi, off=off, cw=cw):
                return E.dma_start(out=hgc[:, :, 0:cw],
                                   in_=d_hg.rearrange("(c p) f -> p c f", p=128)
                                   [:, :, wi * 2048 + off:wi * 2048 + off + cw])
            p.step("sp", lhg, inc=16)
            for f in range(ND):
                def tep(E, wi=wi, f=f, cw=cw):
                    ins = None
                    for d in range(ND):
                        ins = E.matmul(cx.psA[:, 0:cw], pw_sb[:, d, f * 128:(f + 1) * 128],
                                       hgc[:, d, 0:cw], start=(d == 0), stop=(d == ND - 1))
                    return ins
                p.step("te", tep)

                def dvpp(E, wi=wi, f=f, cw=cw):
                    return E.tensor_copy(hp_out[:, f, 0:cw], cx.psA[:, 0:cw])
                p.step("dve", dvpp)

            def stp(E, wi=wi, off=off, cw=cw):
                return E.dma_start(out=d_hp.rearrange("(c p) f -> p c f", p=128)
                                   [:, :, wi * 2048 + off:wi * 2048 + off + cw],
                                   in_=hp_out[:, :, 0:cw])
            p.step("sp", stp, inc=16)
            off += cw
    cx.pb.pop(); cx.pf.pop()

    cx.pb.push(); cx.pf.push()
    gw_sb = cx.pb.take3(ND, 1024)
    cm_sb = cx.pb.take3(ND, 64)
    g_sb = cx.pb.take3(ND, 64)
    xfr3 = cx.pb.take(4096)

    def lgw(E):
        return E.dma_start(out=gw_sb, in_=W["C_gw"].rearrange("(c p) f -> p c f", p=128))
    p.step("gp", lgw, inc=16)
    for d in range(ND):
        def lx3(E, d=d):
            return E.dma_start(out=xfr3, in_=d_xf_fm[d * 128:(d + 1) * 128, :])
        p.step("sp", lx3, inc=16)

        def dvcm(E, d=d):
            return E.tensor_copy(cm_sb[:, d, :], xfr3.rearrange("p (c f) -> p c f", f=64)[:, :, 0])
        p.step("dve", dvcm)
    for f in range(ND):
        def teg(E, f=f):
            ins = None
            for d in range(ND):
                ins = E.matmul(cx.psC[:, 0:64], gw_sb[:, d, f * 128:(f + 1) * 128],
                               cm_sb[:, d, :], start=(d == 0), stop=(d == ND - 1))
            return ins
        p.step("te", teg)

        def scg(E, f=f):
            return E.activation(g_sb[:, f, :], cx.psC[:, 0:64], AF.Sigmoid, scale=1.0 / 64.0)
        p.step("act", scg)
    hpr = cx.pb.take3(2, NC64 * HL)
    yhrow = cx.pb.take(4096)
    xfr4 = cx.pb.take(4096)
    hi32 = cx.pb.take(2 * NC64 * HL).bitcast(F32)
    for d in range(ND):
        def lhp(E, d=d):
            p.dma_in_step(E, hpr[:, 0, :], d_hp[d * 128:(d + 1) * 128, 0:NC64 * HL])
            p.dma_in_step(E, hpr[:, 1, :], d_hp[d * 128:(d + 1) * 128, 2048:2048 + NC64 * HL])
            p.dma_in_step(E, xfr4, d_xf_fm[d * 128:(d + 1) * 128, :])
            return E.dma_start(out=yhrow, in_=d_yhat[d * 128:(d + 1) * 128, :])
        p.step("sp", lhp, inc=16)

        def dvh(E, d=d):
            xv = xfr4.rearrange("p (c f) -> p c f", f=64)
            yv = yhrow.rearrange("p (c f) -> p c f", f=64)
            gb = g_sb[:, d, :][:, :, None].broadcast_to([128, NC64, HL])
            for wi in range(2):
                hv = hpr[:, wi, :].rearrange("p (c l) -> p c l", l=HL)
                if wi == 0:
                    E.tensor_tensor(hi32.rearrange("p (c l) -> p c l", l=HL), hv, gb, ALU.mult)
                    E.tensor_tensor(yv[:, :, 4:33], hi32.rearrange("p (c l) -> p c l", l=HL),
                                    xv[:, :, 4:33], ALU.add)
                else:
                    E.tensor_tensor(hi32.rearrange("p (c l) -> p c l", l=HL),
                                    hv, gb, ALU.mult)
                    E.tensor_tensor(yv[:, :, 36:64],
                                    hi32.rearrange("p (c l) -> p c l", l=HL)[:, :, 0:28],
                                    xv[:, :, 36:64], ALU.add)
            return E.engine_nop()
        p.step("dve", dvh)

        def sty(E, d=d):
            return E.dma_start(out=d_yhat[d * 128:(d + 1) * 128, :], in_=yhrow)
        p.step("sp", sty, inc=16)
    cx.pb.pop(); cx.pf.pop()

    emit_transpose(cx, d_yhat, d_xf_tm, D, T)
    cx.pb.push()
    idft_sb = cx.pb.take(128)

    def lidft(E):
        return E.dma_start(out=idft_sb, in_=W["K_idft_l"][:, :])
    p.step("gp", lidft, inc=16)
    yt2 = cx.pb.take(1024)
    for i in range(NT):
        def ly(E, i=i):
            return E.dma_start(out=yt2, in_=d_xf_tm[i * 128:(i + 1) * 128, :])
        p.step("sp", ly, inc=16)

        def tei(E, i=i):
            E.matmul(cx.psA[:, 0:512], idft_sb, yt2[:, 0:512], start=True, stop=True)
            return E.matmul(cx.psA[:, 512:1024], idft_sb, yt2[:, 512:1024], start=True, stop=True)
        p.step("te", tei)

        def lxi(E, i=i):
            return E.dma_start(out=cx.x32b, in_=xin[i * 128:(i + 1) * 128, :])
        p.step("sp", lxi, inc=16)

        def dvf(E, i=i):
            E.tensor_copy(cx.x32a[:, 0:512], cx.psA[:, 0:512])
            E.tensor_copy(cx.x32a[:, 512:1024], cx.psA[:, 512:1024])
            return E.reduce_sum(st[:, 0:1], cx.x32a, axis=AX.X)
        p.step("dve", dvf)

        def scf(E, i=i):
            return E.activation(cx.x32c, cx.x32a, AF.Square, accum_out=st[:, 1:2])
        p.step("act", scf)

        def dvf2(E, i=i):
            return ln_var_ops(p, E, cx.stat)
        p.step("dve", dvf2)

        def scf2(E, i=i):
            return E.activation(st[:, 3:4], st[:, 2:3], AF.Sqrt)
        p.step("act", scf2)

        def dvf3(E, i=i):
            p.fence(E, E.reciprocal(st[:, 3:4], st[:, 3:4]))
            E.tensor_scalar(cx.x32c, cx.x32a, st[:, 4:5], st[:, 3:4], ALU.subtract, ALU.mult)
            return E.tensor_add(cx.x32b, cx.x32b, cx.x32c)
        p.step("dve", dvf3)

        def stf(E, i=i):
            return E.dma_start(out=xout[i * 128:(i + 1) * 128, :], in_=cx.x32b)
        p.step("sp", stf, inc=16)
    cx.pb.pop()


def emit_stage_d(cx, xin, xout, scr):
    p = cx.p
    W = cx.W
    d_tm0, d_tm1, d_tm2 = scr["tm0"], scr["tm1"], scr["tm2"]
    d_fm0, d_fm1 = scr["fm0"], scr["fm1"]
    d_vsc = scr["tm3"]

    emit_ln_tm(cx, xin, d_tm0)
    cx.pf.push()
    cb_g = cx.pf.take(1024)
    emit_row_bcast(cx, W["D_g12"][0:1, :], cb_g, 1024)
    emit_ln_tm(cx, d_tm0, d_tm1, gbc=cb_g, rms=True, eps=1e-6, ld_eng="gp")
    cx.pf.pop()
    emit_transpose(cx, d_tm1, d_fm0, T, D)

    cx.pb.push(); cx.pf.push()
    wq_sb = cx.pb.take3(ND, 1024)
    wdn_sb = cx.pb.take3(ND, 64)
    wupk_sb = cx.pb.take(1024)
    wupv_sb = cx.pb.take(1024)
    wo_sb = cx.pb.take(16 * 1024).rearrange("p (h f) -> p h f", h=16)
    mneg_sb = cx.pb.take(128)

    def lw(E):
        p.dma_in_step(E, wq_sb, W["D_wq"].rearrange("(c p) f -> p c f", p=128), eng="gp")
        p.dma_in_step(E, wdn_sb, W["D_wdown"].rearrange("(c p) f -> p c f", p=128), eng="gp")
        p.dma_in_step(E, wupk_sb[0:64, :], W["D_wupk"][:, :], eng="gp")
        p.dma_in_step(E, wupv_sb[0:64, :], W["D_wupv"][:, :], eng="gp")
        p.dma_in_step(E, wo_sb[0:64, :, :],
                      W["D_wo"].rearrange("(h p) f -> p h f", p=64), eng="gp")
        return E.dma_start(out=mneg_sb, in_=W["K_m_d_neg"][:, :])
    p.step("gp", lw, inc=16)
    bq_sb = cx.pf.take(16)
    bupk_sb = cx.pf.take(16)
    bdn_sb = cx.pf.take(1)
    bupv_bc = cx.pf.take(1024)
    bo_bc = cx.pf.take(1024)

    def lb(E):
        p.dma_in_step(E, bq_sb[0:64, :], W["D_bq"][0:1, :].rearrange("o (h p) -> p (o h)", p=64),
                      allow_slow_non_contiguous=True)
        p.dma_in_step(E, bupk_sb[0:64, :], W["D_bupk"][0:1, :].rearrange("o (h p) -> p (o h)", p=64),
                      allow_slow_non_contiguous=True)
        return E.dma_start(out=bdn_sb[0:64, :], in_=W["D_bdown"][0:1, :].rearrange("o p -> p o"),
                           allow_slow_non_contiguous=True)
    p.step("sp", lb, inc=16)

    def sc8(E):
        return E.tensor_scalar_mul(bq_sb[0:64, :], bq_sb[0:64, :], 0.125)
    p.step("dve", sc8)
    emit_row_bcast(cx, W["D_bupv"][0:1, :], bupv_bc, 1024)
    emit_row_bcast(cx, W["D_bo"][0:1, :], bo_bc, 1024)

    xc = cx.pb.take3(ND, TC)
    qdm = cx.pb.take(8192).rearrange("p (t h) -> p t h", h=16)
    kdm = cx.pb.take(8192).rearrange("p (t h) -> p t h", h=16)
    dn_sb = cx.pb.take(512)
    vtm_sb = cx.pb.take3(4, 1024)
    v8_sb = cx.pb.take(64 * 64).rearrange("p (g e) -> p g e", g=64)
    es8 = cx.pb.take(128)
    p8_sb = cx.pb.take(128)
    odhm = cx.pb.take(16 * 512).rearrange("p (h t) -> p h t", h=16)
    xd_t = cx.pb.take(1024)
    x2_t = cx.pb.take(1024)
    st = cx.stat
    for ch in range(NCH):
        def lx(E, ch=ch):
            return E.dma_start(out=xc, in_=d_fm0.rearrange("(c p) t -> p c t", p=128)[:, :, ch * TC:(ch + 1) * TC])
        p.step("sp", lx, inc=16)

        def te_dn(E, ch=ch):
            ins = None
            for d in range(ND):
                ins = E.matmul(cx.psC[0:64, 0:512], wdn_sb[:, d, :], xc[:, d, :],
                               start=(d == 0), stop=(d == ND - 1))
            return ins
        p.step("te", te_dn)

        def ev_dn(E, ch=ch):
            return E.tensor_scalar_add(dn_sb[0:64, :], cx.psC[0:64, 0:512], bdn_sb[0:64, 0:1])
        p.step("dve", ev_dn)
        for h in range(16):
            def te_q(E, h=h):
                ins = None
                for d in range(ND):
                    ins = E.matmul(cx.psA[0:64, 0:512], wq_sb[:, d, h * 64:(h + 1) * 64],
                                   xc[:, d, :], start=(d == 0), stop=(d == ND - 1))
                return ins
            p.step("te", te_q)

            def ev_q(E, h=h):
                return E.tensor_scalar(qdm[0:64, :, h:h + 1], cx.psA[0:64, 0:512],
                                       0.125, bq_sb[0:64, h:h + 1], ALU.mult, ALU.add)
            p.step("dve", ev_q)

            def te_k(E, h=h):
                return E.matmul(cx.psA[0:64, 512:1024], wupk_sb[0:64, h * 64:(h + 1) * 64],
                                dn_sb[0:64, :], start=True, stop=True)
            p.step("te", te_k)

            def ev_k(E, h=h):
                return E.tensor_scalar_add(kdm[0:64, :, h:h + 1], cx.psA[0:64, 512:1024],
                                           bupk_sb[0:64, h:h + 1])
            p.step("dve", ev_k)
        for ts in range(4):
            def te_v(E, ts=ts):
                for gc in range(2):
                    ins = E.matmul(cx.psB[:, gc * 512:(gc + 1) * 512],
                                   dn_sb[0:64, ts * 128:(ts + 1) * 128],
                                   wupv_sb[0:64, gc * 512:(gc + 1) * 512],
                                   start=True, stop=True)
                return ins
            p.step("te", te_v)

            def ev_v(E, ts=ts):
                E.scalar_tensor_tensor(vtm_sb[:, ts, 0:512], cx.psB[:, 0:512], 1.0,
                                       bupv_bc[:, 0:512], ALU.mult, ALU.add)
                return E.scalar_tensor_tensor(vtm_sb[:, ts, 512:1024], cx.psB[:, 512:1024], 1.0,
                                              bupv_bc[:, 512:1024], ALU.mult, ALU.add)
            p.step("dve", ev_v)

            def st_v(E, ch=ch, ts=ts):
                r = ch * 512 + ts * 128
                return E.dma_start(out=d_vsc[r:r + 128, :], in_=vtm_sb[:, ts, :])
            p.step("sp", st_v, inc=16)

        def ld_v8(E, ch=ch):
            src = d_vsc.rearrange("(r j) (g e) -> r (j g) e", j=8, g=16)
            src = src[ch * 64:(ch + 1) * 64].transpose([1, 0, 2])
            return E.dma_start(out=v8_sb, in_=src)
        p.step("sp", ld_v8, inc=16)
        for grp in range(64):
            def te_s(E, grp=grp):
                E.matmul(cx.psA[:, 0:128], cx.ident, mneg_sb, start=True, stop=False)
                return E.matmul(cx.psA[:, 0:128],
                                qdm[0:64, :, :].rearrange("p t h -> p (t h)")[:, grp * 128:(grp + 1) * 128],
                                kdm[0:64, :, :].rearrange("p t h -> p (t h)")[:, grp * 128:(grp + 1) * 128],
                                start=False, stop=True)
            p.step("te", te_s)

            def dv_m(E, grp=grp):
                return E.reduce_max(st[:, 8:9], cx.psA[:, 0:128], axis=AX.X, negate=True)
            p.step("dve", dv_m)

            def sc_e(E, grp=grp):
                return E.activation(es8, cx.psA[:, 0:128], AF.Exp, bias=st[:, 8:9],
                                    accum_out=st[:, 9:10])
            p.step("act", sc_e)

            def dv_n(E, grp=grp):
                p.fence(E, E.reciprocal(st[:, 10:11], st[:, 9:10]))
                return E.tensor_single_scalar(es8, es8, st[:, 10:11], ALU.mult)
            p.step("dve", dv_n)

            def te_t(E, grp=grp):
                return E.transpose(cx.psB.bitcast(BF16)[:, 0:128], es8, cx.ident)
            p.step("te", te_t)

            def dv_p(E, grp=grp):
                return E.tensor_copy(p8_sb, cx.psB.bitcast(BF16)[:, 0:128])
            p.step("dve", dv_p)

            def te_pv(E, grp=grp):
                return E.matmul(cx.psC[0:64, 0:128], v8_sb[:, grp, :], p8_sb,
                                start=True, stop=True)
            p.step("te", te_pv)

            def dv_o(E, grp=grp):
                src = cx.psC[0:64, 0:128].rearrange("p (j h) -> p j h", j=8)
                dst = odhm[0:64, :, grp * 8:grp * 8 + 8].transpose([0, 2, 1])
                return E.tensor_copy(dst, src)
            p.step("dve", dv_o)
        for ts in range(4):
            i = ch * 4 + ts

            def ld_xd(E, i=i):
                return E.dma_start(out=xd_t, in_=d_tm0[i * 128:(i + 1) * 128, :])
            p.step("sp", ld_xd, inc=16)

            def te_o(E, ts=ts):
                ins = None
                for fc in range(2):
                    for h in range(16):
                        ins = E.matmul(cx.psA[:, fc * 512:(fc + 1) * 512],
                                       odhm[0:64, h, ts * 128:(ts + 1) * 128],
                                       wo_sb[0:64, h, fc * 512:(fc + 1) * 512],
                                       start=(h == 0), stop=(h == 15))
                return ins
            p.step("te", te_o)

            def dv_x2(E, i=i):
                E.tensor_copy(cx.x32a, xd_t)
                E.tensor_add(cx.x32a, cx.x32a, bo_bc)
                E.tensor_add(cx.x32a[:, 0:512], cx.x32a[:, 0:512], cx.psA[:, 0:512])
                E.tensor_add(cx.x32a[:, 512:1024], cx.x32a[:, 512:1024], cx.psA[:, 512:1024])
                return E.tensor_copy(x2_t, cx.x32a)
            p.step("dve", dv_x2)

            def st_x2(E, i=i):
                return E.dma_start(out=d_tm2[i * 128:(i + 1) * 128, :], in_=x2_t)
            p.step("sp", st_x2, inc=16)
    cx.pb.pop(); cx.pf.pop()

    cx.pf.push()
    cb_g2 = cx.pf.take(1024)
    emit_row_bcast(cx, W["D_g12"][1:2, :], cb_g2, 1024)
    emit_ln_tm(cx, d_tm2, d_tm1, gbc=cb_g2, rms=True, eps=1e-6, ld_eng="gp")
    cx.pf.pop()
    emit_transpose(cx, d_tm1, d_fm1, T, D)

    cx.pb.push(); cx.pf.push()
    wg_sb = cx.pb.take3(ND, 8)
    gate_all = cx.pf.take(NT * 8).rearrange("p (t e) -> p t e", t=NT)
    bg_bc = cx.pf.take(8)
    g32 = cx.pf.take(8)

    def lwg(E):
        return E.dma_start(out=wg_sb, in_=W["D_wg"].rearrange("(c p) f -> p c f", p=128))
    p.step("gp", lwg, inc=16)
    emit_row_bcast(cx, W["D_bg"][0:1, :], bg_bc, 8)
    xc2 = cx.pb.take3(ND, TC)
    for ch in range(NCH):
        def lx2(E, ch=ch):
            return E.dma_start(out=xc2, in_=d_fm1.rearrange("(c p) t -> p c t", p=128)[:, :, ch * TC:(ch + 1) * TC])
        p.step("sp", lx2, inc=16)
        for tt in range(4):
            i = ch * 4 + tt

            def te_g(E, tt=tt):
                ins = None
                for d in range(ND):
                    ins = E.matmul(cx.psC[:, 0:8], xc2[:, d, tt * 128:(tt + 1) * 128],
                                   wg_sb[:, d, :], start=(d == 0), stop=(d == ND - 1))
                return ins
            p.step("te", te_g)

            def dv_g(E, i=i):
                p.fence(E, E.tensor_add(g32, cx.psC[:, 0:8], bg_bc))
                return E.reduce_max(st[:, 8:9], g32, axis=AX.X, negate=True)
            p.step("dve", dv_g)

            def sc_g(E, i=i):
                return E.activation(g32, g32, AF.Exp, bias=st[:, 8:9], accum_out=st[:, 9:10])
            p.step("act", sc_g)

            def dv_g2(E, i=i):
                p.fence(E, E.reciprocal(st[:, 10:11], st[:, 9:10]))
                return E.tensor_single_scalar(gate_all[:, i, :], g32, st[:, 10:11], ALU.mult)
            p.step("dve", dv_g2)
    cx.pb.pop()

    cx.pb.push()
    we_sb = cx.pb.take3(ND, 1024)
    macc2 = cx.pb.take(16 * 1024).rearrange("p (s f) -> p s f", s=16)
    xc3b = [cx.pb.take3(ND, TC) for _ in range(2)]
    x2b = cx.pb.take(1024)
    zbanks = [(cx.psB[:, 0:512], cx.psB[:, 512:1024]), (cx.psC, cx.psD)]
    for thalf in range(2):
        for e in range(8):
            def lwe(E, e=e):
                return E.dma_start(out=we_sb, in_=W["D_we"][e].rearrange("(c p) f -> p c f", p=128))
            p.step("gp", lwe, inc=16)
            e_start = p.mark()
            ld_done = [None] * 4
            tz_done = [None] * 4
            dz_done = []
            for ch in range(4):
                cc = thalf * 4 + ch
                cb = ch % 2

                def lx3(E, cc=cc, cb=cb):
                    return E.dma_start(out=xc3b[cb], in_=d_fm1.rearrange("(c p) t -> p c t", p=128)[:, :, cc * TC:(cc + 1) * TC])
                wa_ = e_start[0] if ch < 2 else tz_done[ch - 2][0]
                ld_done[ch] = p.step("sp", lx3, inc=16, wait_at=wa_)
                for ts in range(4):
                    s_loc = ch * 4 + ts
                    i = cc * 4 + ts
                    zi = ch * 4 + ts
                    zb = zbanks[zi % 2]

                    def te_z(E, ts=ts, cb=cb, zb=zb):
                        ins = None
                        for dc in range(2):
                            for d in range(ND):
                                ins = E.matmul(zb[dc],
                                               xc3b[cb][:, d, ts * 128:(ts + 1) * 128],
                                               we_sb[:, d, dc * 512:(dc + 1) * 512],
                                               start=(d == 0), stop=(d == ND - 1))
                        return ins
                    zwa = e_start[0] if zi < 2 else dz_done[zi - 2][0]
                    tz_done[ch] = p.step("te", te_z, wait_at=zwa, dml_at=ld_done[ch][1])

                    def dv_z(E, e=e, s_loc=s_loc, i=i, zb=zb):
                        gs = gate_all[:, i, e:e + 1]
                        m0 = macc2[:, s_loc, 0:512]
                        m1 = macc2[:, s_loc, 512:1024]
                        if e == 0:
                            E.tensor_single_scalar(m0, zb[0], gs, ALU.mult)
                            return E.tensor_single_scalar(m1, zb[1], gs, ALU.mult)
                        E.scalar_tensor_tensor(m0, zb[0], gs, m0, ALU.mult, ALU.add)
                        return E.scalar_tensor_tensor(m1, zb[1], gs, m1, ALU.mult, ALU.add)
                    dz_done.append(p.step("dve", dv_z, dml_at=ld_done[ch][1]))
        for s in range(16):
            i = thalf * 16 + s

            def ld_x2(E, i=i):
                return E.dma_start(out=x2b, in_=d_tm2[i * 128:(i + 1) * 128, :])
            p.step("sp", ld_x2, inc=16)

            def ld_xi(E, i=i):
                return E.dma_start(out=cx.x32b, in_=xin[i * 128:(i + 1) * 128, :])
            p.step("sp", ld_xi, inc=16)

            def dvz(E, i=i, s=s):
                E.tensor_copy(cx.x32a, x2b)
                E.tensor_add(cx.x32a, cx.x32a, macc2[:, s, :])
                return E.reduce_sum(st[:, 0:1], cx.x32a, axis=AX.X)
            p.step("dve", dvz)

            def scz(E, i=i):
                return E.scalar_tensor_tensor(cx.x32c, cx.x32a, 1.0, cx.x32a, ALU.mult, ALU.mult,
                                              accum_out=st[:, 1:2])
            p.step("dve", scz)

            def dvz2(E, i=i):
                return ln_var_ops(p, E, cx.stat)
            p.step("dve", dvz2)

            def scz2(E, i=i):
                return E.activation(st[:, 3:4], st[:, 2:3], AF.Sqrt)
            p.step("act", scz2)

            def dvz3(E, i=i):
                p.fence(E, E.reciprocal(st[:, 3:4], st[:, 3:4]))
                E.tensor_scalar(cx.x32c, cx.x32a, st[:, 4:5], st[:, 3:4], ALU.subtract, ALU.mult)
                return E.tensor_add(cx.x32b, cx.x32b, cx.x32c)
            p.step("dve", dvz3)

            def stx(E, i=i):
                return E.dma_start(out=xout[i * 128:(i + 1) * 128, :], in_=cx.x32b)
            p.step("sp", stx, inc=16)
    cx.pb.pop(); cx.pf.pop()


def run_stages(stage_list, x_full, inputs):
    H = host_prep(inputs)
    nc = build_program(stage_list)
    in_maps = []
    for c in range(B):
        m = {"xin": np.ascontiguousarray(x_full[c])}
        m.update(H)
        in_maps.append(m)
    res = run_bass_kernel_spmd(nc, in_maps, core_ids=list(range(B)))
    out = np.stack([res.results[c]["xout"] for c in range(B)])
    if DEBUG_TAPS:
        return out, res.results
    return out


def kernel(**inputs):
    x = np.asarray(inputs["L0_attn_hidden_states"])
    return run_stages(("a", "b", "c", "d"), x, inputs)
